# version 10
# baseline (speedup 1.0000x reference)
import bisect
import math
import numpy as np
import ml_dtypes
import concourse.bass as bass
import concourse.mybir as mybir
from concourse.bass_utils import run_bass_kernel_spmd

F32 = mybir.dt.float32
BF16 = mybir.dt.bfloat16
U8 = mybir.dt.uint8
A = mybir.AluOpType
AF = mybir.ActivationFunctionType
AX = mybir.AxisListType

B, T, D = 4, 4096, 2048
DK = DV = 128
HKh, HVh = 8, 16
QC, KC, VC = 1024, 1024, 2048
MIXC = QC + KC + VC
CONVK = 4
CH = 128
NCH = T // CH
EPS_L2 = 1e-6
EPS_CORE = 1e-6
EPS_LN = 1e-5
EPS_RMS = 1e-5
DKSC = DK ** -0.5

ENG = ("pe", "act", "dve", "pool", "sp")
_ENG_ATTR = {"pe": "tensor", "act": "scalar", "dve": "vector", "pool": "gpsimd", "sp": "sync"}


def cdiv(a, b):
    return (a + b - 1) // b


class TV:
    __slots__ = ("mk", "key", "p0", "p1", "e0", "e1")

    def __init__(self, mk, key, p0, p1, e0, e1):
        self.mk, self.key, self.p0, self.p1, self.e0, self.e1 = mk, key, p0, p1, e0, e1

    def ap(self):
        return self.mk()


class Buf:

    def __init__(self, handle, name, shape, whole=False):
        self.h = handle
        self.name = name
        self.shape = list(shape)
        self.whole = whole

    def _rng(self, sl, n):
        if isinstance(sl, int):
            return sl, sl
        start, stop, step = sl.indices(n)
        return start, max(start, stop - 1)

    def __getitem__(self, idx):
        if not isinstance(idx, tuple):
            idx = (idx,)
        idx = tuple(idx) + tuple(slice(None) for _ in range(len(self.shape) - len(idx)))
        p0, p1 = self._rng(idx[0], self.shape[0])
        dims = self.shape[1:]
        strides = []
        s = 1
        for d in reversed(dims):
            strides.append(s)
            s *= d
        strides = list(reversed(strides))
        lo = hi = 0
        for sl, n, st in zip(idx[1:], dims, strides):
            a0, a1 = self._rng(sl, n)
            lo += a0 * st
            hi += a1 * st
        if self.whole:
            p0, p1 = 0, self.shape[0] - 1
            lo, hi = 0, int(np.prod(self.shape[1:])) - 1 if len(self.shape) > 1 else 0
        h = self.h

        def mk(idx=idx, h=h):
            ap = h.ap() if hasattr(h, "ap") else h
            return ap[idx]
        return TV(mk, self.name, p0, p1, lo, hi)

    def bcf(self, idx, n):
        tv = self[idx]

        def mk(base=tv.mk, nn=n):
            ap = base()
            shape = list(ap.shape)
            shape[-1] = nn
            return ap.broadcast_to(shape)
        return TV(mk, tv.key, tv.p0, tv.p1, tv.e0, tv.e1)

    def tv_sub(self, i, n):
        return self[0:1, i * n:(i + 1) * n]

    def full(self):
        return self[tuple(slice(None) for _ in self.shape)]

    def tv_custom(self, apfn, p0, p1, e0, e1):
        h = self.h

        def mk():
            ap = h.ap() if hasattr(h, "ap") else h
            return apfn(ap)
        return TV(mk, self.name, p0, p1, e0, e1)


def _overlap(a, b):
    return not (a.p1 < b.p0 or b.p1 < a.p0 or a.e1 < b.e0 or b.e1 < a.e0)


class Rec:
    __slots__ = ("eng", "idx", "tv")

    def __init__(self, eng, idx, tv):
        self.eng, self.idx, self.tv = eng, idx, tv


class Op:
    __slots__ = ("fn", "waits", "incs")

    def __init__(self, fn):
        self.fn = fn
        self.waits = []
        self.incs = []


class Sched:

    def __init__(self, nc):
        self.nc = nc
        self.ops = {e: [] for e in ENG}
        self.icount = {e: 0 for e in ENG}
        self.incpts = {e: ([], []) for e in ENG}
        self.semval = {e: 0 for e in ENG}
        self.waited = {e: {} for e in ENG}
        self.dmacnt = {}
        self.dmaq = {}
        self.last_join = {}
        self.dwaited = {e: {} for e in ENG}
        self.board = {}
        self.semnames = set()

    def _deps_for(self, reads, writes):
        deps = []
        for tv in reads:
            ent = self.board.get(tv.key)
            if not ent:
                continue
            for rec in ent["w"]:
                if _overlap(rec.tv, tv):
                    deps.append(rec)
        for tv in writes:
            ent = self.board.get(tv.key)
            if not ent:
                continue
            for rec in ent["w"]:
                if _overlap(rec.tv, tv):
                    deps.append(rec)
            for rec in ent["r"]:
                if _overlap(rec.tv, tv):
                    deps.append(rec)
        return deps

    def _record(self, eng, idx, reads, writes):
        for tv, kind in [(t, "r") for t in reads] + [(t, "w") for t in writes]:
            ent = self.board.setdefault(tv.key, {"w": [], "r": []})
            lst = ent[kind]
            lst.append(Rec(eng, idx, tv))
            if len(lst) > 96:
                del lst[:-64]

    def _resolve_inc(self, eng, idx):
        idxs, vals = self.incpts[eng]
        i = bisect.bisect_left(idxs, idx)
        if i < len(idxs):
            return vals[i]
        self.semval[eng] += 1
        v = self.semval[eng]
        self.ops[eng][idx].incs.append(("__eng_" + eng, 1))
        self.semnames.add("__eng_" + eng)
        idxs.append(idx)
        vals.append(v)
        return v

    def _join_dma(self, name, cnt):
        qeng = self.dmaq[name]
        last = self.last_join.get(name)
        if last is not None and last[1] >= cnt:
            return qeng, last[0]
        cur = self.dmacnt[name]
        inc = 1 if name == "dcc" else 16
        op = Op(lambda q: None)
        op.waits.append((name, inc * cur))
        self.semnames.add(name)
        idx = self.icount[qeng]
        self.ops[qeng].append(op)
        self.icount[qeng] += 1
        self.last_join[name] = (idx, cur)
        return qeng, idx

    def _apply_deps(self, eng, op, deps, ddeps):
        need = {}
        for rec in deps:
            if rec.eng == "pe" and eng == "pe":
                continue
            v = self._resolve_inc(rec.eng, rec.idx)
            if need.get(rec.eng, 0) < v:
                need[rec.eng] = v
        for name, cnt in sorted(ddeps.items()):
            qeng, jidx = self._join_dma(name, cnt)
            if qeng == eng:
                continue
            v = self._resolve_inc(qeng, jidx)
            if need.get(qeng, 0) < v:
                need[qeng] = v
        for e2, v in sorted(need.items()):
            if self.waited[eng].get(e2, 0) < v:
                op.waits.append(("__eng_" + e2, v))
                self.semnames.add("__eng_" + e2)
                self.waited[eng][e2] = v

    def _split_deps(self, deps):
        ddeps = {}
        rdeps = []
        for rec in deps:
            if rec.eng.startswith("dma:"):
                name = rec.eng[4:]
                if ddeps.get(name, 0) < rec.idx:
                    ddeps[name] = rec.idx
            else:
                rdeps.append(rec)
        return rdeps, ddeps

    def emit(self, eng, fn, reads=(), writes=()):
        deps, ddeps = self._split_deps(self._deps_for(reads, writes))
        op = Op(fn)
        self._apply_deps(eng, op, deps, ddeps)
        idx = self.icount[eng]
        self.ops[eng].append(op)
        self.icount[eng] += 1
        self._record(eng, idx, reads, writes)
        return idx

    def dma(self, eng, out_tv, in_tv, semname):
        prev_q = self.dmaq.setdefault(semname, eng)
        assert prev_q == eng, f"dma sem {semname} used from {prev_q} and {eng}"
        deps, ddeps = self._split_deps(self._deps_for([in_tv], [out_tv]))
        op = Op(lambda q, o=out_tv, i=in_tv: q.dma_start(out=o.ap(), in_=i.ap()))
        self._apply_deps(eng, op, deps, ddeps)
        self.dmacnt[semname] = self.dmacnt.get(semname, 0) + 1
        cnt = self.dmacnt[semname]
        op.incs.append((semname, 16))
        self.semnames.add(semname)
        self.ops[eng].append(op)
        self.icount[eng] += 1
        for tv, kind in ((in_tv, "r"), (out_tv, "w")):
            ent = self.board.setdefault(tv.key, {"w": [], "r": []})
            ent[kind].append(Rec("dma:" + semname, cnt, tv))
            if len(ent[kind]) > 96:
                del ent[kind][:-64]

    def global_barrier(self):
        for name in sorted(self.dmacnt):
            qeng = self.dmaq[name]
            self._join_dma(name, self.dmacnt[name])
        vals = {}
        for e in ENG:
            if self.icount[e]:
                vals[e] = self._resolve_inc(e, self.icount[e] - 1)
        for e in ENG:
            op = Op(lambda q: None)
            for e2, v in sorted(vals.items()):
                if e2 == e:
                    continue
                if self.waited[e].get(e2, 0) < v:
                    op.waits.append(("__eng_" + e2, v))
                    self.semnames.add("__eng_" + e2)
                    self.waited[e][e2] = v
            if op.waits:
                self.ops[e].append(op)
                self.icount[e] += 1

    def cc(self, kind, aluop, groups, in_tv, out_tv):
        name = "dcc"
        self.dmaq.setdefault(name, "pool")
        deps, ddeps = self._split_deps(self._deps_for([in_tv], [out_tv]))
        def fn(q, i=in_tv, o=out_tv):
            return q.collective_compute(kind, aluop, replica_groups=groups,
                                        ins=[i.ap().opt()], outs=[o.ap().opt()])
        op = Op(fn)
        self._apply_deps("pool", op, deps, ddeps)
        self.dmacnt[name] = self.dmacnt.get(name, 0) + 1
        cnt = self.dmacnt[name]
        op.incs.append((name, 1))
        self.semnames.add(name)
        self.ops["pool"].append(op)
        self.icount["pool"] += 1
        for tv, kind2 in ((in_tv, "r"), (out_tv, "w")):
            ent = self.board.setdefault(tv.key, {"w": [], "r": []})
            ent[kind2].append(Rec("dma:" + name, cnt, tv))

    def wait_all_dma(self, eng, semname):
        cnt = self.dmacnt.get(semname, 0)
        if not cnt:
            return
        qeng = self.dmaq[semname]
        inc = 1 if semname == "dcc" else 16
        if eng == qeng:
            op = Op(lambda q: None)
            op.waits.append((semname, inc * cnt))
            self.semnames.add(semname)
            self.ops[eng].append(op)
            self.icount[eng] += 1
        else:
            qeng2, jidx = self._join_dma(semname, cnt)
            v = self._resolve_inc(qeng2, jidx)
            if self.waited[eng].get(qeng2, 0) < v:
                op = Op(lambda q: None)
                op.waits.append(("__eng_" + qeng2, v))
                self.semnames.add("__eng_" + qeng2)
                self.waited[eng][qeng2] = v
                self.ops[eng].append(op)
                self.icount[eng] += 1

    def finalize(self, block, sems):
        def run(eng):
            def body(q):
                for op in self.ops[eng]:
                    for name, val in op.waits:
                        q.wait_ge(sems[name], val)
                    inst = op.fn(q)
                    for name, amt in op.incs:
                        if inst is None:
                            inst = q.nop()
                        inst.then_inc(sems[name], amt)
            return body
        block.tensor(run("pe"))
        block.scalar(run("act"))
        block.vector(run("dve"))
        block.gpsimd(run("pool"))
        block.sync(run("sp"))


class Ctx:
    def __init__(self, nc):
        self.nc = nc
        self.s = Sched(nc)
        self.consts_done = False
        self.stack = None
        self._dmarot = {}
        self._uid = 0

    def uid(self):
        self._uid += 1
        return self._uid

    def sb(self, name, shape, dt):
        if self.stack is not None:
            h = self.stack.enter_context(self.nc.sbuf_tensor(name, shape, dt))
        else:
            h = self.nc.alloc_sbuf_tensor(name, shape, dt)
        return Buf(h, name, shape)

    def phase_begin(self):
        import contextlib
        assert self.stack is None
        self.stack = contextlib.ExitStack()

    def phase_end(self):
        self.s.global_barrier()
        self.stack.close()
        self.stack = None

    def ps(self, name, shape, dt=F32):
        if self.stack is not None:
            h = self.stack.enter_context(self.nc.psum_tensor(name, shape, dt))
        else:
            h = self.nc.alloc_psum_tensor(name, shape, dt)
        return Buf(h, name, shape, whole=True)

    def ps_bf16_alias(self, buf):
        b2 = Buf(buf.h, buf.name, [buf.shape[0], buf.shape[1] * 2], whole=True)
        orig_getitem = Buf.__getitem__

        class _BV(Buf):
            def __getitem__(self, idx):
                tv = orig_getitem(self, idx)
                h = self.h
                idx2 = idx if isinstance(idx, tuple) else (idx,)
                tv.mk = lambda: (h.ap() if hasattr(h, "ap") else h).bitcast(BF16)[idx2]
                return tv
        b2.__class__ = _BV
        return b2

    def din(self, name, shape, dt):
        h = self.nc.dram_tensor(name, shape, dt, kind="ExternalInput")
        return Buf(h, name, shape)

    def dout(self, name, shape, dt):
        h = self.nc.dram_tensor(name, shape, dt, kind="ExternalOutput")
        return Buf(h, name, shape)

    def dint(self, name, shape, dt):
        h = self.nc.dram_tensor(name, shape, dt, kind="Internal")
        return Buf(h, name, shape)

    def mm(self, out, lhsT, rhs, start=True, stop=True):
        self.s.emit("pe", lambda q, o=out, l=lhsT, r=rhs, st=start, sp=stop:
                    q.matmul(o.ap(), l.ap(), r.ap(), start=st, stop=sp),
                    reads=[lhsT, rhs], writes=[out])

    def tr(self, out, in_, ident):
        self.s.emit("pe", lambda q, o=out, i=in_, d=ident:
                    q.transpose(o.ap(), i.ap(), d.ap()),
                    reads=[in_, ident], writes=[out])

    def act(self, eng_out, in_, func, bias=0.0, scale=1.0, accum=None):
        def fn(q, o=eng_out, i=in_, f=func, b=bias, sc=scale, ac=accum):
            kw = {}
            if ac is not None:
                kw["accum_out"] = ac.ap()
            bb = b.ap() if isinstance(b, TV) else b
            ss = sc.ap() if isinstance(sc, TV) else sc
            return q.activation(o.ap(), i.ap(), f, bias=bb, scale=ss, **kw)
        reads = [in_] + [x for x in (bias, scale) if isinstance(x, TV)]
        writes = [eng_out] + ([accum] if accum is not None else [])
        self.s.emit("act", fn, reads=reads, writes=writes)

    def _vec(self, eng, name, out, ins, args, kwargs=None):
        def fn(q, nm=name, o=out, ii=ins, aa=args, kk=kwargs):
            f = getattr(q, nm)
            aps = [x.ap() if isinstance(x, TV) else x for x in ii]
            return f(o.ap(), *aps, *aa, **(kk or {}))
        self.s.emit(eng, fn, reads=[x for x in ins if isinstance(x, TV)], writes=[out])

    def tt(self, out, a, b, op, eng="dve"):
        self._vec(eng, "tensor_tensor", out, [a, b], [op])

    def ts(self, out, a, s1, op0, s2=None, op1=None, eng="dve"):
        def fn(q, o=out, a_=a, s1_=s1, s2_=s2, op0_=op0, op1_=op1):
            s1v = s1_.ap() if isinstance(s1_, TV) else s1_
            s2v = s2_.ap() if isinstance(s2_, TV) else s2_
            if op1_ is None:
                return q.tensor_scalar(o.ap(), a_.ap(), s1v, None, op0_)
            return q.tensor_scalar(o.ap(), a_.ap(), s1v, s2v, op0_, op1_)
        reads = [a] + [x for x in (s1, s2) if isinstance(x, TV)]
        self.s.emit(eng, fn, reads=reads, writes=[out])

    def stt(self, out, in0, scalar, in1, op0, op1, eng="dve"):
        def fn(q, o=out, a_=in0, s_=scalar, b_=in1, o0=op0, o1=op1):
            sv = s_.ap() if isinstance(s_, TV) else s_
            return q.scalar_tensor_tensor(o.ap(), a_.ap(), sv, b_.ap(), o0, o1)
        reads = [in0, in1] + ([scalar] if isinstance(scalar, TV) else [])
        self.s.emit(eng, fn, reads=reads, writes=[out])

    def cp(self, out, in_, eng="dve"):
        self._vec(eng, "tensor_copy", out, [in_], [])

    def cpa(self, out, in_):
        def fn(q, o=out, i=in_):
            return q.copy(o.ap(), i.ap())
        self.s.emit("act", fn, reads=[in_], writes=[out])

    def red(self, out, in_, op, eng="dve"):
        self._vec(eng, "tensor_reduce", out, [in_], [AX.X, op])

    def recip(self, out, in_):
        self._vec("dve", "reciprocal", out, [in_], [])

    def scan(self, out, d0, d1, init, op0, op1, eng="dve"):
        def fn(q, o=out, a_=d0, b_=d1, i_=init, o0=op0, o1=op1):
            iv = i_.ap() if isinstance(i_, TV) else i_
            return q.tensor_tensor_scan(o.ap(), a_.ap(), b_.ap(), iv, o0, o1)
        reads = [d0, d1] + ([init] if isinstance(init, TV) else [])
        self.s.emit(eng, fn, reads=reads, writes=[out])

    def memset(self, out, val, eng="pool"):
        self._vec(eng, "memset", out, [], [val])

    def affsel(self, out, in_, pattern, cm, base, cmp_op, fill):
        def fn(q, o=out, i=in_, p=pattern, c=cm, b=base, co=cmp_op, f=fill):
            return q.affine_select(out=o.ap(), in_=i.ap(), compare_op=co, fill=f,
                                   base=b, pattern=p, channel_multiplier=c)
        self.s.emit("pool", fn, reads=[in_], writes=[out])

    def dma(self, out, in_, sem="d0", eng="sp", rot=4):
        n = self._dmarot.get(sem, 0)
        self._dmarot[sem] = n + 1
        self.s.dma(eng, out, in_, f"{sem}.{n % rot}")

    def build_consts(self):
        c = self
        if self.consts_done:
            return
        self.consts_done = True
        saved, self.stack = self.stack, None
        self._build_consts_inner()
        self.stack = saved

    def _build_consts_inner(self):
        c = self
        self.ident_bf = c.sb("c_ident_bf", [128, 128], BF16)
        self.ident_f = c.sb("c_ident_f", [128, 128], F32)
        self.tril_f = c.sb("c_tril_f", [128, 128], F32)
        self.ones1_bf = c.sb("c_ones1", [1, 128], BF16)
        self.onesk_bf = c.sb("c_onesk", [128, 1], BF16)
        self.ident2_bf = c.sb("c_id2", [128, 128], BF16)
        self.trilS_f = c.sb("c_trilS", [128, 128], F32)
        self.triuI_f = c.sb("c_triuI", [128, 128], F32)
        self.ones1f = c.sb("c_ones1f", [1, 128], F32)
        ms = c.memset
        af = c.affsel
        ms(self.ident_bf.full(), 0.0)
        ms(self.ident_f.full(), 0.0)
        ms(self.tril_f.full(), 1.0)
        ms(self.ones1_bf.full(), 1.0)
        ms(self.onesk_bf.full(), 1.0)
        ms(self.ident2_bf.full(), 0.0)
        ms(self.trilS_f.full(), 1.0)
        ms(self.triuI_f.full(), 1.0)
        ms(self.ones1f.full(), 1.0)
        af(self.ident_bf.full(), self.ident_bf.full(), [[-1, 128]], 1, 0, A.not_equal, 1.0)
        af(self.ident_f.full(), self.ident_f.full(), [[-1, 128]], 1, 0, A.not_equal, 1.0)
        af(self.ident2_bf.full(), self.ident2_bf.full(), [[-1, 128]], 1, 0, A.not_equal, 2.0)
        af(self.tril_f.full(), self.tril_f.full(), [[1, 128]], -1, 0, A.is_ge, 0.0)
        af(self.trilS_f.full(), self.trilS_f.full(), [[-1, 128]], 1, -1, A.is_ge, 0.0)
        af(self.triuI_f.full(), self.triuI_f.full(), [[1, 128]], -1, 0, A.is_ge, 0.0)


def emit_l0attn(c, TS, NSC, io):
    c.build_consts()
    s = c.s
    NQ, NK, NVt = 8, 8, 16
    NMIX = NQ + NK + NVt
    NZ = 16
    NCHS = TS // CH

    xin = c.sb("a_xin", [128, 2, TS], F32)
    xbf = c.sb("a_xbf", [128, 16, TS], BF16)
    xsq = c.sb("a_xsq", [128, 2, TS], BF16)
    rowp = c.sb("a_rowp", [1, 8 * TS], F32)
    wstg = c.sb("a_wstg", [128, 2, 2048], F32)
    wbf = c.sb("a_wbf", [128, 2, 2048], BF16)
    mixr = c.sb("a_mixr", [128, NMIX, CONVK - 1 + TS], BF16)
    tails = c.sb("a_tails", [128, NMIX, CONVK - 1], BF16)
    zbuf = c.sb("a_zbuf", [128, NZ, TS], BF16)
    convt = c.sb("a_convt", [128, TS], F32)
    batok = c.sb("a_batok", [128, NCHS, 32], F32)
    gtok = c.sb("a_gtok", [128, NCHS, 16], F32)
    btok = c.sb("a_btok", [128, NCHS, 16], F32)
    sptmp = c.sb("a_sptmp", [128, 3, 16], F32)
    Lall = c.sb("a_Lall", [128, 16], F32)
    bLall = c.sb("a_bLall", [128, 16], F32)
    bcola = c.sb("a_bcola", [128, 16], F32)
    Sst = [c.sb("a_S0", [128, 16, 128], F32), c.sb("a_S1", [128, 16, 128], F32)]
    def pair(nm, shape, dt):
        return [c.sb(nm + "0", shape, dt), c.sb(nm + "1", shape, dt)]
    Sbf_p = pair("a_Sbf", [128, 128], BF16)
    qsq_p = pair("a_qsq", [128, 128], BF16)
    ksq_p = pair("a_ksq", [128, 128], BF16)
    neggr_p = pair("a_neggr", [128, 128], F32)
    Dsb_p = pair("a_Dsb", [128, 128], F32)
    DTsb_p = pair("a_DTsb", [128, 128], F32)
    Asb_p = pair("a_Asb", [128, 128], F32)
    WTsb_p = pair("a_WTsb", [128, 128], BF16)
    Xsb_p = pair("a_Xsb", [128, 128], BF16)
    XTsb_p = pair("a_XTsb", [128, 128], BF16)
    MTsb_p = pair("a_MTsb", [128, 128], BF16)
    Gsb_p = pair("a_Gsb", [128, 128], BF16)
    vtok_p = pair("a_vtok", [128, 128], BF16)
    ktok_p = pair("a_ktok", [128, 128], BF16)
    kusb_p = pair("a_kusb", [128, 128], BF16)
    rsb_p = pair("a_rsb", [128, 128], BF16)
    r1sb_p = pair("a_r1sb", [128, 128], F32)
    dsb_p = pair("a_dsb", [128, 128], BF16)
    KKs_p = pair("a_KKs", [128, 128], F32)
    KQs_p = pair("a_KQs", [128, 128], F32)
    wtmp_p = pair("a_wtmp", [128, 128], F32)
    rnbcs_p = pair("a_rnbcs", [128, 128], F32)
    rqbc_p = pair("a_rqbc", [128, 128], F32)
    ofsb_p = pair("a_ofsb", [128, 128], F32)
    oftmp_p = pair("a_oftmp", [128, 128], F32)
    osq_p = pair("a_osq", [128, 128], BF16)
    colp_p = pair("a_colp", [128, 64], F32)
    rowsc_p = pair("a_rowsc", [1, 6 * 128], F32)
    ogbuf = c.sb("a_ogbuf", [128, 16, TS], BF16)
    lnr = c.sb("a_lnr", [128, 2, TS], F32)
    outst = c.sb("a_outst", [128, 2, TS], BF16)
    wbasb = c.sb("a_wba", [128, 16, 32], BF16)
    wbastg = c.sb("a_wbastg", [128, 16, 32], F32)
    cwsb = c.sb("a_cw", [128, NMIX, CONVK], F32)
    nwcol = c.sb("a_nwcol", [128, 1], F32)
    dtbc = c.sb("a_dtbc", [128, 16], F32)
    negea = c.sb("a_negea", [128, 16], F32)
    ones1f = c.ones1f

    p0 = c.ps("a_p0", [128, 512])
    p1 = c.ps("a_p1", [128, 512])
    p2 = c.ps("a_p2", [128, 512])
    p3 = c.ps("a_p3", [128, 512])
    p4 = c.ps("a_p4", [128, 512])
    p5 = c.ps("a_p5", [128, 512])
    p6 = c.ps("a_p6", [128, 512])
    p7b = c.ps("a_p7b", [128, 1024], BF16)
    pt_p = [p7b, c.ps_bf16_alias(p6)]
    pA_p = [p0, p1]
    pB_p = [p2, p3]
    pC_p = [p4, p5]

    c.build_consts()
    c.memset(tails.full(), 0.0)
    for par in range(2):
        c.memset(Sst[par].full(), 0.0)
    c.dma(wbastg.full(), io["wba"].full(), "dw")
    c.cp(wbasb.full(), wbastg.full())
    c.dma(cwsb.full(), io["convw"].full(), "dw")
    c.dma(nwcol.full(), io["nw"].full(), "dw")
    alrow = c.sb("a_alrow", [1, 16], F32)
    dtrow = c.sb("a_dtrow", [1, 16], F32)
    onesb1 = c.sb("a_onesb1", [1, 128], BF16)
    c.memset(onesb1.full(), 1.0)
    c.dma(alrow.full(), io["alog"].full(), "dw")
    c.dma(dtrow.full(), io["dtb"].full(), "dw")
    c.mm(p6[:, 0:16], ones1f[0:1, :], alrow[0:1, :])
    c.act(negea.full(), p6[:, 0:16], AF.Exp)
    c.ts(negea.full(), negea.full(), -1.0, A.mult)
    c.mm(p6[:, 16:32], ones1f[0:1, :], dtrow[0:1, :])
    c.cp(dtbc.full(), p6[:, 16:32])

    for sc in range(NSC):
        t0 = sc * TS
        for kt in range(16):
            c.dma(xin[:, kt % 2, :], io["xfeat"][kt * 128:(kt + 1) * 128, t0:t0 + TS], "dx")
            c.cp(xbf[:, kt, :], xin[:, kt % 2, :])
        for kt in range(16):
            c.mm(p5[0:1, 0:TS], c.onesk_bf[:, 0:1], xbf[:, kt, :],
                 start=(kt == 0), stop=(kt == 15))
        c.ts(rowp.tv_sub(0, TS), p5[0:1, 0:TS], 1.0 / D, A.mult)
        for kt in range(16):
            c.act(xsq[:, kt % 2, :], xbf[:, kt, :], AF.Square)
            c.mm(p6[0:1, 0:TS], c.onesk_bf[:, 0:1], xsq[:, kt % 2, :],
                 start=(kt == 0), stop=(kt == 15))
        m_ = rowp.tv_sub(0, TS)
        sq_ = rowp.tv_sub(1, TS)
        var_ = rowp.tv_sub(2, TS)
        rst_ = rowp.tv_sub(3, TS)
        mr_ = rowp.tv_sub(4, TS)
        c.ts(sq_, p6[0:1, 0:TS], 1.0 / D, A.mult)
        c.tt(var_, m_, m_, A.mult)
        c.tt(var_, sq_, var_, A.subtract)
        c.ts(var_, var_, EPS_LN, A.add)
        c.act(rst_, var_, AF.Ln)
        c.act(rst_, rst_, AF.Exp, scale=-0.5)
        c.tt(mr_, m_, rst_, A.mult)
        c.mm(p5[:, 0:TS], ones1f[0:1, :], rst_)
        c.cp(lnr[:, 0, :], p5[:, 0:TS])
        c.mm(p6[:, 0:TS], ones1f[0:1, :], mr_)
        c.cp(lnr[:, 1, :], p6[:, 0:TS])
        for kt in range(16):
            c.tt(xbf[:, kt, :], xbf[:, kt, :], lnr[:, 0, :], A.mult)
            c.tt(xbf[:, kt, :], xbf[:, kt, :], lnr[:, 1, :], A.subtract)

        def proj(wdram, ot, dest_tv, pb, pbi):
            c.dma(wstg[:, pbi, :], wdram[ot], "dw")
            c.cp(wbf[:, pbi, :], wstg[:, pbi, :])
            for kc in range(16):
                c.mm(pb[:, 0:TS], wbf[:, pbi, kc * 128:(kc + 1) * 128], xbf[:, kc, :],
                     start=(kc == 0), stop=(kc == 15))
            c.cp(dest_tv, pb[:, 0:TS])

        for ot in range(NMIX):
            proj(io["wqkv"], ot, mixr[:, ot, CONVK - 1:], [p0, p1][ot % 2], ot % 2)
        for ot in range(NZ):
            proj(io["wz"], ot, zbuf[:, ot, :], [p0, p1][ot % 2], ot % 2)
            c.act(zbuf[:, ot, :], zbuf[:, ot, :], AF.Silu)

        for cc in range(NCHS):
            for kc in range(16):
                c.mm(p2[:, 0:32], xbf[:, kc, cc * CH:(cc + 1) * CH], wbasb[:, kc, :],
                     start=(kc == 0), stop=(kc == 15))
            c.cp(batok[:, cc, :], p2[:, 0:32])
            c.act(btok[:, cc, :], batok[:, cc, 0:16], AF.Sigmoid)
            y_ = sptmp[:, 0, :]
            c.tt(y_, batok[:, cc, 16:32], dtbc.full(), A.add)
            ax = sptmp[:, 1, :]
            c.act(ax, y_, AF.Abs)
            c.act(ax, ax, AF.Exp, scale=-1.0)
            c.act(ax, ax, AF.Ln, bias=1.0)
            mx = sptmp[:, 2, :]
            c.ts(mx, y_, 0.0, A.max)
            c.tt(mx, mx, ax, A.add)
            c.tt(gtok[:, cc, :], mx, negea.full(), A.mult)

        for ot in range(NMIX):
            c.cp(mixr[:, ot, 0:CONVK - 1], tails[:, ot, :])
            c.cp(tails[:, ot, :], mixr[:, ot, TS:TS + CONVK - 1])
            c.ts(convt.full(), mixr[:, ot, 0:TS], cwsb[:, ot, 0:1], A.mult)
            for j in range(1, CONVK):
                c.stt(convt.full(), mixr[:, ot, j:j + TS], cwsb[:, ot, j:j + 1],
                      convt.full(), A.mult, A.add)
            c.act(mixr[:, ot, CONVK - 1:], convt.full(), AF.Silu)

        for cc in range(NCHS):
            gch = (sc * NCHS + cc)
            Sprev = Sst[gch % 2]
            Snew = Sst[(gch + 1) % 2]
            c.mm(p2[:, 0:16], c.tril_f[:, :], gtok[:, cc, :])
            c.cp(bcola.full(), p2[:, 0:16])
            c.act(Lall.full(), bcola.full(), AF.Exp)
            c.tt(bLall.full(), btok[:, cc, :], Lall.full(), A.mult)
            def kh_group(kh, cc=cc, Sprev=Sprev, Snew=Snew):
                P = kh % 2
                pA, pB, pC, pT = pA_p[P], pB_p[P], pC_p[P], pt_p[P]
                qsq, ksq, KKs, KQs, rnbcs, ktok = (qsq_p[P], ksq_p[P], KKs_p[P],
                                                   KQs_p[P], rnbcs_p[P], ktok_p[P])
                Sbf, neggr, Dsb, DTsb, Asb, WTsb = (Sbf_p[P], neggr_p[P], Dsb_p[P],
                                                    DTsb_p[P], Asb_p[P], WTsb_p[P])
                Xsb, XTsb, MTsb, Gsb, vtok, kusb = (Xsb_p[P], XTsb_p[P], MTsb_p[P],
                                                    Gsb_p[P], vtok_p[P], kusb_p[P])
                rsb, r1sb, dsb, wtmp, rqbc = (rsb_p[P], r1sb_p[P], dsb_p[P],
                                              wtmp_p[P], rqbc_p[P])
                ofsb, oftmp, osq, colp, rowsc = (ofsb_p[P], oftmp_p[P], osq_p[P],
                                                 colp_p[P], rowsc_p[P])
                qf = mixr[:, kh, CONVK - 1 + cc * CH: CONVK - 1 + (cc + 1) * CH]
                kf = mixr[:, NQ + kh, CONVK - 1 + cc * CH: CONVK - 1 + (cc + 1) * CH]
                c.tt(qsq.full(), qf, qf, A.mult)
                yield
                c.tt(ksq.full(), kf, kf, A.mult)
                yield
                c.mm(pA[:, 0:128], kf, kf)
                c.mm(pB[:, 0:128], kf, qf)
                c.mm(pC[0:1, 0:128], c.onesk_bf[:, 0:1], ksq.full())
                yield
                c.mm(pC[0:1, 128:256], c.onesk_bf[:, 0:1], qsq.full())
                yield
                c.mm(pC[:, 300:301], ksq.full(), c.onesk_bf[:, 0:1])
                yield
                nr = rowsc.tv_sub(0, 128)
                mrq = rowsc.tv_sub(1, 128)
                c.ts(nr, pC[0:1, 0:128], EPS_L2, A.add)
                yield
                c.act(nr, nr, AF.Ln)
                yield
                c.act(nr, nr, AF.Exp, scale=-0.5)
                yield
                c.ts(mrq, pC[0:1, 128:256], EPS_L2, A.add)
                yield
                c.act(mrq, mrq, AF.Ln)
                yield
                c.act(mrq, mrq, AF.Exp, scale=-0.5)
                yield
                rnc = colp[:, 2:3]
                c.ts(rnc, pC[:, 300:301], EPS_L2, A.add)
                yield
                c.act(rnc, rnc, AF.Ln)
                yield
                c.act(rnc, rnc, AF.Exp, scale=-0.5)
                yield
                c.mm(pC[:, 0:128], ones1f[0:1, :], nr)
                yield
                c.cp(rnbcs.full(), pC[:, 0:128])
                yield
                c.cp(KKs.full(), pA[:, 0:128])
                yield
                c.cp(KQs.full(), pB[:, 0:128])
                yield
                c.tr(pT[:, 0:128], kf, c.ident_bf[:, :])
                yield
                c.cp(ktok.full(), pT[:, 0:128])
                yield
                for vi in range(2):
                    vh = 2 * kh + vi
                    vf = mixr[:, NQ + NK + vh, CONVK - 1 + cc * CH: CONVK - 1 + (cc + 1) * CH]
                    gcol = gtok[:, cc, vh:vh + 1]
                    bcol = bcola[:, vh:vh + 1]
                    betac = btok[:, cc, vh:vh + 1]
                    c.tr(pT[:, 128:256], vf, c.ident_bf[:, :])
                    yield
                    c.cp(vtok.full(), pT[:, 128:256])
                    yield
                    c.ts(neggr.full(), gtok.bcf((slice(None), cc, slice(vh, vh + 1)), 128), -1.0, A.mult)
                    yield
                    c.mm(pB[:, 0:128], neggr.full(), c.tril_f[:, :])
                    yield
                    c.ts(Dsb.full(), pB[:, 0:128], bcol, A.add, 0.0, A.min)
                    yield
                    c.act(Dsb.full(), Dsb.full(), AF.Exp)
                    yield
                    c.tr(pB[:, 0:128], Dsb.full(), c.ident_f[:, :])
                    yield
                    c.cp(DTsb.full(), pB[:, 0:128])
                    yield
                    c.tt(Asb.full(), Dsb.full(), rnbcs.full(), A.mult)
                    yield
                    c.tt(Asb.full(), Asb.full(), KKs.full(), A.mult)
                    yield
                    brn = colp[:, 4:5]
                    c.tt(brn, betac, rnc, A.mult)
                    yield
                    c.stt(Asb.full(), Asb.full(), brn, c.trilS_f[:, :], A.mult, A.mult)
                    yield
                    rmr = rowsc.tv_sub(2, 128)
                    c.ts(rmr, mrq, DKSC, A.mult)
                    yield
                    c.mm(pC[:, 0:128], ones1f[0:1, :], rmr)
                    c.tt(wtmp.full(), DTsb.full(), pC[:, 0:128], A.mult)
                    yield
                    c.tt(wtmp.full(), wtmp.full(), KQs.full(), A.mult)
                    yield
                    c.stt(WTsb.full(), wtmp.full(), rnc, c.triuI_f[:, :], A.mult, A.mult)
                    yield
                    c.tr(pB[:, 0:128], Asb.full(), c.ident_f[:, :])
                    yield
                    c.stt(Xsb.full(), Asb.full(), -1.0, c.ident_f[:, :], A.mult, A.add)
                    yield
                    c.stt(XTsb.full(), pB[:, 0:128], -1.0, c.ident_f[:, :], A.mult, A.add)
                    yield
                    c.stt(MTsb.full(), pB[:, 0:128], 1.0, c.ident_f[:, :], A.mult, A.add)
                    yield
                    for it in range(7):
                        c.mm(pA[:, 0:128], MTsb.full(), Xsb.full())
                        yield
                        c.stt(Gsb.full(), pA[:, 0:128], -1.0, c.ident2_bf[:, :], A.mult, A.add)
                        yield
                        c.mm(pB[:, 0:128], XTsb.full(), Gsb.full())
                        yield
                        c.mm(pA[:, 128:256], Gsb.full(), XTsb.full())
                        yield
                        c.cpa(Xsb.full(), pB[:, 0:128])
                        c.cp(XTsb.full(), pA[:, 128:256])
                        yield
                    c.cp(Sbf.full(), Sprev[:, vh, :])
                    yield
                    c.mm(pA[:, 0:128], kf, Sbf.full())
                    c.ts(r1sb.full(), vtok.full(), betac, A.mult)
                    yield
                    bLrn = colp[:, 5:6]
                    c.stt(bLrn, bLall[:, vh:vh + 1], -1.0, rnc, A.mult, A.mult)
                    yield
                    c.stt(rsb.full(), pA[:, 0:128], bLrn, r1sb.full(), A.mult, A.add)
                    yield
                    c.mm(pB[:, 0:128], XTsb.full(), rsb.full())
                    yield
                    c.cp(dsb.full(), pB[:, 0:128])
                    yield
                    c.mm(pB[:, 128:256], dsb.full(), WTsb.full())
                    c.mm(pA[:, 128:256], Sbf.full(), qf)
                    c.mm(pC[0:1, 0:128], gcol, c.tril_f[:, :])
                    yield
                    brow = rowsc.tv_sub(3, 128)
                    c.act(brow, pC[0:1, 0:128], AF.Exp)
                    rqr = rowsc.tv_sub(4, 128)
                    c.stt(rqr, brow, DKSC, mrq, A.mult, A.mult)
                    yield
                    c.mm(pC[:, 0:128], ones1f[0:1, :], rqr)
                    yield
                    c.cp(rqbc.full(), pC[:, 0:128])
                    yield
                    c.tt(oftmp.full(), pA[:, 128:256], rqbc.full(), A.mult)
                    yield
                    c.tt(ofsb.full(), oftmp.full(), pB[:, 128:256], A.add)
                    yield
                    c.tt(osq.full(), ofsb.full(), ofsb.full(), A.mult)
                    yield
                    c.mm(pC[0:1, 128:256], c.onesk_bf[:, 0:1], osq.full())
                    yield
                    vr = rowsc.tv_sub(5, 128)
                    c.ts(vr, pC[0:1, 128:256], 1.0 / DV, A.mult, EPS_CORE, A.add)
                    yield
                    c.act(vr, vr, AF.Ln)
                    yield
                    c.act(vr, vr, AF.Exp, scale=-0.5)
                    yield
                    c.mm(pC[:, 0:128], ones1f[0:1, :], vr)
                    yield
                    c.tt(oftmp.full(), ofsb.full(), pC[:, 0:128], A.mult)
                    yield
                    c.stt(ogbuf[:, vh, cc * CH:(cc + 1) * CH], oftmp.full(), nwcol[:, 0:1],
                          zbuf[:, vh, cc * CH:(cc + 1) * CH], A.mult, A.mult)
                    ucol = colp[:, 6:7]
                    c.tt(ucol, DTsb[:, 127:128], rnc, A.mult)
                    yield
                    c.ts(kusb.full(), ktok.full(), ucol, A.mult)
                    yield
                    c.mm(pA[:, 256:384], kusb.full(), dsb.full())
                    lcc = colp[:, 7:8]
                    c.tt(lcc, DTsb[:, 127:128], Lall[:, vh:vh + 1], A.mult)
                    yield
                    c.stt(Snew[:, vh, :], Sprev[:, vh, :], lcc, pA[:, 256:384], A.mult, A.add)
                    yield


            def drive(gens):
                gens = [g for g in gens]
                while gens:
                    nxt = []
                    for g in gens:
                        try:
                            next(g)
                            nxt.append(g)
                        except StopIteration:
                            pass
                    gens = nxt

            for kh0 in range(0, HKh, 2):
                drive([kh_group(kh0), kh_group(kh0 + 1)])

        for dt_ in range(16):
            c.dma(wstg[:, dt_ % 2, :], io["wout"][dt_], "dw")
            c.cp(wbf[:, dt_ % 2, :], wstg[:, dt_ % 2, :])
            for vh in range(16):
                c.mm(p0[:, 0:TS], wbf[:, dt_ % 2, vh * 128:(vh + 1) * 128],
                     ogbuf[:, vh, :], start=(vh == 0), stop=(vh == 15))
            c.cp(outst[:, dt_ % 2, :], p0[:, 0:TS])
            c.dma(io["part"][dt_ * 128:(dt_ + 1) * 128, t0:t0 + TS],
                  outst[:, dt_ % 2, :], "do", eng="pool")


def finalize_core(c):
    nc = c.nc
    s = c.s
    for name in sorted(s.dmacnt):
        s.wait_all_dma("pool", name)
        s.wait_all_dma("sp", name)
    sems = {}
    for i, name in enumerate(sorted(s.semnames)):
        sems[name] = nc.alloc_semaphore(f"s{i}")
    with nc.Block() as block:
        s.finalize(block, sems)
    return nc


def _tile_lhsT(W, nk, no):
    K, M = W.shape
    assert K == nk * 128 and M == no * 128
    return np.ascontiguousarray(
        W.reshape(nk, 128, no, 128).transpose(2, 1, 0, 3))


def prep_l0attn(core, inp):
    b, sd = core // 2, core % 2
    x = np.asarray(inp["L0_attn_hidden_states"][b])
    xfeat = np.ascontiguousarray(x.T)
    wqkv = np.asarray(inp["L0_attn_w_qkv"])
    qs = wqkv[:, sd * QC:(sd + 1) * QC]
    ks = wqkv[:, 2048 + sd * KC:2048 + (sd + 1) * KC]
    vs = wqkv[:, 4096 + sd * VC:4096 + (sd + 1) * VC]
    wq = np.concatenate([qs, ks, vs], axis=1)
    conv = np.asarray(inp["L0_attn_conv_w"])
    cq = conv[sd * QC:(sd + 1) * QC]
    ck = conv[2048 + sd * KC:2048 + (sd + 1) * KC]
    cv = conv[4096 + sd * VC:4096 + (sd + 1) * VC]
    cw = np.concatenate([cq, ck, cv], axis=0)
    wz = np.asarray(inp["L0_attn_w_z"])[:, sd * 2048:(sd + 1) * 2048]
    wb = np.asarray(inp["L0_attn_w_b"])[:, sd * 16:(sd + 1) * 16]
    wa = np.asarray(inp["L0_attn_w_a"])[:, sd * 16:(sd + 1) * 16]
    wba = np.concatenate([wb, wa], axis=1)
    wout = np.asarray(inp["L0_attn_w_out"])[sd * 2048:(sd + 1) * 2048]
    alog = np.asarray(inp["L0_attn_A_log"])[sd * 16:(sd + 1) * 16]
    dtb = np.asarray(inp["L0_attn_dt_bias"])[sd * 16:(sd + 1) * 16]
    nw = np.asarray(inp["L0_attn_norm_weight"])
    return {
        "xfeat": np.ascontiguousarray(xfeat, np.float32),
        "wqkv": _tile_lhsT(wq, 16, 32),
        "wz": _tile_lhsT(wz, 16, 16),
        "wba": np.ascontiguousarray(wba.reshape(16, 128, 32).transpose(1, 0, 2), np.float32),
        "convw": np.ascontiguousarray(cw.reshape(32, 128, 4).transpose(1, 0, 2), np.float32),
        "alog": np.ascontiguousarray(alog.reshape(1, 16), np.float32),
        "dtb": np.ascontiguousarray(dtb.reshape(1, 16), np.float32),
        "nw": np.ascontiguousarray(nw.reshape(128, 1), np.float32),
        "wout": _tile_lhsT(wout, 16, 16),
    }


def build_l0attn(T_used=T):
    nc = bass.Bass("TRN2", target_bir_lowering=False)
    c = Ctx(nc)
    TS_ = 512
    nsc = T_used // TS_
    io = {
        "xfeat": c.din("xfeat", [D, T_used], F32),
        "wqkv": c.din("wqkv", [32, 128, 16, 128], F32),
        "wz": c.din("wz", [16, 128, 16, 128], F32),
        "wba": c.din("wba", [128, 16, 32], F32),
        "convw": c.din("convw", [128, 32, 4], F32),
        "alog": c.din("alog", [1, 16], F32),
        "dtb": c.din("dtb", [1, 16], F32),
        "nw": c.din("nw", [128, 1], F32),
        "wout": c.din("wout", [16, 128, 16, 128], F32),
        "part": c.dout("part", [D, T_used], BF16),
    }
    emit_l0attn(c, TS_, nsc, io)
    finalize_core(c)
    return nc


def emit_decode_nvfp4(c, io, key, nrow_tiles, nkcode, scales_blocks, dstfn, pbank, bufs):
    bts, nib, ff, uu, vv, r4, dec, scl, dtr = bufs
    NBC = 512
    npass = (nkcode // 2) // NBC
    for rt in range(nrow_tiles):
        c.dma(scl.full()[0:1] if False else scl[:, 0:scales_blocks], io[key + "_s"][rt], "dw")
        for ps_ in range(npass):
            c.dma(bts[:, 0:NBC], io[key].tv_custom(
                lambda ap, rt=rt, ps_=ps_: ap[rt, :, ps_ * NBC:(ps_ + 1) * NBC],
                rt, rt, ps_ * NBC, (ps_ + 1) * NBC - 1), "dw")
            c.ts(nib[:, 0, 0:NBC], bts[:, 0:NBC], 15, A.bitwise_and)
            c.ts(nib[:, 1, 0:NBC], bts[:, 0:NBC], 4, A.logical_shift_right)
            nibf = nib.tv_custom(lambda ap: ap[:, :, 0:NBC].rearrange("p a b -> p (a b)"),
                                 0, 127, 0, 2 * NBC - 1)
            c.cp(ff[:, 0:2 * NBC], nibf)
            c.ts(uu[:, 0:2 * NBC], ff[:, 0:2 * NBC], 8.0, A.is_ge)
            c.stt(vv[:, 0:2 * NBC], uu[:, 0:2 * NBC], -8.0, ff[:, 0:2 * NBC], A.mult, A.add)
            c.ts(r4[:, 0:2 * NBC], vv[:, 0:2 * NBC], 4.0, A.subtract, 0.0, A.max)
            c.ts(r4[:, 0:2 * NBC], r4[:, 0:2 * NBC], 0.5, A.mult)
            c.stt(r4[:, 0:2 * NBC], vv[:, 0:2 * NBC], 0.5, r4[:, 0:2 * NBC], A.mult, A.add)
            c.ts(vv[:, 0:2 * NBC], vv[:, 0:2 * NBC], 6.0, A.subtract, 0.0, A.max)
            c.tt(r4[:, 0:2 * NBC], r4[:, 0:2 * NBC], vv[:, 0:2 * NBC], A.add)
            c.ts(uu[:, 0:2 * NBC], uu[:, 0:2 * NBC], -2.0, A.mult, 1.0, A.add)
            c.tt(r4[:, 0:2 * NBC], r4[:, 0:2 * NBC], uu[:, 0:2 * NBC], A.mult)
            deck = dec.tv_custom(
                lambda ap: ap[:, 0:2 * NBC].rearrange("p (b a) -> p a b", a=2),
                0, 127, 0, 2 * NBC - 1)
            r43 = r4.tv_custom(lambda ap: ap[:, 0:2 * NBC].rearrange("p (a b) -> p a b", a=2),
                               0, 127, 0, 2 * NBC - 1)
            uu3 = uu.tv_custom(lambda ap: ap[:, 0:2 * NBC].rearrange("p (a b) -> p a b", a=2),
                               0, 127, 0, 2 * NBC - 1)
            c.cp(deck, r43)
            nbl = 2 * NBC // 32
            dec3 = dec.tv_custom(
                lambda ap: ap[:, 0:2 * NBC].rearrange("p (j i) -> p j i", j=nbl),
                0, 127, 0, 2 * NBC - 1)
            scl3 = scl.tv_custom(
                lambda ap, ps_=ps_: ap[:, ps_ * nbl:(ps_ + 1) * nbl].unsqueeze(2).broadcast_to([128, nbl, 32]),
                0, 127, 0, scales_blocks - 1)
            c.tt(dec3, dec3, scl3, A.mult)
            for j in range(2 * NBC // 128):
                kc = ps_ * (2 * NBC // 128) + j
                c.tr(pbank[:, 0:128], dec[:, j * 128:(j + 1) * 128], c.ident_bf[:, :])
                c.cp(dtr.full(), pbank[:, 0:128])
                c.dma(dstfn(rt, kc), dtr.full(), "ddec", eng="pool")


def emit_l0mid(c, TS, NSC, io):
    NMT = 64
    c.build_consts()
    ones1f = c.ones1f
    wg = c.dint("b_wg", [NMT, 128, 16, 128], BF16)
    wu = c.dint("b_wu", [NMT, 128, 16, 128], BF16)
    wdt = c.dint("b_wdt", [2, 64, 128, 8, 128], BF16)
    p0 = c.ps("b_p0", [128, 512])
    p1 = c.ps("b_p1", [128, 512])
    p5 = c.ps("b_p5", [128, 512])
    p6 = c.ps("b_p6", [128, 512])
    p2 = c.ps("b_p2", [128, 512])
    p3 = c.ps("b_p3", [128, 512])
    p4 = c.ps("b_p4", [128, 512])
    p7 = c.ps("b_p7", [128, 512])
    pt = c.ps_bf16_alias(p7)
    dbufs = (
        c.sb("de_bts", [128, 512], U8),
        c.sb("de_nib", [128, 2, 512], U8),
        c.sb("de_ff", [128, 1024], F32),
        c.sb("de_uu", [128, 1024], F32),
        c.sb("de_vv", [128, 1024], F32),
        c.sb("de_r4", [128, 1024], F32),
        c.sb("de_dec", [128, 1024], BF16),
        c.sb("de_scl", [128, 256], F32),
        c.sb("de_dtr", [128, 128], BF16),
    )
    def dst_gu(w):
        def f(rt, kc):
            return w.tv_custom(lambda ap, rt=rt, kc=kc: ap[rt, :, kc, :],
                               rt, rt, kc * 128, (kc + 1) * 128 * 16 - 1)
        return f
    def dst_dn(rt, kc):
        return wdt.tv_custom(lambda ap, rt=rt, kc=kc: ap[rt // 8, kc, :, rt % 8, :],
                             0, 1, 0, 1 << 40)
    emit_decode_nvfp4(c, io, "gatep", NMT, 2048, 64, dst_gu(wg), pt, dbufs)
    emit_decode_nvfp4(c, io, "upp", NMT, 2048, 64, dst_gu(wu), pt, dbufs)
    emit_decode_nvfp4(c, io, "downp", 16, 8192, 256, dst_dn, pt, dbufs)

    at = c.sb("b_at", [128, 16, TS], BF16)
    xsq = c.sb("b_xsq", [128, 2, TS], BF16)
    xin = c.sb("b_xin", [128, 2, TS], F32)
    x1 = c.sb("b_x1", [128, 16, TS], F32)
    x1b = c.sb("b_x1b", [128, 16, TS], BF16)
    rowp = c.sb("b_rowp", [1, 8 * TS], F32)
    lnr = c.sb("b_lnr", [128, 2, TS], F32)
    wstg = c.sb("b_wstg", [128, 4, 2048], BF16)
    gt = c.sb("b_gt", [128, TS], F32)
    gt2 = c.sb("b_gt2", [128, TS], F32)
    outst = c.sb("b_outst", [128, 2, TS], BF16)
    intr = c.sb("b_intr", [128, 2, TS], BF16)
    iscr = c.dint("b_iscr", [NMT, 128, TS], BF16)

    def ln_rows(src_tiles, n, pb1, pb2):
        for kt in range(n):
            c.mm(pb1[0:1, 0:TS], c.onesk_bf[:, 0:1], src_tiles(kt),
                 start=(kt == 0), stop=(kt == n - 1))
        m_ = rowp.tv_sub(0, TS)
        c.ts(m_, pb1[0:1, 0:TS], 1.0 / D, A.mult)
        for kt in range(n):
            c.act(xsq[:, kt % 2, :], src_tiles(kt), AF.Square)
            c.mm(pb2[0:1, 0:TS], c.onesk_bf[:, 0:1], xsq[:, kt % 2, :],
                 start=(kt == 0), stop=(kt == n - 1))
        sq_ = rowp.tv_sub(1, TS)
        var_ = rowp.tv_sub(2, TS)
        rst_ = rowp.tv_sub(3, TS)
        mr_ = rowp.tv_sub(4, TS)
        c.ts(sq_, pb2[0:1, 0:TS], 1.0 / D, A.mult)
        c.tt(var_, m_, m_, A.mult)
        c.tt(var_, sq_, var_, A.subtract)
        c.ts(var_, var_, EPS_LN, A.add)
        c.act(rst_, var_, AF.Ln)
        c.act(rst_, rst_, AF.Exp, scale=-0.5)
        c.tt(mr_, m_, rst_, A.mult)
        c.mm(pb1[:, 0:TS], ones1f[0:1, :], rst_)
        c.cp(lnr[:, 0, :], pb1[:, 0:TS])
        c.mm(pb2[:, 0:TS], ones1f[0:1, :], mr_)
        c.cp(lnr[:, 1, :], pb2[:, 0:TS])

    for sc in range(NSC):
        t0 = sc * TS
        for kt in range(16):
            c.dma(at[:, kt, :], io["pa"][kt * 128:(kt + 1) * 128, t0:t0 + TS], "dx")
            if "pb" in io:
                c.dma(x1b[:, kt, :], io["pb"][kt * 128:(kt + 1) * 128, t0:t0 + TS], "dx")
                c.tt(at[:, kt, :], at[:, kt, :], x1b[:, kt, :], A.add)
        ln_rows(lambda kt: at[:, kt, :], 16, p5, p6)
        for kt in range(16):
            c.dma(xin[:, kt % 2, :], io["xfeat"][kt * 128:(kt + 1) * 128, t0:t0 + TS], "dx")
            c.tt(gt.full(), at[:, kt, :], lnr[:, 0, :], A.mult)
            c.tt(gt.full(), gt.full(), lnr[:, 1, :], A.subtract)
            c.tt(x1[:, kt, :], xin[:, kt % 2, :], gt.full(), A.add)
            c.dma(io["x1f"][kt * 128:(kt + 1) * 128, t0:t0 + TS], x1[:, kt, :], "do", eng="pool")
            c.cp(at[:, kt, :], x1[:, kt, :])
        ln_rows(lambda kt: at[:, kt, :], 16, p5, p6)
        for kt in range(16):
            c.tt(x1b[:, kt, :], x1[:, kt, :], lnr[:, 0, :], A.mult)
            c.tt(x1b[:, kt, :], x1b[:, kt, :], lnr[:, 1, :], A.subtract)
        for mt in range(NMT):
            pg = [p0, p1][mt % 2]
            pu = [p2, p3][mt % 2]
            gtm = [gt, gt2][mt % 2]
            sg = (2 * mt) % 4
            su = (2 * mt + 1) % 4
            c.dma(wstg[:, sg, :], wg.tv_custom(
                lambda ap, mt=mt: ap[mt].rearrange("b a c -> b (a c)"),
                mt, mt, 0, 16 * 128 * 128 - 1), "dwg")
            c.dma(wstg[:, su, :], wu.tv_custom(
                lambda ap, mt=mt: ap[mt].rearrange("b a c -> b (a c)"),
                mt, mt, 0, 16 * 128 * 128 - 1), "dwg")
            for kc in range(16):
                c.mm(pg[:, 0:TS], wstg[:, sg, kc * 128:(kc + 1) * 128], x1b[:, kc, :],
                     start=(kc == 0), stop=(kc == 15))
            for kc in range(16):
                c.mm(pu[:, 0:TS], wstg[:, su, kc * 128:(kc + 1) * 128], x1b[:, kc, :],
                     start=(kc == 0), stop=(kc == 15))
            c.act(gtm.full(), pg[:, 0:TS], AF.Gelu_apprx_tanh)
            c.tt(intr[:, mt % 2, :], pu[:, 0:TS], gtm.full(), A.mult)
            c.dma(iscr[mt], intr[:, mt % 2, :], "dis", eng="pool")
        banks = [p0, p1, p2, p3, p4, p5, p6, p7]
        for half in range(2):
            for kc in range(NMT):
                c.dma(intr[:, kc % 2, :], iscr[kc], "dis2")
                c.dma(wstg[:, kc % 4, 0:1024], wdt.tv_custom(
                    lambda ap, half=half, kc=kc: ap[half, kc].rearrange("b a c -> b (a c)"),
                    0, 1, 0, 1 << 40), "dwd")
                for j in range(8):
                    c.mm(banks[j][:, 0:TS], wstg[:, kc % 4, j * 128:(j + 1) * 128], intr[:, kc % 2, :],
                         start=(kc == 0), stop=(kc == NMT - 1))
            for j in range(8):
                dt_ = half * 8 + j
                c.cp(outst[:, j % 2, :], banks[j][:, 0:TS])
                c.dma(io["dpart"][dt_ * 128:(dt_ + 1) * 128, t0:t0 + TS],
                      outst[:, j % 2, :], "do", eng="pool")


def prep_l0mid(core, inp):
    sd = core % 2

    def bytes_of(a):
        return np.ascontiguousarray(a.view(np.uint8).reshape(a.shape[0], a.shape[1], 4)[:, :, 0])

    gp = bytes_of(np.asarray(inp["L0_ffn_gate_W_packed"]))[sd * 8192:(sd + 1) * 8192]
    up = bytes_of(np.asarray(inp["L0_ffn_up_W_packed"]))[sd * 8192:(sd + 1) * 8192]
    dp = bytes_of(np.asarray(inp["L0_ffn_down_W_packed"]))[:, sd * 4096:(sd + 1) * 4096]
    gs = np.asarray(inp["L0_ffn_gate_scales"])[sd * 8192:(sd + 1) * 8192]
    us = np.asarray(inp["L0_ffn_up_scales"])[sd * 8192:(sd + 1) * 8192]
    ds = np.asarray(inp["L0_ffn_down_scales"])[:, sd * 256:(sd + 1) * 256]
    return {
        "gatep": np.ascontiguousarray(gp.reshape(64, 128, 1024)),
        "gatep_s": np.ascontiguousarray(gs.reshape(64, 128, 64), np.float32),
        "upp": np.ascontiguousarray(up.reshape(64, 128, 1024)),
        "upp_s": np.ascontiguousarray(us.reshape(64, 128, 64), np.float32),
        "downp": np.ascontiguousarray(dp.reshape(16, 128, 4096)),
        "downp_s": np.ascontiguousarray(ds.reshape(16, 128, 256), np.float32),
    }


def build_l0mid(T_used=T):
    nc = bass.Bass("TRN2", target_bir_lowering=False)
    c = Ctx(nc)
    TS_ = 512
    nsc = T_used // TS_
    io = {
        "xfeat": c.din("xfeat", [D, T_used], F32),
        "pa": c.din("pa", [D, T_used], BF16),
        "pb": c.din("pb", [D, T_used], BF16),
        "gatep": c.din("gatep", [64, 128, 1024], U8),
        "gatep_s": c.din("gatep_s", [64, 128, 64], F32),
        "upp": c.din("upp", [64, 128, 1024], U8),
        "upp_s": c.din("upp_s", [64, 128, 64], F32),
        "downp": c.din("downp", [16, 128, 4096], U8),
        "downp_s": c.din("downp_s", [16, 128, 256], F32),
        "x1f": c.dout("x1f", [D, T_used], F32),
        "dpart": c.dout("dpart", [D, T_used], BF16),
    }
    emit_l0mid(c, TS_, nsc, io)
    finalize_core(c)
    return nc


def emit_l1pre(c, TS, NSC, io):
    NE = 16
    c.build_consts()
    ones1f = c.ones1f
    p0 = c.ps("c_p0", [128, 512])
    p1 = c.ps("c_p1", [128, 512])
    p5 = c.ps("c_p5", [128, 512])
    p6 = c.ps("c_p6", [128, 512])
    aa = c.sb("c_aa", [128, 16, TS], BF16)
    bb = c.sb("c_bb", [128, 16, TS], F32)
    bt_ = c.sb("c_bt", [128, 16, TS], BF16)
    rnb = c.sb("c_rnb", [128, 16, TS], BF16)
    xsq = c.sb("c_xsq", [128, 2, TS], BF16)
    rowp = c.sb("c_rowp", [1, 8 * TS], F32)
    lnr = c.sb("c_lnr", [128, 2, TS], F32)
    gt = c.sb("c_gt", [128, TS], F32)
    wstg = c.sb("c_wstg", [128, 2, 2048], F32)
    wbf = c.sb("c_wbf", [128, 2, 2048], BF16)
    mixr = c.sb("c_mixr", [128, NE, CONVK - 1 + TS], BF16)
    tails = c.sb("c_tails", [128, NE, CONVK - 1], BF16)
    convt = c.sb("c_convt", [128, TS], F32)
    cwsb = c.sb("c_cw", [128, NE, CONVK], F32)
    cbsb = c.sb("c_cb", [128, NE], F32)
    nwsb = c.sb("c_nw", [128, 16], F32)
    wd1sb = c.sb("c_wd1", [128, 16, 128], BF16)
    wd1st = c.sb("c_wd1st", [128, 16, 128], F32)
    wbcsb = c.sb("c_wbc", [128, 16, 32], BF16)
    wbcst = c.sb("c_wbcst", [128, 16, 32], F32)
    outst = c.sb("c_outst", [128, 2, TS], BF16)
    c.memset(tails.full(), 0.0)
    c.dma(cwsb.full(), io["convw1"].full(), "dw")
    c.dma(cbsb.full(), io["convb"].full(), "dw")
    c.dma(nwsb.full(), io["nrmw"].full(), "dw")
    c.dma(wd1st.full(), io["wd1"].full(), "dw")
    c.cp(wd1sb.full(), wd1st.full())
    c.dma(wbcst.full(), io["wbc"].full(), "dw")
    c.cp(wbcsb.full(), wbcst.full())

    def ln_rows(src_tiles, n, pb1, pb2, eps=EPS_LN, mean=True):
        for kt in range(n):
            c.act(xsq[:, kt % 2, :], src_tiles(kt), AF.Square)
            c.mm(pb2[0:1, 0:TS], c.onesk_bf[:, 0:1], xsq[:, kt % 2, :],
                 start=(kt == 0), stop=(kt == n - 1))
        sq_ = rowp.tv_sub(1, TS)
        var_ = rowp.tv_sub(2, TS)
        rst_ = rowp.tv_sub(3, TS)
        mr_ = rowp.tv_sub(4, TS)
        c.ts(sq_, pb2[0:1, 0:TS], 1.0 / D, A.mult)
        if mean:
            for kt in range(n):
                c.mm(pb1[0:1, 0:TS], c.onesk_bf[:, 0:1], src_tiles(kt),
                     start=(kt == 0), stop=(kt == n - 1))
            m_ = rowp.tv_sub(0, TS)
            c.ts(m_, pb1[0:1, 0:TS], 1.0 / D, A.mult)
            c.tt(var_, m_, m_, A.mult)
            c.tt(var_, sq_, var_, A.subtract)
        else:
            c.cp(var_, sq_)
        c.ts(var_, var_, eps, A.add)
        c.act(rst_, var_, AF.Ln)
        c.act(rst_, rst_, AF.Exp, scale=-0.5)
        c.mm(pb1[:, 0:TS], ones1f[0:1, :], rst_)
        c.cp(lnr[:, 0, :], pb1[:, 0:TS])
        if mean:
            c.tt(mr_, m_, rst_, A.mult)
            c.mm(pb2[:, 0:TS], ones1f[0:1, :], mr_)
            c.cp(lnr[:, 1, :], pb2[:, 0:TS])

    for sc in range(NSC):
        t0 = sc * TS
        for kt in range(16):
            c.dma(aa[:, kt, :], io["dpa"][kt * 128:(kt + 1) * 128, t0:t0 + TS], "dx")
            if "dpb" in io:
                c.dma(bb[:, kt, :], io["dpb"][kt * 128:(kt + 1) * 128, t0:t0 + TS], "dx")
                c.tt(aa[:, kt, :], aa[:, kt, :], bb[:, kt, :], A.add)
            c.cp(bt_[:, kt, :], aa[:, kt, :])
        ln_rows(lambda kt: bt_[:, kt, :], 16, p5, p6)
        for kt in range(16):
            c.dma(bb[:, kt, :], io["x1f"][kt * 128:(kt + 1) * 128, t0:t0 + TS], "dx")
        for kt in range(16):
            c.tt(gt.full(), aa[:, kt, :], lnr[:, 0, :], A.mult)
            c.tt(gt.full(), gt.full(), lnr[:, 1, :], A.subtract)
            c.tt(bb[:, kt, :], bb[:, kt, :], gt.full(), A.add)
            c.dma(io["x2f"][kt * 128:(kt + 1) * 128, t0:t0 + TS], bb[:, kt, :], "do", eng="pool")
            c.cp(bt_[:, kt, :], bb[:, kt, :])
        ln_rows(lambda kt: bt_[:, kt, :], 16, p5, p6)
        for kt in range(16):
            c.tt(aa[:, kt, :], bb[:, kt, :], lnr[:, 0, :], A.mult)
            c.tt(aa[:, kt, :], aa[:, kt, :], lnr[:, 1, :], A.subtract)
            c.cp(bt_[:, kt, :], aa[:, kt, :])
        ln_rows(lambda kt: bt_[:, kt, :], 16, p5, p6, eps=EPS_RMS, mean=False)
        for kt in range(16):
            c.tt(gt.full(), aa[:, kt, :], lnr[:, 0, :], A.mult)
            c.ts(rnb[:, kt, :], gt.full(), nwsb[:, kt:kt + 1], A.mult)
        for ot in range(NE):
            c.dma(wstg[:, ot % 2, :], io["wskip"][ot], "dw")
            c.cp(wbf[:, ot % 2, :], wstg[:, ot % 2, :])
            for kc in range(16):
                c.mm(p0[:, 0:TS], wbf[:, ot % 2, kc * 128:(kc + 1) * 128], rnb[:, kc, :],
                     start=(kc == 0), stop=(kc == 15))
            c.act(gt.full(), p0[:, 0:TS], AF.Silu)
            c.cp(convt.full(), gt.full())
            c.dma(io["ssil"][ot * 128:(ot + 1) * 128, t0:t0 + TS],
                  convt.tv_custom(lambda ap: ap[:, 0:TS], 0, 127, 0, TS - 1), "do", eng="pool") \
                if False else None
            c.cp(mixr[:, ot, CONVK - 1:], gt.full())
            c.dma(io["ssil"][ot * 128:(ot + 1) * 128, t0:t0 + TS], mixr[:, ot, CONVK - 1:],
                  "do", eng="pool")
        for ot in range(NE):
            c.dma(wstg[:, ot % 2, :], io["win"][ot], "dw")
            c.cp(wbf[:, ot % 2, :], wstg[:, ot % 2, :])
            for kc in range(16):
                c.mm(p1[:, 0:TS], wbf[:, ot % 2, kc * 128:(kc + 1) * 128], rnb[:, kc, :],
                     start=(kc == 0), stop=(kc == 15))
            c.cp(mixr[:, ot, CONVK - 1:], p1[:, 0:TS])
        for ot in range(NE):
            c.cp(mixr[:, ot, 0:CONVK - 1], tails[:, ot, :])
            c.cp(tails[:, ot, :], mixr[:, ot, TS:TS + CONVK - 1])
            c.ts(convt.full(), mixr[:, ot, 0:TS], cwsb[:, ot, 0:1], A.mult)
            for j in range(1, CONVK):
                c.stt(convt.full(), mixr[:, ot, j:j + TS], cwsb[:, ot, j:j + 1],
                      convt.full(), A.mult, A.add)
            c.act(mixr[:, ot, CONVK - 1:], convt.full(), AF.Silu, bias=cbsb[:, ot:ot + 1])
            c.dma(io["xg"][ot * 128:(ot + 1) * 128, t0:t0 + TS], mixr[:, ot, CONVK - 1:],
                  "do", eng="pool")
        for kc in range(16):
            c.mm(p0[:, 0:TS], wd1sb[:, kc, :], mixr[:, kc, CONVK - 1:],
                 start=(kc == 0), stop=(kc == 15))
        c.cp(outst[:, 0, :], p0[:, 0:TS])
        c.dma(io["dpre"][0:128, t0:t0 + TS], outst[:, 0, :], "do", eng="pool")
        for kc in range(16):
            c.mm(p1[:, 0:32] if False else p1[0:32, 0:TS], wbcsb[:, kc, :],
                 mixr[:, kc, CONVK - 1:], start=(kc == 0), stop=(kc == 15))
        c.cp(outst[0:32, 1, :], p1[0:32, 0:TS])
        c.dma(io["dpre"][128:160, t0:t0 + TS], outst[0:32, 1, :], "do", eng="pool")


def prep_l1pre(core, inp):
    sd = core % 2
    ws = np.asarray(inp["L1_attn_W_skip"])[:, sd * 2048:(sd + 1) * 2048]
    wi = np.asarray(inp["L1_attn_W_in"])[:, sd * 2048:(sd + 1) * 2048]
    cw = np.asarray(inp["L1_attn_conv_w"])[:, 0, sd * 2048:(sd + 1) * 2048].T
    cb = np.asarray(inp["L1_attn_conv_b"])[sd * 2048:(sd + 1) * 2048]
    nw = np.asarray(inp["L1_attn_norm_w"])
    wd1 = np.asarray(inp["L1_attn_W_delta_1"])[sd * 2048:(sd + 1) * 2048]
    wb = np.asarray(inp["L1_attn_W_B"])[sd * 2048:(sd + 1) * 2048]
    wc = np.asarray(inp["L1_attn_W_C"])[sd * 2048:(sd + 1) * 2048]
    wbc = np.concatenate([wb, wc], axis=1)
    return {
        "wskip": _tile_lhsT(ws, 16, 16),
        "win": _tile_lhsT(wi, 16, 16),
        "convw1": np.ascontiguousarray(cw.reshape(16, 128, 4).transpose(1, 0, 2), np.float32),
        "convb": np.ascontiguousarray(cb.reshape(16, 128).T, np.float32),
        "nrmw": np.ascontiguousarray(nw.reshape(16, 128).T, np.float32),
        "wd1": np.ascontiguousarray(wd1.reshape(16, 128, 128).transpose(1, 0, 2), np.float32),
        "wbc": np.ascontiguousarray(wbc.reshape(16, 128, 32).transpose(1, 0, 2), np.float32),
    }


def build_l1pre(T_used=T):
    nc = bass.Bass("TRN2", target_bir_lowering=False)
    c = Ctx(nc)
    TS_ = 512
    nsc = T_used // TS_
    io = {
        "x1f": c.din("x1f", [D, T_used], F32),
        "dpa": c.din("dpa", [D, T_used], BF16),
        "dpb": c.din("dpb", [D, T_used], BF16),
        "wskip": c.din("wskip", [16, 128, 16, 128], F32),
        "win": c.din("win", [16, 128, 16, 128], F32),
        "convw1": c.din("convw1", [128, 16, 4], F32),
        "convb": c.din("convb", [128, 16], F32),
        "nrmw": c.din("nrmw", [128, 16], F32),
        "wd1": c.din("wd1", [128, 16, 128], F32),
        "wbc": c.din("wbc", [128, 16, 32], F32),
        "x2f": c.dout("x2f", [D, T_used], F32),
        "ssil": c.dout("ssil", [2048, T_used], BF16),
        "xg": c.dout("xg", [2048, T_used], BF16),
        "dpre": c.dout("dpre", [160, T_used], BF16),
    }
    emit_l1pre(c, TS_, nsc, io)
    finalize_core(c)
    return nc


def emit_l1scan(c, TS, NSC, io):
    NE = 16
    N1_ = 16
    c.build_consts()
    ones1f = c.ones1f
    p0 = c.ps("s_p0", [128, 512])
    p1 = c.ps("s_p1", [128, 512])
    p2 = c.ps("s_p2", [128, 512])
    p5 = c.ps("s_p5", [128, 512])
    p6 = c.ps("s_p6", [128, 512])
    dbuf = c.sb("s_dbuf", [128, NE, TS], BF16)
    wxg = c.sb("s_wxg", [128, NE, TS], BF16)
    xgl = c.sb("s_xgl", [128, NE, TS], BF16)
    x2t = c.sb("s_x2t", [128, 2, TS], F32)
    bt_ = c.sb("s_bt", [128, 16, TS], BF16)
    xsq = c.sb("s_xsq", [128, 2, TS], BF16)
    rowp = c.sb("s_rowp", [1, 5 * TS], F32)
    lnr = c.sb("s_lnr", [128, 2, TS], F32)
    ygb = c.sb("s_ygb", [128, NE, TS], BF16)
    d32 = c.sb("s_d32", [32, 2, TS], BF16)
    d32f = c.sb("s_d32f", [32, TS], F32)
    brow = c.sb("s_brow", [1, 2, TS], F32)
    dect_p = [c.sb("s_dect0", [128, TS], F32), c.sb("s_dect1", [128, TS], F32)]
    usb_p = [c.sb("s_usb0", [128, TS], F32), c.sb("s_usb1", [128, TS], F32)]
    hb_p = [c.sb("s_hb0", [128, TS], BF16), c.sb("s_hb1", [128, TS], BF16)]
    dect = dect_p[0]
    usb = usb_p[0]
    tmp_ = usb_p[1]
    yacc = c.sb("s_yacc", [128, TS], F32)
    Bbc = c.sb("s_Bbc", [128, 16, TS], BF16)
    Cbc = c.sb("s_Cbc", [128, 16, TS], BF16)
    hst = c.sb("s_hst", [128, NE, N1_], F32)
    A1 = c.sb("s_A1", [128, NE, N1_], F32)
    A1st = c.sb("s_A1st", [128, NE, N1_], F32)
    bd2 = c.sb("s_bd2", [128, 16], F32)
    wD = c.sb("s_wD", [128, 16], F32)
    rwc = c.sb("s_rwc", [128, 1], F32)
    wstg = c.sb("s_wstg", [128, 2, 2048], F32)
    wbf = c.sb("s_wbf", [128, 2, 2048], BF16)
    wd2sb = c.sb("s_wd2", [128, 16, 128], BF16)
    wd2st = c.sb("s_wd2st", [128, 16, 128], F32)
    dsb = c.sb("s_dsb", [128, TS], BF16)
    sp1b = c.sb("s_sp1b", [128, TS], BF16)
    ssl = c.sb("s_ssl", [128, 2, TS], BF16)
    outst = c.sb("s_outst", [128, 1, TS], BF16)
    rt_ = c.sb("s_rt", [128, TS], F32)
    bscr = c.dint("s_bscr", [32, T], F32)
    c.memset(hst.full(), 0.0)
    c.dma(A1st.full(), io["alog1"].full(), "dw")
    c.act(A1.full(), A1st.full(), AF.Exp)
    c.ts(A1.full(), A1.full(), -1.0, A.mult)
    c.dma(bd2.full(), io["bd2"].full(), "dw")
    c.dma(wD.full(), io["wDc"].full(), "dw")
    c.dma(rwc.full(), io["rw"].full(), "dw")
    c.dma(wd2st.full(), io["wd2"].full(), "dw")
    c.cp(wd2sb.full(), wd2st.full())

    def ln_rows(src_tiles, n, pb1, pb2):
        for kt in range(n):
            c.mm(pb1[0:1, 0:TS], c.onesk_bf[:, 0:1], src_tiles(kt),
                 start=(kt == 0), stop=(kt == n - 1))
        m_ = rowp.tv_sub(0, TS)
        c.ts(m_, pb1[0:1, 0:TS], 1.0 / D, A.mult)
        for kt in range(n):
            c.act(xsq[:, kt % 2, :], src_tiles(kt), AF.Square)
            c.mm(pb2[0:1, 0:TS], c.onesk_bf[:, 0:1], xsq[:, kt % 2, :],
                 start=(kt == 0), stop=(kt == n - 1))
        sq_ = rowp.tv_sub(1, TS)
        var_ = rowp.tv_sub(2, TS)
        rst_ = rowp.tv_sub(3, TS)
        mr_ = rowp.tv_sub(4, TS)
        c.ts(sq_, pb2[0:1, 0:TS], 1.0 / D, A.mult)
        c.tt(var_, m_, m_, A.mult)
        c.tt(var_, sq_, var_, A.subtract)
        c.ts(var_, var_, EPS_LN, A.add)
        c.act(rst_, var_, AF.Ln)
        c.act(rst_, rst_, AF.Exp, scale=-0.5)
        c.tt(mr_, m_, rst_, A.mult)
        c.mm(pb1[:, 0:TS], ones1f[0:1, :], rst_)
        c.cp(lnr[:, 0, :], pb1[:, 0:TS])
        c.mm(pb2[:, 0:TS], ones1f[0:1, :], mr_)
        c.cp(lnr[:, 1, :], pb2[:, 0:TS])

    for sc in range(NSC):
        t0 = sc * TS
        c.dma(dsb.full(), io["dpa"][0:128, t0:t0 + TS], "dx")
        if "dpb" in io:
            c.dma(sp1b.full(), io["dpb"][0:128, t0:t0 + TS], "dx")
            c.tt(dsb.full(), dsb.full(), sp1b.full(), A.add)
        c.dma(d32[:, 0, :], io["dpa"][128:160, t0:t0 + TS], "dx")
        if "dpb" in io:
            c.dma(d32[:, 1, :], io["dpb"][128:160, t0:t0 + TS], "dx")
            c.tt(d32[:, 0, :], d32[:, 0, :], d32[:, 1, :], A.add)
        c.cp(d32f.full(), d32[:, 0, :])
        c.dma(bscr[:, t0:t0 + TS], d32f.full(), "dbs", eng="pool")
        for n in range(N1_):
            c.dma(brow[0:1, 0, :], bscr.tv_custom(
                lambda ap, n=n, t0=t0: ap[n:n + 1, t0:t0 + TS], n, n, t0, t0 + TS - 1), "dbr")
            c.mm([p5, p6][n % 2][:, 0:TS], ones1f[0:1, :], brow[0:1, 0, :])
            c.cp(Bbc[:, n, :], [p5, p6][n % 2][:, 0:TS])
            c.dma(brow[0:1, 1, :], bscr.tv_custom(
                lambda ap, n=n, t0=t0: ap[16 + n:17 + n, t0:t0 + TS], 16 + n, 16 + n,
                t0, t0 + TS - 1), "dbr")
            c.mm([p2, p1][n % 2][:, 0:TS], ones1f[0:1, :], brow[0:1, 1, :])
            c.cp(Cbc[:, n, :], [p2, p1][n % 2][:, 0:TS])
        for et in range(NE):
            c.mm(p0[:, 0:TS], wd2sb[:, et, :], dsb.full())
            c.ts(tmp_.full(), p0[:, 0:TS], bd2[:, et:et + 1], A.add)
            c.act(usb.full(), tmp_.full(), AF.Abs)
            c.act(usb.full(), usb.full(), AF.Exp, scale=-1.0)
            c.act(usb.full(), usb.full(), AF.Ln, bias=1.0)
            c.ts(tmp_.full(), tmp_.full(), 0.0, A.max)
            c.tt(dbuf[:, et, :], tmp_.full(), usb.full(), A.add)
            c.dma(xgl[:, et, :], io["xg"][et * 128:(et + 1) * 128, t0:t0 + TS], "dx")
            c.tt(wxg[:, et, :], dbuf[:, et, :], xgl[:, et, :], A.mult)
        for et in range(NE):
            def scan_n(n, et=et):
                Pn = n % 2
                dectn, usbn, hbn, tmpn = dect_p[Pn], usb_p[Pn], hb_p[Pn], dect_p[Pn]
                c.act(dectn.full(), dbuf[:, et, :], AF.Exp, scale=A1[:, et, n:n + 1])
                yield
                c.tt(usbn.full(), wxg[:, et, :], Bbc[:, n, :], A.mult, eng="pool")
                yield
                c.scan(hbn.full(), dectn.full(), usbn.full(), hst[:, et, n:n + 1], A.mult, A.add)
                yield
                c.cp(hst[:, et, n:n + 1], hbn[:, TS - 1:TS])
                yield
                if n == 0:
                    c.tt(yacc.full(), hbn.full(), Cbc[:, n, :], A.mult, eng="pool")
                else:
                    c.tt(tmpn.full(), hbn.full(), Cbc[:, n, :], A.mult, eng="pool")
                    yield
                    c.tt(yacc.full(), yacc.full(), tmpn.full(), A.add, eng="pool")

            def drive2(gens):
                gens = [g for g in gens]
                while gens:
                    nxt = []
                    for g in gens:
                        try:
                            next(g)
                            nxt.append(g)
                        except StopIteration:
                            pass
                    gens = nxt
            for n0 in range(0, N1_, 2):
                drive2([scan_n(n0), scan_n(n0 + 1)])
            c.stt(yacc.full(), xgl[:, et, :], wD[:, et:et + 1], yacc.full(), A.mult, A.add)
            c.dma(ssl[:, et % 2, :], io["ssil"][et * 128:(et + 1) * 128, t0:t0 + TS], "dx")
            c.tt(ygb[:, et, :], yacc.full(), ssl[:, et % 2, :], A.mult)
        for kt in range(16):
            c.dma(x2t[:, kt % 2, :], io["x2f"][kt * 128:(kt + 1) * 128, t0:t0 + TS], "dx")
            c.cp(bt_[:, kt, :], x2t[:, kt % 2, :])
        ln_rows(lambda kt: bt_[:, kt, :], 16, p5, p6)
        for dt_ in range(16):
            c.dma(wstg[:, dt_ % 2, :], io["wout1"][dt_], "dw")
            c.cp(wbf[:, dt_ % 2, :], wstg[:, dt_ % 2, :])
            for et in range(NE):
                c.mm([p0, p1][dt_ % 2][:, 0:TS], wbf[:, dt_ % 2, et * 128:(et + 1) * 128],
                     ygb[:, et, :], start=(et == 0), stop=(et == 15))
            c.tt(rt_.full(), bt_[:, dt_, :], lnr[:, 0, :], A.mult)
            c.tt(rt_.full(), rt_.full(), lnr[:, 1, :], A.subtract)
            c.stt(outst[:, 0, :], rt_.full(), rwc[:, 0:1],
                  [p0, p1][dt_ % 2][:, 0:TS], A.mult, A.add)
            c.dma(io["wpart"][dt_ * 128:(dt_ + 1) * 128, t0:t0 + TS],
                  outst[:, 0, :], "do", eng="pool")


def prep_l1scan(core, inp):
    sd = core % 2
    wd2 = np.asarray(inp["L1_attn_W_delta_2"])[:, sd * 2048:(sd + 1) * 2048]
    bd2 = np.asarray(inp["L1_attn_b_delta_2"])[sd * 2048:(sd + 1) * 2048]
    al = np.asarray(inp["L1_attn_A_log"])[sd * 2048:(sd + 1) * 2048]
    wD = np.asarray(inp["L1_attn_W_D"])[sd * 2048:(sd + 1) * 2048]
    wo = np.asarray(inp["L1_attn_W_out"])[sd * 2048:(sd + 1) * 2048]
    return {
        "wd2": np.ascontiguousarray(wd2.reshape(128, 16, 128), np.float32),
        "bd2": np.ascontiguousarray(bd2.reshape(16, 128).T, np.float32),
        "alog1": np.ascontiguousarray(al.reshape(16, 128, 16).transpose(1, 0, 2), np.float32),
        "wDc": np.ascontiguousarray(wD.reshape(16, 128).T, np.float32),
        "wout1": _tile_lhsT(wo, 16, 16),
        "rw": np.full((128, 1), 1.0 if sd == 0 else 0.0, np.float32),
    }


def build_l1scan(T_used=T):
    nc = bass.Bass("TRN2", target_bir_lowering=False)
    c = Ctx(nc)
    TS_ = 512
    nsc = T_used // TS_
    io = {
        "x2f": c.din("x2f", [D, T_used], F32),
        "dpa": c.din("dpa", [160, T_used], BF16),
        "dpb": c.din("dpb", [160, T_used], BF16),
        "xg": c.din("xg", [2048, T_used], BF16),
        "ssil": c.din("ssil", [2048, T_used], BF16),
        "wd2": c.din("wd2", [128, 16, 128], F32),
        "bd2": c.din("bd2", [128, 16], F32),
        "alog1": c.din("alog1", [128, 16, 16], F32),
        "wDc": c.din("wDc", [128, 16], F32),
        "wout1": c.din("wout1", [16, 128, 16, 128], F32),
        "rw": c.din("rw", [128, 1], F32),
        "wpart": c.dout("wpart", [D, T_used], BF16),
    }
    emit_l1scan(c, TS_, nsc, io)
    finalize_core(c)
    return nc


def emit_l1post(c, TS, NSC, io):
    NH = 32
    c.build_consts()
    ones1f = c.ones1f
    p0 = c.ps("e_p0", [128, 512])
    p1 = c.ps("e_p1", [128, 512])
    p5 = c.ps("e_p5", [128, 512])
    p6 = c.ps("e_p6", [128, 512])
    aa = c.sb("e_aa", [128, 16, TS], BF16)
    bb = c.sb("e_bb", [128, 16, TS], F32)
    bt_ = c.sb("e_bt", [128, 16, TS], BF16)
    x3b = c.sb("e_x3b", [128, 16, TS], BF16)
    xsq = c.sb("e_xsq", [128, 2, TS], BF16)
    rowp = c.sb("e_rowp", [1, 8 * TS], F32)
    lnr = c.sb("e_lnr", [128, 2, TS], F32)
    gt = c.sb("e_gt", [128, TS], F32)
    wstg = c.sb("e_wstg", [128, 2, 2048], F32)
    wsg = c.sb("e_wsg", [128, 2, 2048], BF16)
    hbuf = c.sb("e_hbuf", [128, NH, TS], BF16)
    upsc = c.sb("e_upsc", [128, NH], F32)
    outst = c.sb("e_outst", [128, 2, TS], BF16)
    c.dma(upsc.full(), io["upsc"].full(), "dw")

    def ln_rows(src_tiles, n, pb1, pb2):
        for kt in range(n):
            c.mm(pb1[0:1, 0:TS], c.onesk_bf[:, 0:1], src_tiles(kt),
                 start=(kt == 0), stop=(kt == n - 1))
        m_ = rowp.tv_sub(0, TS)
        c.ts(m_, pb1[0:1, 0:TS], 1.0 / D, A.mult)
        for kt in range(n):
            c.act(xsq[:, kt % 2, :], src_tiles(kt), AF.Square)
            c.mm(pb2[0:1, 0:TS], c.onesk_bf[:, 0:1], xsq[:, kt % 2, :],
                 start=(kt == 0), stop=(kt == n - 1))
        sq_ = rowp.tv_sub(1, TS)
        var_ = rowp.tv_sub(2, TS)
        rst_ = rowp.tv_sub(3, TS)
        mr_ = rowp.tv_sub(4, TS)
        c.ts(sq_, pb2[0:1, 0:TS], 1.0 / D, A.mult)
        c.tt(var_, m_, m_, A.mult)
        c.tt(var_, sq_, var_, A.subtract)
        c.ts(var_, var_, EPS_LN, A.add)
        c.act(rst_, var_, AF.Ln)
        c.act(rst_, rst_, AF.Exp, scale=-0.5)
        c.tt(mr_, m_, rst_, A.mult)
        c.mm(pb1[:, 0:TS], ones1f[0:1, :], rst_)
        c.cp(lnr[:, 0, :], pb1[:, 0:TS])
        c.mm(pb2[:, 0:TS], ones1f[0:1, :], mr_)
        c.cp(lnr[:, 1, :], pb2[:, 0:TS])

    for sc in range(NSC):
        t0 = sc * TS
        for kt in range(16):
            c.dma(aa[:, kt, :], io["wpa"][kt * 128:(kt + 1) * 128, t0:t0 + TS], "dx")
            if "wpb" in io:
                c.dma(bb[:, kt, :], io["wpb"][kt * 128:(kt + 1) * 128, t0:t0 + TS], "dx")
                c.tt(aa[:, kt, :], aa[:, kt, :], bb[:, kt, :], A.add)
            c.cp(bt_[:, kt, :], aa[:, kt, :])
        ln_rows(lambda kt: bt_[:, kt, :], 16, p5, p6)
        for kt in range(16):
            c.dma(bb[:, kt, :], io["x2f"][kt * 128:(kt + 1) * 128, t0:t0 + TS], "dx")
        for kt in range(16):
            c.tt(gt.full(), aa[:, kt, :], lnr[:, 0, :], A.mult)
            c.tt(gt.full(), gt.full(), lnr[:, 1, :], A.subtract)
            c.tt(bb[:, kt, :], bb[:, kt, :], gt.full(), A.add)
            c.dma(io["x3f"][kt * 128:(kt + 1) * 128, t0:t0 + TS], bb[:, kt, :], "do", eng="pool")
            c.cp(bt_[:, kt, :], bb[:, kt, :])
        ln_rows(lambda kt: bt_[:, kt, :], 16, p5, p6)
        for kt in range(16):
            c.tt(x3b[:, kt, :], bb[:, kt, :], lnr[:, 0, :], A.mult)
            c.tt(x3b[:, kt, :], x3b[:, kt, :], lnr[:, 1, :], A.subtract)
        for ht in range(NH):
            c.dma(wstg[:, ht % 2, :], io["upw"][ht], "dw")
            c.act(wsg[:, ht % 2, :], wstg[:, ht % 2, :], AF.Sign)
            pb = [p0, p1][ht % 2]
            for kc in range(16):
                c.mm(pb[:, 0:TS], wsg[:, ht % 2, kc * 128:(kc + 1) * 128], x3b[:, kc, :],
                     start=(kc == 0), stop=(kc == 15))
            c.act(hbuf[:, ht, :], pb[:, 0:TS], AF.Relu, scale=upsc[:, ht:ht + 1])
        for dt_ in range(16):
            pb = [p0, p1][dt_ % 2]
            for g2 in range(2):
                c.dma(wstg[:, dt_ % 2, :], io["dww"].tv_custom(
                    lambda ap, dt_=dt_, g2=g2: ap[dt_, :, g2 * 16:(g2 + 1) * 16, :].rearrange("b a c -> b a c"),
                    dt_, dt_, g2 * 16 * 128 * 128, (g2 + 1) * 16 * 128 * 128 - 1), "dw")
                c.act(wsg[:, dt_ % 2, :], wstg[:, dt_ % 2, :], AF.Sign)
                for j in range(16):
                    kc = g2 * 16 + j
                    c.mm(pb[:, 0:TS], wsg[:, dt_ % 2, j * 128:(j + 1) * 128], hbuf[:, kc, :],
                         start=(kc == 0), stop=(kc == 31))
            c.cp(outst[:, dt_ % 2, :], pb[:, 0:TS])
            c.dma(io["dp2"][dt_ * 128:(dt_ + 1) * 128, t0:t0 + TS],
                  outst[:, dt_ % 2, :], "do", eng="pool")


def prep_l1post(core, inp):
    sd = core % 2
    upw = np.asarray(inp["L1_ffn_up_weight"])[sd * 4096:(sd + 1) * 4096]
    us = np.asarray(inp["L1_ffn_up_scale"])[sd * 4096:(sd + 1) * 4096]
    dw = np.asarray(inp["L1_ffn_down_weight"])[:, sd * 4096:(sd + 1) * 4096]
    return {
        "upw": _tile_lhsT(np.ascontiguousarray(upw.T), 16, 32),
        "upsc": np.ascontiguousarray(us.reshape(32, 128).T, np.float32),
        "dww": _tile_lhsT(np.ascontiguousarray(dw.T), 32, 16),
    }


def build_l1post(T_used=T):
    nc = bass.Bass("TRN2", target_bir_lowering=False)
    c = Ctx(nc)
    TS_ = 512
    nsc = T_used // TS_
    io = {
        "x2f": c.din("x2f", [D, T_used], F32),
        "wpa": c.din("wpa", [D, T_used], BF16),
        "wpb": c.din("wpb", [D, T_used], BF16),
        "upw": c.din("upw", [32, 128, 16, 128], F32),
        "upsc": c.din("upsc", [128, 32], F32),
        "dww": c.din("dww", [16, 128, 32, 128], F32),
        "x3f": c.dout("x3f", [D, T_used], F32),
        "dp2": c.dout("dp2", [D, T_used], BF16),
    }
    emit_l1post(c, TS_, nsc, io)
    finalize_core(c)
    return nc


def emit_final(c, TS, NSC, io):
    c.build_consts()
    ones1f = c.ones1f
    p5 = c.ps("f_p5", [128, 512])
    p6 = c.ps("f_p6", [128, 512])
    aa = c.sb("f_aa", [128, 16, TS], BF16)
    bb = c.sb("f_bb", [128, 16, TS], F32)
    bt_ = c.sb("f_bt", [128, 16, TS], BF16)
    xsq = c.sb("f_xsq", [128, 2, TS], BF16)
    rowp = c.sb("f_rowp", [1, 8 * TS], F32)
    lnr = c.sb("f_lnr", [128, 2, TS], F32)
    gt = c.sb("f_gt", [128, TS], F32)
    dsc = c.sb("f_dsc", [128, 16], F32)
    osc = c.sb("f_osc", [1, 1], F32)
    oscb = c.sb("f_oscb", [128, 1], F32)
    c.dma(dsc.full(), io["dscale"].full(), "dw")
    c.dma(osc.full(), io["oscale"].full(), "dw")
    c.mm(p5[:, 0:1], ones1f[0:1, :], osc[0:1, 0:1])
    c.cp(oscb.full(), p5[:, 0:1])

    def ln_rows(src_tiles, n, pb1, pb2):
        for kt in range(n):
            c.mm(pb1[0:1, 0:TS], c.onesk_bf[:, 0:1], src_tiles(kt),
                 start=(kt == 0), stop=(kt == n - 1))
        m_ = rowp.tv_sub(0, TS)
        c.ts(m_, pb1[0:1, 0:TS], 1.0 / D, A.mult)
        for kt in range(n):
            c.act(xsq[:, kt % 2, :], src_tiles(kt), AF.Square)
            c.mm(pb2[0:1, 0:TS], c.onesk_bf[:, 0:1], xsq[:, kt % 2, :],
                 start=(kt == 0), stop=(kt == n - 1))
        sq_ = rowp.tv_sub(1, TS)
        var_ = rowp.tv_sub(2, TS)
        rst_ = rowp.tv_sub(3, TS)
        mr_ = rowp.tv_sub(4, TS)
        c.ts(sq_, pb2[0:1, 0:TS], 1.0 / D, A.mult)
        c.tt(var_, m_, m_, A.mult)
        c.tt(var_, sq_, var_, A.subtract)
        c.ts(var_, var_, EPS_LN, A.add)
        c.act(rst_, var_, AF.Ln)
        c.act(rst_, rst_, AF.Exp, scale=-0.5)
        c.tt(mr_, m_, rst_, A.mult)
        c.mm(pb1[:, 0:TS], ones1f[0:1, :], rst_)
        c.cp(lnr[:, 0, :], pb1[:, 0:TS])
        c.mm(pb2[:, 0:TS], ones1f[0:1, :], mr_)
        c.cp(lnr[:, 1, :], pb2[:, 0:TS])

    for sc in range(NSC):
        t0 = sc * TS
        for kt in range(16):
            c.dma(aa[:, kt, :], io["da"][kt * 128:(kt + 1) * 128, t0:t0 + TS], "dx")
            if "db" in io:
                c.dma(bb[:, kt, :], io["db"][kt * 128:(kt + 1) * 128, t0:t0 + TS], "dx")
                c.tt(aa[:, kt, :], aa[:, kt, :], bb[:, kt, :], A.add)
            c.ts(aa[:, kt, :], aa[:, kt, :], dsc[:, kt:kt + 1], A.mult)
            c.ts(aa[:, kt, :], aa[:, kt, :], oscb[:, 0:1], A.mult)
            c.cp(bt_[:, kt, :], aa[:, kt, :])
        ln_rows(lambda kt: bt_[:, kt, :], 16, p5, p6)
        for kt in range(16):
            c.dma(bb[:, kt, :], io["x3f"][kt * 128:(kt + 1) * 128, t0:t0 + TS], "dx")
        for kt in range(16):
            c.tt(gt.full(), aa[:, kt, :], lnr[:, 0, :], A.mult)
            c.tt(gt.full(), gt.full(), lnr[:, 1, :], A.subtract)
            c.tt(bb[:, kt, :], bb[:, kt, :], gt.full(), A.add)
            c.dma(io["xout"][kt * 128:(kt + 1) * 128, t0:t0 + TS], bb[:, kt, :], "do", eng="pool")


def prep_final(core, inp):
    ds = np.asarray(inp["L1_ffn_down_scale"])
    os_ = np.asarray(inp["L1_ffn_output_scale"])
    return {
        "dscale": np.ascontiguousarray(ds.reshape(16, 128).T, np.float32),
        "oscale": np.ascontiguousarray(os_.reshape(1, 1), np.float32),
    }


def build_final(T_used=T):
    nc = bass.Bass("TRN2", target_bir_lowering=False)
    c = Ctx(nc)
    TS_ = 512
    nsc = T_used // TS_
    io = {
        "x3f": c.din("x3f", [D, T_used], F32),
        "da": c.din("da", [D, T_used], BF16),
        "db": c.din("db", [D, T_used], BF16),
        "dscale": c.din("dscale", [128, 16], F32),
        "oscale": c.din("oscale", [1, 1], F32),
        "xout": c.dout("xout", [D, T_used], F32),
    }
    emit_final(c, TS_, nsc, io)
    finalize_core(c)
    return nc


def kernel_unfused(**inputs):
    ncores = 8
    nc1 = build_l0attn()
    maps1 = [prep_l0attn(cr, inputs) for cr in range(ncores)]
    r1 = run_bass_kernel_spmd(nc1, maps1, core_ids=list(range(ncores)))
    parts = [r1.results[cr]["part"] for cr in range(ncores)]
    xf = [maps1[cr]["xfeat"] for cr in range(ncores)]
    nc2 = build_l0mid()
    maps2 = []
    for cr in range(ncores):
        m = prep_l0mid(cr, inputs)
        m["xfeat"] = xf[cr]
        m["pa"] = parts[cr]
        m["pb"] = parts[cr ^ 1]
        maps2.append(m)
    r2 = run_bass_kernel_spmd(nc2, maps2, core_ids=list(range(ncores)))
    x1f = [r2.results[cr]["x1f"] for cr in range(ncores)]
    dpart = [r2.results[cr]["dpart"] for cr in range(ncores)]
    nc3 = build_l1pre()
    maps3 = []
    for cr in range(ncores):
        m = prep_l1pre(cr, inputs)
        m["x1f"] = x1f[cr]
        m["dpa"] = dpart[cr]
        m["dpb"] = dpart[cr ^ 1]
        maps3.append(m)
    r3 = run_bass_kernel_spmd(nc3, maps3, core_ids=list(range(ncores)))
    x2f = [r3.results[cr]["x2f"] for cr in range(ncores)]
    ssil = [r3.results[cr]["ssil"] for cr in range(ncores)]
    xg = [r3.results[cr]["xg"] for cr in range(ncores)]
    dpre = [r3.results[cr]["dpre"] for cr in range(ncores)]
    nc4 = build_l1scan()
    maps4 = []
    for cr in range(ncores):
        m = prep_l1scan(cr, inputs)
        m["x2f"] = x2f[cr]
        m["dpa"] = dpre[cr]
        m["dpb"] = dpre[cr ^ 1]
        m["xg"] = xg[cr].view(ml_dtypes.bfloat16) if xg[cr].dtype != ml_dtypes.bfloat16 else xg[cr]
        m["ssil"] = ssil[cr].view(ml_dtypes.bfloat16) if ssil[cr].dtype != ml_dtypes.bfloat16 else ssil[cr]
        maps4.append(m)
    r4 = run_bass_kernel_spmd(nc4, maps4, core_ids=list(range(ncores)))
    wpart = [r4.results[cr]["wpart"] for cr in range(ncores)]
    nc5 = build_l1post()
    maps5 = []
    for cr in range(ncores):
        m = prep_l1post(cr, inputs)
        m["x2f"] = x2f[cr]
        m["wpa"] = wpart[cr]
        m["wpb"] = wpart[cr ^ 1]
        maps5.append(m)
    r5 = run_bass_kernel_spmd(nc5, maps5, core_ids=list(range(ncores)))
    x3f = [r5.results[cr]["x3f"] for cr in range(ncores)]
    dp2 = [r5.results[cr]["dp2"] for cr in range(ncores)]
    nc6 = build_final()
    maps6 = []
    for cr in range(ncores):
        m = prep_final(cr, inputs)
        m["x3f"] = x3f[cr]
        m["da"] = dp2[cr]
        m["db"] = dp2[cr ^ 1]
        maps6.append(m)
    r6 = run_bass_kernel_spmd(nc6, maps6, core_ids=list(range(ncores)))
    out = np.empty((B, T, D), np.float32)
    for b_ in range(B):
        out[b_] = r6.results[2 * b_]["xout"].T
    return out


PAIRS = [[0, 1], [2, 3], [4, 5], [6, 7]]


def build_fused(T_used=T):
    nc = bass.Bass("TRN2", target_bir_lowering=False)
    c = Ctx(nc)
    TS_ = 512
    nsc = T_used // TS_
    din, dout, dint = c.din, c.dout, c.dint
    xfeat = din("xfeat", [D, T_used], F32)
    part = dint("i_part", [D, T_used], BF16)
    parts = dint("i_parts", [D, T_used], BF16)
    x1f = dint("i_x1f", [D, T_used], F32)
    dpart = dint("i_dpart", [D, T_used], BF16)
    dparts = dint("i_dparts", [D, T_used], BF16)
    x2f = dint("i_x2f", [D, T_used], F32)
    ssil = dint("i_ssil", [2048, T_used], BF16)
    xg = dint("i_xg", [2048, T_used], BF16)
    dpre = dint("i_dpre", [160, T_used], BF16)
    dpres = dint("i_dpres", [160, T_used], BF16)
    wpart = dint("i_wpart", [D, T_used], BF16)
    wparts = dint("i_wparts", [D, T_used], BF16)
    x3f = dint("i_x3f", [D, T_used], F32)
    dp2 = dint("i_dp2", [D, T_used], BF16)
    dp2s = dint("i_dp2s", [D, T_used], BF16)
    xout = dout("xout", [D, T_used], F32)

    c.build_consts()
    ioA = {
        "xfeat": xfeat, "part": part,
        "wqkv": din("wqkv", [32, 128, 16, 128], F32),
        "wz": din("wz", [16, 128, 16, 128], F32),
        "wba": din("wba", [128, 16, 32], F32),
        "convw": din("convw", [128, 32, 4], F32),
        "alog": din("alog", [1, 16], F32),
        "dtb": din("dtb", [1, 16], F32),
        "nw": din("nw", [128, 1], F32),
        "wout": din("wout", [16, 128, 16, 128], F32),
    }
    c.phase_begin()
    emit_l0attn(c, TS_, nsc, ioA)
    c.phase_end()
    c.s.cc("AllReduce", A.add, PAIRS, part.full(), parts.full())
    ioB = {
        "xfeat": xfeat, "pa": parts, "x1f": x1f, "dpart": dpart,
        "gatep": din("gatep", [64, 128, 1024], U8),
        "gatep_s": din("gatep_s", [64, 128, 64], F32),
        "upp": din("upp", [64, 128, 1024], U8),
        "upp_s": din("upp_s", [64, 128, 64], F32),
        "downp": din("downp", [16, 128, 4096], U8),
        "downp_s": din("downp_s", [16, 128, 256], F32),
    }
    c.phase_begin()
    emit_l0mid(c, TS_, nsc, ioB)
    c.phase_end()
    c.s.cc("AllReduce", A.add, PAIRS, dpart.full(), dparts.full())
    ioC = {
        "x1f": x1f, "dpa": dparts, "x2f": x2f, "ssil": ssil, "xg": xg, "dpre": dpre,
        "wskip": din("wskip", [16, 128, 16, 128], F32),
        "win": din("win", [16, 128, 16, 128], F32),
        "convw1": din("convw1", [128, 16, 4], F32),
        "convb": din("convb", [128, 16], F32),
        "nrmw": din("nrmw", [128, 16], F32),
        "wd1": din("wd1", [128, 16, 128], F32),
        "wbc": din("wbc", [128, 16, 32], F32),
    }
    c.phase_begin()
    emit_l1pre(c, TS_, nsc, ioC)
    c.phase_end()
    c.s.cc("AllReduce", A.add, PAIRS, dpre.full(), dpres.full())
    ioD = {
        "x2f": x2f, "dpa": dpres, "xg": xg, "ssil": ssil, "wpart": wpart,
        "wd2": din("wd2", [128, 16, 128], F32),
        "bd2": din("bd2", [128, 16], F32),
        "alog1": din("alog1", [128, 16, 16], F32),
        "wDc": din("wDc", [128, 16], F32),
        "wout1": din("wout1", [16, 128, 16, 128], F32),
        "rw": din("rw", [128, 1], F32),
    }
    c.phase_begin()
    emit_l1scan(c, TS_, nsc, ioD)
    c.phase_end()
    c.s.cc("AllReduce", A.add, PAIRS, wpart.full(), wparts.full())
    ioE = {
        "x2f": x2f, "wpa": wparts, "x3f": x3f, "dp2": dp2,
        "upw": din("upw", [32, 128, 16, 128], F32),
        "upsc": din("upsc", [128, 32], F32),
        "dww": din("dww", [16, 128, 32, 128], F32),
    }
    c.phase_begin()
    emit_l1post(c, TS_, nsc, ioE)
    c.phase_end()
    c.s.cc("AllReduce", A.add, PAIRS, dp2.full(), dp2s.full())
    ioF = {
        "x3f": x3f, "da": dp2s, "xout": xout,
        "dscale": din("dscale", [128, 16], F32),
        "oscale": din("oscale", [1, 1], F32),
    }
    c.phase_begin()
    emit_final(c, TS_, nsc, ioF)
    c.phase_end()
    finalize_core(c)
    return nc


def prep_fused(core, inputs):
    m = {}
    m.update(prep_l0attn(core, inputs))
    m.update(prep_l0mid(core, inputs))
    m.update(prep_l1pre(core, inputs))
    m.update(prep_l1scan(core, inputs))
    m.update(prep_l1post(core, inputs))
    m.update(prep_final(core, inputs))
    return m


def kernel(**inputs):
    ncores = 8
    nc = build_fused()
    maps = [prep_fused(cr, inputs) for cr in range(ncores)]
    r = run_bass_kernel_spmd(nc, maps, core_ids=list(range(ncores)))
    out = np.empty((B, T, D), np.float32)
    for b_ in range(B):
        out[b_] = r.results[2 * b_]["xout"].T
    return out


# version 12 (final)
# speedup vs baseline: 1.1623x; 1.1623x over previous
import bisect
import math
import numpy as np
import ml_dtypes
import concourse.bass as bass
import concourse.mybir as mybir
from concourse.bass_utils import run_bass_kernel_spmd

F32 = mybir.dt.float32
BF16 = mybir.dt.bfloat16
U8 = mybir.dt.uint8
A = mybir.AluOpType
AF = mybir.ActivationFunctionType
AX = mybir.AxisListType

B, T, D = 4, 4096, 2048
DK = DV = 128
HKh, HVh = 8, 16
QC, KC, VC = 1024, 1024, 2048
MIXC = QC + KC + VC
CONVK = 4
CH = 128
NCH = T // CH
EPS_L2 = 1e-6
EPS_CORE = 1e-6
EPS_LN = 1e-5
EPS_RMS = 1e-5
DKSC = DK ** -0.5

ENG = ("pe", "act", "dve", "pool", "sp")
_ENG_ATTR = {"pe": "tensor", "act": "scalar", "dve": "vector", "pool": "gpsimd", "sp": "sync"}


def cdiv(a, b):
    return (a + b - 1) // b


class TV:
    __slots__ = ("mk", "key", "p0", "p1", "e0", "e1")

    def __init__(self, mk, key, p0, p1, e0, e1):
        self.mk, self.key, self.p0, self.p1, self.e0, self.e1 = mk, key, p0, p1, e0, e1

    def ap(self):
        return self.mk()


class Buf:

    def __init__(self, handle, name, shape, whole=False):
        self.h = handle
        self.name = name
        self.shape = list(shape)
        self.whole = whole

    def _rng(self, sl, n):
        if isinstance(sl, int):
            return sl, sl
        start, stop, step = sl.indices(n)
        return start, max(start, stop - 1)

    def __getitem__(self, idx):
        if not isinstance(idx, tuple):
            idx = (idx,)
        idx = tuple(idx) + tuple(slice(None) for _ in range(len(self.shape) - len(idx)))
        p0, p1 = self._rng(idx[0], self.shape[0])
        dims = self.shape[1:]
        strides = []
        s = 1
        for d in reversed(dims):
            strides.append(s)
            s *= d
        strides = list(reversed(strides))
        lo = hi = 0
        for sl, n, st in zip(idx[1:], dims, strides):
            a0, a1 = self._rng(sl, n)
            lo += a0 * st
            hi += a1 * st
        if self.whole:
            p0, p1 = 0, self.shape[0] - 1
            lo, hi = 0, int(np.prod(self.shape[1:])) - 1 if len(self.shape) > 1 else 0
        h = self.h

        def mk(idx=idx, h=h):
            ap = h.ap() if hasattr(h, "ap") else h
            return ap[idx]
        return TV(mk, self.name, p0, p1, lo, hi)

    def bcf(self, idx, n):
        tv = self[idx]

        def mk(base=tv.mk, nn=n):
            ap = base()
            shape = list(ap.shape)
            shape[-1] = nn
            return ap.broadcast_to(shape)
        return TV(mk, tv.key, tv.p0, tv.p1, tv.e0, tv.e1)

    def tv_sub(self, i, n):
        return self[0:1, i * n:(i + 1) * n]

    def full(self):
        return self[tuple(slice(None) for _ in self.shape)]

    def tv_custom(self, apfn, p0, p1, e0, e1):
        h = self.h

        def mk():
            ap = h.ap() if hasattr(h, "ap") else h
            return apfn(ap)
        return TV(mk, self.name, p0, p1, e0, e1)


def _overlap(a, b):
    return not (a.p1 < b.p0 or b.p1 < a.p0 or a.e1 < b.e0 or b.e1 < a.e0)


class Rec:
    __slots__ = ("eng", "idx", "tv")

    def __init__(self, eng, idx, tv):
        self.eng, self.idx, self.tv = eng, idx, tv


class Op:
    __slots__ = ("fn", "waits", "incs")

    def __init__(self, fn):
        self.fn = fn
        self.waits = []
        self.incs = []


class Sched:

    def __init__(self, nc):
        self.nc = nc
        self.ops = {e: [] for e in ENG}
        self.icount = {e: 0 for e in ENG}
        self.incpts = {e: ([], []) for e in ENG}
        self.semval = {e: 0 for e in ENG}
        self.waited = {e: {} for e in ENG}
        self.dmacnt = {}
        self.dmaq = {}
        self.last_join = {}
        self.dwaited = {e: {} for e in ENG}
        self.board = {}
        self.semnames = set()

    def _deps_for(self, reads, writes):
        deps = []
        for tv in reads:
            ent = self.board.get(tv.key)
            if not ent:
                continue
            for rec in ent["w"]:
                if _overlap(rec.tv, tv):
                    deps.append(rec)
        for tv in writes:
            ent = self.board.get(tv.key)
            if not ent:
                continue
            for rec in ent["w"]:
                if _overlap(rec.tv, tv):
                    deps.append(rec)
            for rec in ent["r"]:
                if _overlap(rec.tv, tv):
                    deps.append(rec)
        return deps

    def _record(self, eng, idx, reads, writes):
        for tv, kind in [(t, "r") for t in reads] + [(t, "w") for t in writes]:
            ent = self.board.setdefault(tv.key, {"w": [], "r": []})
            lst = ent[kind]
            lst.append(Rec(eng, idx, tv))
            if len(lst) > 96:
                del lst[:-64]

    def _resolve_inc(self, eng, idx):
        idxs, vals = self.incpts[eng]
        i = bisect.bisect_left(idxs, idx)
        if i < len(idxs):
            return vals[i]
        self.semval[eng] += 1
        v = self.semval[eng]
        self.ops[eng][idx].incs.append(("__eng_" + eng, 1))
        self.semnames.add("__eng_" + eng)
        idxs.append(idx)
        vals.append(v)
        return v

    def _join_dma(self, name, cnt):
        qeng = self.dmaq[name]
        last = self.last_join.get(name)
        if last is not None and last[1] >= cnt:
            return qeng, last[0]
        cur = self.dmacnt[name]
        inc = 1 if name == "dcc" else 16
        op = Op(lambda q: None)
        op.waits.append((name, inc * cur))
        self.semnames.add(name)
        idx = self.icount[qeng]
        self.ops[qeng].append(op)
        self.icount[qeng] += 1
        self.last_join[name] = (idx, cur)
        return qeng, idx

    def _apply_deps(self, eng, op, deps, ddeps):
        need = {}
        for rec in deps:
            if rec.eng == "pe" and eng == "pe":
                continue
            v = self._resolve_inc(rec.eng, rec.idx)
            if need.get(rec.eng, 0) < v:
                need[rec.eng] = v
        for name, cnt in sorted(ddeps.items()):
            qeng, jidx = self._join_dma(name, cnt)
            if qeng == eng:
                continue
            v = self._resolve_inc(qeng, jidx)
            if need.get(qeng, 0) < v:
                need[qeng] = v
        for e2, v in sorted(need.items()):
            if self.waited[eng].get(e2, 0) < v:
                op.waits.append(("__eng_" + e2, v))
                self.semnames.add("__eng_" + e2)
                self.waited[eng][e2] = v

    def _split_deps(self, deps):
        ddeps = {}
        rdeps = []
        for rec in deps:
            if rec.eng.startswith("dma:"):
                name = rec.eng[4:]
                if ddeps.get(name, 0) < rec.idx:
                    ddeps[name] = rec.idx
            else:
                rdeps.append(rec)
        return rdeps, ddeps

    def emit(self, eng, fn, reads=(), writes=()):
        deps, ddeps = self._split_deps(self._deps_for(reads, writes))
        op = Op(fn)
        self._apply_deps(eng, op, deps, ddeps)
        idx = self.icount[eng]
        self.ops[eng].append(op)
        self.icount[eng] += 1
        self._record(eng, idx, reads, writes)
        return idx

    def dma(self, eng, out_tv, in_tv, semname):
        prev_q = self.dmaq.setdefault(semname, eng)
        assert prev_q == eng, f"dma sem {semname} used from {prev_q} and {eng}"
        deps, ddeps = self._split_deps(self._deps_for([in_tv], [out_tv]))
        op = Op(lambda q, o=out_tv, i=in_tv: q.dma_start(out=o.ap(), in_=i.ap()))
        self._apply_deps(eng, op, deps, ddeps)
        self.dmacnt[semname] = self.dmacnt.get(semname, 0) + 1
        cnt = self.dmacnt[semname]
        op.incs.append((semname, 16))
        self.semnames.add(semname)
        self.ops[eng].append(op)
        self.icount[eng] += 1
        for tv, kind in ((in_tv, "r"), (out_tv, "w")):
            ent = self.board.setdefault(tv.key, {"w": [], "r": []})
            ent[kind].append(Rec("dma:" + semname, cnt, tv))
            if len(ent[kind]) > 96:
                del ent[kind][:-64]

    def global_barrier(self):
        for name in sorted(self.dmacnt):
            qeng = self.dmaq[name]
            self._join_dma(name, self.dmacnt[name])
        vals = {}
        for e in ENG:
            if self.icount[e]:
                vals[e] = self._resolve_inc(e, self.icount[e] - 1)
        for e in ENG:
            op = Op(lambda q: None)
            for e2, v in sorted(vals.items()):
                if e2 == e:
                    continue
                if self.waited[e].get(e2, 0) < v:
                    op.waits.append(("__eng_" + e2, v))
                    self.semnames.add("__eng_" + e2)
                    self.waited[e][e2] = v
            if op.waits:
                self.ops[e].append(op)
                self.icount[e] += 1

    def cc(self, kind, aluop, groups, in_tv, out_tv):
        name = "dcc"
        self.dmaq.setdefault(name, "pool")
        deps, ddeps = self._split_deps(self._deps_for([in_tv], [out_tv]))
        def fn(q, i=in_tv, o=out_tv):
            return q.collective_compute(kind, aluop, replica_groups=groups,
                                        ins=[i.ap().opt()], outs=[o.ap().opt()])
        op = Op(fn)
        self._apply_deps("pool", op, deps, ddeps)
        self.dmacnt[name] = self.dmacnt.get(name, 0) + 1
        cnt = self.dmacnt[name]
        op.incs.append((name, 1))
        self.semnames.add(name)
        self.ops["pool"].append(op)
        self.icount["pool"] += 1
        for tv, kind2 in ((in_tv, "r"), (out_tv, "w")):
            ent = self.board.setdefault(tv.key, {"w": [], "r": []})
            ent[kind2].append(Rec("dma:" + name, cnt, tv))

    def wait_all_dma(self, eng, semname):
        cnt = self.dmacnt.get(semname, 0)
        if not cnt:
            return
        qeng = self.dmaq[semname]
        inc = 1 if semname == "dcc" else 16
        if eng == qeng:
            op = Op(lambda q: None)
            op.waits.append((semname, inc * cnt))
            self.semnames.add(semname)
            self.ops[eng].append(op)
            self.icount[eng] += 1
        else:
            qeng2, jidx = self._join_dma(semname, cnt)
            v = self._resolve_inc(qeng2, jidx)
            if self.waited[eng].get(qeng2, 0) < v:
                op = Op(lambda q: None)
                op.waits.append(("__eng_" + qeng2, v))
                self.semnames.add("__eng_" + qeng2)
                self.waited[eng][qeng2] = v
                self.ops[eng].append(op)
                self.icount[eng] += 1

    def finalize(self, block, sems):
        def run(eng):
            def body(q):
                for op in self.ops[eng]:
                    for name, val in op.waits:
                        q.wait_ge(sems[name], val)
                    inst = op.fn(q)
                    for name, amt in op.incs:
                        if inst is None:
                            inst = q.nop()
                        inst.then_inc(sems[name], amt)
            return body
        block.tensor(run("pe"))
        block.scalar(run("act"))
        block.vector(run("dve"))
        block.gpsimd(run("pool"))
        block.sync(run("sp"))


class Ctx:
    def __init__(self, nc):
        self.nc = nc
        self.s = Sched(nc)
        self.consts_done = False
        self.stack = None
        self._dmarot = {}
        self._uid = 0

    def uid(self):
        self._uid += 1
        return self._uid

    def sb(self, name, shape, dt):
        if self.stack is not None:
            h = self.stack.enter_context(self.nc.sbuf_tensor(name, shape, dt))
        else:
            h = self.nc.alloc_sbuf_tensor(name, shape, dt)
        return Buf(h, name, shape)

    def phase_begin(self):
        import contextlib
        assert self.stack is None
        self.stack = contextlib.ExitStack()

    def phase_end(self):
        self.s.global_barrier()
        self.stack.close()
        self.stack = None

    def ps(self, name, shape, dt=F32):
        if self.stack is not None:
            h = self.stack.enter_context(self.nc.psum_tensor(name, shape, dt))
        else:
            h = self.nc.alloc_psum_tensor(name, shape, dt)
        return Buf(h, name, shape, whole=True)

    def ps_bf16_alias(self, buf):
        b2 = Buf(buf.h, buf.name, [buf.shape[0], buf.shape[1] * 2], whole=True)
        orig_getitem = Buf.__getitem__

        class _BV(Buf):
            def __getitem__(self, idx):
                tv = orig_getitem(self, idx)
                h = self.h
                idx2 = idx if isinstance(idx, tuple) else (idx,)
                tv.mk = lambda: (h.ap() if hasattr(h, "ap") else h).bitcast(BF16)[idx2]
                return tv
        b2.__class__ = _BV
        return b2

    def din(self, name, shape, dt):
        h = self.nc.dram_tensor(name, shape, dt, kind="ExternalInput")
        return Buf(h, name, shape)

    def dout(self, name, shape, dt):
        h = self.nc.dram_tensor(name, shape, dt, kind="ExternalOutput")
        return Buf(h, name, shape)

    def dint(self, name, shape, dt):
        h = self.nc.dram_tensor(name, shape, dt, kind="Internal")
        return Buf(h, name, shape)

    def mm(self, out, lhsT, rhs, start=True, stop=True):
        self.s.emit("pe", lambda q, o=out, l=lhsT, r=rhs, st=start, sp=stop:
                    q.matmul(o.ap(), l.ap(), r.ap(), start=st, stop=sp),
                    reads=[lhsT, rhs], writes=[out])

    def tr(self, out, in_, ident):
        self.s.emit("pe", lambda q, o=out, i=in_, d=ident:
                    q.transpose(o.ap(), i.ap(), d.ap()),
                    reads=[in_, ident], writes=[out])

    def act(self, eng_out, in_, func, bias=0.0, scale=1.0, accum=None):
        def fn(q, o=eng_out, i=in_, f=func, b=bias, sc=scale, ac=accum):
            kw = {}
            if ac is not None:
                kw["accum_out"] = ac.ap()
            bb = b.ap() if isinstance(b, TV) else b
            ss = sc.ap() if isinstance(sc, TV) else sc
            return q.activation(o.ap(), i.ap(), f, bias=bb, scale=ss, **kw)
        reads = [in_] + [x for x in (bias, scale) if isinstance(x, TV)]
        writes = [eng_out] + ([accum] if accum is not None else [])
        self.s.emit("act", fn, reads=reads, writes=writes)

    def _vec(self, eng, name, out, ins, args, kwargs=None):
        def fn(q, nm=name, o=out, ii=ins, aa=args, kk=kwargs):
            f = getattr(q, nm)
            aps = [x.ap() if isinstance(x, TV) else x for x in ii]
            return f(o.ap(), *aps, *aa, **(kk or {}))
        self.s.emit(eng, fn, reads=[x for x in ins if isinstance(x, TV)], writes=[out])

    def tt(self, out, a, b, op, eng="dve"):
        self._vec(eng, "tensor_tensor", out, [a, b], [op])

    def ts(self, out, a, s1, op0, s2=None, op1=None, eng="dve"):
        def fn(q, o=out, a_=a, s1_=s1, s2_=s2, op0_=op0, op1_=op1):
            s1v = s1_.ap() if isinstance(s1_, TV) else s1_
            s2v = s2_.ap() if isinstance(s2_, TV) else s2_
            if op1_ is None:
                return q.tensor_scalar(o.ap(), a_.ap(), s1v, None, op0_)
            return q.tensor_scalar(o.ap(), a_.ap(), s1v, s2v, op0_, op1_)
        reads = [a] + [x for x in (s1, s2) if isinstance(x, TV)]
        self.s.emit(eng, fn, reads=reads, writes=[out])

    def stt(self, out, in0, scalar, in1, op0, op1, eng="dve"):
        def fn(q, o=out, a_=in0, s_=scalar, b_=in1, o0=op0, o1=op1):
            sv = s_.ap() if isinstance(s_, TV) else s_
            return q.scalar_tensor_tensor(o.ap(), a_.ap(), sv, b_.ap(), o0, o1)
        reads = [in0, in1] + ([scalar] if isinstance(scalar, TV) else [])
        self.s.emit(eng, fn, reads=reads, writes=[out])

    def cp(self, out, in_, eng="dve"):
        self._vec(eng, "tensor_copy", out, [in_], [])

    def cpa(self, out, in_):
        def fn(q, o=out, i=in_):
            return q.copy(o.ap(), i.ap())
        self.s.emit("act", fn, reads=[in_], writes=[out])

    def red(self, out, in_, op, eng="dve"):
        self._vec(eng, "tensor_reduce", out, [in_], [AX.X, op])

    def recip(self, out, in_):
        self._vec("dve", "reciprocal", out, [in_], [])

    def scan(self, out, d0, d1, init, op0, op1, eng="dve"):
        def fn(q, o=out, a_=d0, b_=d1, i_=init, o0=op0, o1=op1):
            iv = i_.ap() if isinstance(i_, TV) else i_
            return q.tensor_tensor_scan(o.ap(), a_.ap(), b_.ap(), iv, o0, o1)
        reads = [d0, d1] + ([init] if isinstance(init, TV) else [])
        self.s.emit(eng, fn, reads=reads, writes=[out])

    def memset(self, out, val, eng="pool"):
        self._vec(eng, "memset", out, [], [val])

    def affsel(self, out, in_, pattern, cm, base, cmp_op, fill):
        def fn(q, o=out, i=in_, p=pattern, c=cm, b=base, co=cmp_op, f=fill):
            return q.affine_select(out=o.ap(), in_=i.ap(), compare_op=co, fill=f,
                                   base=b, pattern=p, channel_multiplier=c)
        self.s.emit("pool", fn, reads=[in_], writes=[out])

    def dma(self, out, in_, sem="d0", eng="sp", rot=4):
        n = self._dmarot.get(sem, 0)
        self._dmarot[sem] = n + 1
        self.s.dma(eng, out, in_, f"{sem}.{n % rot}")

    def build_consts(self):
        c = self
        if self.consts_done:
            return
        self.consts_done = True
        saved, self.stack = self.stack, None
        self._build_consts_inner()
        self.stack = saved

    def _build_consts_inner(self):
        c = self
        self.ident_bf = c.sb("c_ident_bf", [128, 128], BF16)
        self.ident_f = c.sb("c_ident_f", [128, 128], F32)
        self.tril_f = c.sb("c_tril_f", [128, 128], F32)
        self.ones1_bf = c.sb("c_ones1", [1, 128], BF16)
        self.onesk_bf = c.sb("c_onesk", [128, 1], BF16)
        self.ident2_bf = c.sb("c_id2", [128, 128], BF16)
        self.trilS_f = c.sb("c_trilS", [128, 128], F32)
        self.triuI_f = c.sb("c_triuI", [128, 128], F32)
        self.ones1f = c.sb("c_ones1f", [1, 128], F32)
        ms = c.memset
        af = c.affsel
        ms(self.ident_bf.full(), 0.0)
        ms(self.ident_f.full(), 0.0)
        ms(self.tril_f.full(), 1.0)
        ms(self.ones1_bf.full(), 1.0)
        ms(self.onesk_bf.full(), 1.0)
        ms(self.ident2_bf.full(), 0.0)
        ms(self.trilS_f.full(), 1.0)
        ms(self.triuI_f.full(), 1.0)
        ms(self.ones1f.full(), 1.0)
        af(self.ident_bf.full(), self.ident_bf.full(), [[-1, 128]], 1, 0, A.not_equal, 1.0)
        af(self.ident_f.full(), self.ident_f.full(), [[-1, 128]], 1, 0, A.not_equal, 1.0)
        af(self.ident2_bf.full(), self.ident2_bf.full(), [[-1, 128]], 1, 0, A.not_equal, 2.0)
        af(self.tril_f.full(), self.tril_f.full(), [[1, 128]], -1, 0, A.is_ge, 0.0)
        af(self.trilS_f.full(), self.trilS_f.full(), [[-1, 128]], 1, -1, A.is_ge, 0.0)
        af(self.triuI_f.full(), self.triuI_f.full(), [[1, 128]], -1, 0, A.is_ge, 0.0)


def emit_l0attn(c, TS, NSC, io):
    c.build_consts()
    s = c.s
    NQ, NK, NVt = 8, 8, 16
    NMIX = NQ + NK + NVt
    NZ = 16
    NCHS = TS // CH

    xin = c.sb("a_xin", [128, 2, TS], F32)
    xbf = c.sb("a_xbf", [128, 16, TS], BF16)
    xsq = c.sb("a_xsq", [128, 2, TS], BF16)
    rowp = c.sb("a_rowp", [1, 8 * TS], F32)
    wstg = c.sb("a_wstg", [128, 2, 2048], F32)
    wbf = c.sb("a_wbf", [128, 2, 2048], BF16)
    mixr = c.sb("a_mixr", [128, NMIX, CONVK - 1 + TS], BF16)
    tails = c.sb("a_tails", [128, NMIX, CONVK - 1], BF16)
    zbuf = c.sb("a_zbuf", [128, NZ, TS], BF16)
    convt = c.sb("a_convt", [128, TS], F32)
    batok = c.sb("a_batok", [128, NCHS, 32], F32)
    gtok = c.sb("a_gtok", [128, NCHS, 16], F32)
    btok = c.sb("a_btok", [128, NCHS, 16], F32)
    sptmp = c.sb("a_sptmp", [128, 3, 16], F32)
    Lall = c.sb("a_Lall", [128, 16], F32)
    bLall = c.sb("a_bLall", [128, 16], F32)
    bcola = c.sb("a_bcola", [128, 16], F32)
    Sst = [c.sb("a_S0", [128, 16, 128], F32), c.sb("a_S1", [128, 16, 128], F32)]
    def pair(nm, shape, dt):
        return [c.sb(nm + "0", shape, dt), c.sb(nm + "1", shape, dt)]
    Sbf_p = pair("a_Sbf", [128, 128], BF16)
    qsq_p = pair("a_qsq", [128, 128], BF16)
    ksq_p = pair("a_ksq", [128, 128], BF16)
    neggr_p = pair("a_neggr", [128, 128], F32)
    Dsb_p = pair("a_Dsb", [128, 128], F32)
    DTsb_p = pair("a_DTsb", [128, 128], F32)
    Asb_p = pair("a_Asb", [128, 128], F32)
    WTsb_p = pair("a_WTsb", [128, 128], BF16)
    Xsb_p = pair("a_Xsb", [128, 128], BF16)
    XTsb_p = pair("a_XTsb", [128, 128], BF16)
    MTsb_p = pair("a_MTsb", [128, 128], BF16)
    Gsb_p = pair("a_Gsb", [128, 128], BF16)
    vtok_p = pair("a_vtok", [128, 128], BF16)
    ktok_p = pair("a_ktok", [128, 128], BF16)
    kusb_p = pair("a_kusb", [128, 128], BF16)
    rsb_p = pair("a_rsb", [128, 128], BF16)
    r1sb_p = pair("a_r1sb", [128, 128], F32)
    dsb_p = pair("a_dsb", [128, 128], BF16)
    KKs_p = pair("a_KKs", [128, 128], F32)
    KQs_p = pair("a_KQs", [128, 128], F32)
    wtmp_p = pair("a_wtmp", [128, 128], F32)
    rnbcs_p = pair("a_rnbcs", [128, 128], F32)
    rqbc_p = pair("a_rqbc", [128, 128], F32)
    ofsb_p = pair("a_ofsb", [128, 128], F32)
    oftmp_p = pair("a_oftmp", [128, 128], F32)
    osq_p = pair("a_osq", [128, 128], BF16)
    colp_p = pair("a_colp", [128, 64], F32)
    rowsc_p = pair("a_rowsc", [1, 6 * 128], F32)
    ogbuf = c.sb("a_ogbuf", [128, 16, TS], BF16)
    lnr = c.sb("a_lnr", [128, 2, TS], F32)
    outst = c.sb("a_outst", [128, 2, TS], BF16)
    wbasb = c.sb("a_wba", [128, 16, 32], BF16)
    wbastg = c.sb("a_wbastg", [128, 16, 32], F32)
    cwsb = c.sb("a_cw", [128, NMIX, CONVK], F32)
    nwcol = c.sb("a_nwcol", [128, 1], F32)
    dtbc = c.sb("a_dtbc", [128, 16], F32)
    negea = c.sb("a_negea", [128, 16], F32)
    ones1f = c.ones1f

    p0 = c.ps("a_p0", [128, 512])
    p1 = c.ps("a_p1", [128, 512])
    p2 = c.ps("a_p2", [128, 512])
    p3 = c.ps("a_p3", [128, 512])
    p4 = c.ps("a_p4", [128, 512])
    p5 = c.ps("a_p5", [128, 512])
    p6 = c.ps("a_p6", [128, 512])
    p7b = c.ps("a_p7b", [128, 1024], BF16)
    pt_p = [p7b, c.ps_bf16_alias(p6)]
    pA_p = [p0, p1]
    pB_p = [p2, p3]
    pC_p = [p4, p5]

    c.build_consts()
    c.memset(tails.full(), 0.0)
    for par in range(2):
        c.memset(Sst[par].full(), 0.0)
    c.dma(wbastg.full(), io["wba"].full(), "dw")
    c.cp(wbasb.full(), wbastg.full())
    c.dma(cwsb.full(), io["convw"].full(), "dw")
    c.dma(nwcol.full(), io["nw"].full(), "dw")
    alrow = c.sb("a_alrow", [1, 16], F32)
    dtrow = c.sb("a_dtrow", [1, 16], F32)
    onesb1 = c.sb("a_onesb1", [1, 128], BF16)
    c.memset(onesb1.full(), 1.0)
    c.dma(alrow.full(), io["alog"].full(), "dw")
    c.dma(dtrow.full(), io["dtb"].full(), "dw")
    c.mm(p6[:, 0:16], ones1f[0:1, :], alrow[0:1, :])
    c.act(negea.full(), p6[:, 0:16], AF.Exp)
    c.ts(negea.full(), negea.full(), -1.0, A.mult)
    c.mm(p6[:, 16:32], ones1f[0:1, :], dtrow[0:1, :])
    c.cp(dtbc.full(), p6[:, 16:32])

    for sc in range(NSC):
        t0 = sc * TS
        for kt in range(16):
            c.dma(xin[:, kt % 2, :], io["xfeat"][kt * 128:(kt + 1) * 128, t0:t0 + TS], "dx")
            c.cp(xbf[:, kt, :], xin[:, kt % 2, :])
        for kt in range(16):
            c.mm(p5[0:1, 0:TS], c.onesk_bf[:, 0:1], xbf[:, kt, :],
                 start=(kt == 0), stop=(kt == 15))
        c.ts(rowp.tv_sub(0, TS), p5[0:1, 0:TS], 1.0 / D, A.mult)
        for kt in range(16):
            c.act(xsq[:, kt % 2, :], xbf[:, kt, :], AF.Square)
            c.mm(p6[0:1, 0:TS], c.onesk_bf[:, 0:1], xsq[:, kt % 2, :],
                 start=(kt == 0), stop=(kt == 15))
        m_ = rowp.tv_sub(0, TS)
        sq_ = rowp.tv_sub(1, TS)
        var_ = rowp.tv_sub(2, TS)
        rst_ = rowp.tv_sub(3, TS)
        mr_ = rowp.tv_sub(4, TS)
        c.ts(sq_, p6[0:1, 0:TS], 1.0 / D, A.mult)
        c.tt(var_, m_, m_, A.mult)
        c.tt(var_, sq_, var_, A.subtract)
        c.ts(var_, var_, EPS_LN, A.add)
        c.act(rst_, var_, AF.Ln)
        c.act(rst_, rst_, AF.Exp, scale=-0.5)
        c.tt(mr_, m_, rst_, A.mult)
        c.mm(p5[:, 0:TS], ones1f[0:1, :], rst_)
        c.cp(lnr[:, 0, :], p5[:, 0:TS])
        c.mm(p6[:, 0:TS], ones1f[0:1, :], mr_)
        c.cp(lnr[:, 1, :], p6[:, 0:TS])
        for kt in range(16):
            c.tt(xbf[:, kt, :], xbf[:, kt, :], lnr[:, 0, :], A.mult)
            c.tt(xbf[:, kt, :], xbf[:, kt, :], lnr[:, 1, :], A.subtract)

        def proj(wdram, ot, dest_tv, pb, pbi):
            c.dma(wstg[:, pbi, :], wdram[ot], "dw")
            c.cp(wbf[:, pbi, :], wstg[:, pbi, :])
            for kc in range(16):
                c.mm(pb[:, 0:TS], wbf[:, pbi, kc * 128:(kc + 1) * 128], xbf[:, kc, :],
                     start=(kc == 0), stop=(kc == 15))
            c.cp(dest_tv, pb[:, 0:TS])

        for ot in range(NMIX):
            proj(io["wqkv"], ot, mixr[:, ot, CONVK - 1:], [p0, p1][ot % 2], ot % 2)
        for ot in range(NZ):
            proj(io["wz"], ot, zbuf[:, ot, :], [p0, p1][ot % 2], ot % 2)
            c.act(zbuf[:, ot, :], zbuf[:, ot, :], AF.Silu)

        for cc in range(NCHS):
            for kc in range(16):
                c.mm(p2[:, 0:32], xbf[:, kc, cc * CH:(cc + 1) * CH], wbasb[:, kc, :],
                     start=(kc == 0), stop=(kc == 15))
            c.cp(batok[:, cc, :], p2[:, 0:32])
            c.act(btok[:, cc, :], batok[:, cc, 0:16], AF.Sigmoid)
            y_ = sptmp[:, 0, :]
            c.tt(y_, batok[:, cc, 16:32], dtbc.full(), A.add)
            ax = sptmp[:, 1, :]
            c.act(ax, y_, AF.Abs)
            c.act(ax, ax, AF.Exp, scale=-1.0)
            c.act(ax, ax, AF.Ln, bias=1.0)
            mx = sptmp[:, 2, :]
            c.ts(mx, y_, 0.0, A.max)
            c.tt(mx, mx, ax, A.add)
            c.tt(gtok[:, cc, :], mx, negea.full(), A.mult)

        for ot in range(NMIX):
            c.cp(mixr[:, ot, 0:CONVK - 1], tails[:, ot, :])
            c.cp(tails[:, ot, :], mixr[:, ot, TS:TS + CONVK - 1])
            c.ts(convt.full(), mixr[:, ot, 0:TS], cwsb[:, ot, 0:1], A.mult)
            for j in range(1, CONVK):
                c.stt(convt.full(), mixr[:, ot, j:j + TS], cwsb[:, ot, j:j + 1],
                      convt.full(), A.mult, A.add)
            c.act(mixr[:, ot, CONVK - 1:], convt.full(), AF.Silu)

        for cc in range(NCHS):
            gch = (sc * NCHS + cc)
            Sprev = Sst[gch % 2]
            Snew = Sst[(gch + 1) % 2]
            c.mm(p2[:, 0:16], c.tril_f[:, :], gtok[:, cc, :])
            c.cp(bcola.full(), p2[:, 0:16])
            c.act(Lall.full(), bcola.full(), AF.Exp)
            c.tt(bLall.full(), btok[:, cc, :], Lall.full(), A.mult)
            def kh_group(kh, cc=cc, Sprev=Sprev, Snew=Snew):
                P = kh % 2
                pA, pB, pC, pT = pA_p[P], pB_p[P], pC_p[P], pt_p[P]
                qsq, ksq, KKs, KQs, rnbcs, ktok = (qsq_p[P], ksq_p[P], KKs_p[P],
                                                   KQs_p[P], rnbcs_p[P], ktok_p[P])
                Sbf, neggr, Dsb, DTsb, Asb, WTsb = (Sbf_p[P], neggr_p[P], Dsb_p[P],
                                                    DTsb_p[P], Asb_p[P], WTsb_p[P])
                Xsb, XTsb, MTsb, Gsb, vtok, kusb = (Xsb_p[P], XTsb_p[P], MTsb_p[P],
                                                    Gsb_p[P], vtok_p[P], kusb_p[P])
                rsb, r1sb, dsb, wtmp, rqbc = (rsb_p[P], r1sb_p[P], dsb_p[P],
                                              wtmp_p[P], rqbc_p[P])
                ofsb, oftmp, osq, colp, rowsc = (ofsb_p[P], oftmp_p[P], osq_p[P],
                                                 colp_p[P], rowsc_p[P])
                qf = mixr[:, kh, CONVK - 1 + cc * CH: CONVK - 1 + (cc + 1) * CH]
                kf = mixr[:, NQ + kh, CONVK - 1 + cc * CH: CONVK - 1 + (cc + 1) * CH]
                c.tt(qsq.full(), qf, qf, A.mult)
                yield
                c.tt(ksq.full(), kf, kf, A.mult)
                yield
                c.mm(pA[:, 0:128], kf, kf)
                c.mm(pB[:, 0:128], kf, qf)
                c.mm(pC[0:1, 0:128], c.onesk_bf[:, 0:1], ksq.full())
                yield
                c.mm(pC[0:1, 128:256], c.onesk_bf[:, 0:1], qsq.full())
                yield
                c.mm(pC[:, 300:301], ksq.full(), c.onesk_bf[:, 0:1])
                yield
                nr = rowsc.tv_sub(0, 128)
                mrq = rowsc.tv_sub(1, 128)
                c.ts(nr, pC[0:1, 0:128], EPS_L2, A.add)
                yield
                c.act(nr, nr, AF.Ln)
                yield
                c.act(nr, nr, AF.Exp, scale=-0.5)
                yield
                c.ts(mrq, pC[0:1, 128:256], EPS_L2, A.add)
                yield
                c.act(mrq, mrq, AF.Ln)
                yield
                c.act(mrq, mrq, AF.Exp, scale=-0.5)
                yield
                rnc = colp[:, 2:3]
                c.ts(rnc, pC[:, 300:301], EPS_L2, A.add)
                yield
                c.act(rnc, rnc, AF.Ln)
                yield
                c.act(rnc, rnc, AF.Exp, scale=-0.5)
                yield
                c.mm(pC[:, 0:128], ones1f[0:1, :], nr)
                yield
                c.cp(rnbcs.full(), pC[:, 0:128])
                yield
                c.cp(KKs.full(), pA[:, 0:128])
                yield
                c.cp(KQs.full(), pB[:, 0:128])
                yield
                c.tr(pT[:, 0:128], kf, c.ident_bf[:, :])
                yield
                c.cpa(ktok.full(), pT[:, 0:128])
                yield
                for vi in range(2):
                    vh = 2 * kh + vi
                    vf = mixr[:, NQ + NK + vh, CONVK - 1 + cc * CH: CONVK - 1 + (cc + 1) * CH]
                    gcol = gtok[:, cc, vh:vh + 1]
                    bcol = bcola[:, vh:vh + 1]
                    betac = btok[:, cc, vh:vh + 1]
                    c.tr(pT[:, 128:256], vf, c.ident_bf[:, :])
                    yield
                    c.cpa(vtok.full(), pT[:, 128:256])
                    yield
                    c.ts(neggr.full(), gtok.bcf((slice(None), cc, slice(vh, vh + 1)), 128), -1.0, A.mult)
                    yield
                    c.mm(pB[:, 0:128], neggr.full(), c.tril_f[:, :])
                    yield
                    c.ts(Dsb.full(), pB[:, 0:128], bcol, A.add, 0.0, A.min)
                    yield
                    c.act(Dsb.full(), Dsb.full(), AF.Exp)
                    yield
                    c.tr(pB[:, 0:128], Dsb.full(), c.ident_f[:, :])
                    yield
                    c.cp(DTsb.full(), pB[:, 0:128])
                    yield
                    c.tt(Asb.full(), Dsb.full(), rnbcs.full(), A.mult)
                    yield
                    c.tt(Asb.full(), Asb.full(), KKs.full(), A.mult)
                    yield
                    brn = colp[:, 4:5]
                    c.tt(brn, betac, rnc, A.mult)
                    yield
                    c.stt(Asb.full(), Asb.full(), brn, c.trilS_f[:, :], A.mult, A.mult)
                    yield
                    rmr = rowsc.tv_sub(2, 128)
                    c.ts(rmr, mrq, DKSC, A.mult)
                    yield
                    c.mm(pC[:, 0:128], ones1f[0:1, :], rmr)
                    c.tt(wtmp.full(), DTsb.full(), pC[:, 0:128], A.mult)
                    yield
                    c.tt(wtmp.full(), wtmp.full(), KQs.full(), A.mult)
                    yield
                    c.stt(WTsb.full(), wtmp.full(), rnc, c.triuI_f[:, :], A.mult, A.mult)
                    yield
                    c.tr(pB[:, 0:128], Asb.full(), c.ident_f[:, :])
                    yield
                    c.stt(Xsb.full(), Asb.full(), -1.0, c.ident_f[:, :], A.mult, A.add)
                    yield
                    c.stt(XTsb.full(), pB[:, 0:128], -1.0, c.ident_f[:, :], A.mult, A.add)
                    yield
                    c.stt(MTsb.full(), pB[:, 0:128], 1.0, c.ident_f[:, :], A.mult, A.add)
                    yield
                    for it in range(7):
                        c.mm(pA[:, 0:128], MTsb.full(), Xsb.full())
                        yield
                        c.stt(Gsb.full(), pA[:, 0:128], -1.0, c.ident2_bf[:, :], A.mult, A.add)
                        yield
                        c.mm(pB[:, 0:128], XTsb.full(), Gsb.full())
                        yield
                        c.mm(pA[:, 128:256], Gsb.full(), XTsb.full())
                        yield
                        c.cpa(Xsb.full(), pB[:, 0:128])
                        c.cp(XTsb.full(), pA[:, 128:256])
                        yield
                    c.cpa(Sbf.full(), Sprev[:, vh, :])
                    yield
                    c.mm(pA[:, 0:128], kf, Sbf.full())
                    c.ts(r1sb.full(), vtok.full(), betac, A.mult)
                    yield
                    bLrn = colp[:, 5:6]
                    c.stt(bLrn, bLall[:, vh:vh + 1], -1.0, rnc, A.mult, A.mult)
                    yield
                    c.stt(rsb.full(), pA[:, 0:128], bLrn, r1sb.full(), A.mult, A.add)
                    yield
                    c.mm(pB[:, 0:128], XTsb.full(), rsb.full())
                    yield
                    c.cpa(dsb.full(), pB[:, 0:128])
                    yield
                    c.mm(pB[:, 128:256], dsb.full(), WTsb.full())
                    c.mm(pA[:, 128:256], Sbf.full(), qf)
                    c.mm(pC[0:1, 0:128], gcol, c.tril_f[:, :])
                    yield
                    brow = rowsc.tv_sub(3, 128)
                    c.act(brow, pC[0:1, 0:128], AF.Exp)
                    rqr = rowsc.tv_sub(4, 128)
                    c.stt(rqr, brow, DKSC, mrq, A.mult, A.mult)
                    yield
                    c.mm(pC[:, 0:128], ones1f[0:1, :], rqr)
                    yield
                    c.cp(rqbc.full(), pC[:, 0:128])
                    yield
                    c.tt(oftmp.full(), pA[:, 128:256], rqbc.full(), A.mult)
                    yield
                    c.tt(ofsb.full(), oftmp.full(), pB[:, 128:256], A.add)
                    yield
                    c.tt(osq.full(), ofsb.full(), ofsb.full(), A.mult)
                    yield
                    c.mm(pC[0:1, 128:256], c.onesk_bf[:, 0:1], osq.full())
                    yield
                    vr = rowsc.tv_sub(5, 128)
                    c.ts(vr, pC[0:1, 128:256], 1.0 / DV, A.mult, EPS_CORE, A.add)
                    yield
                    c.act(vr, vr, AF.Ln)
                    yield
                    c.act(vr, vr, AF.Exp, scale=-0.5)
                    yield
                    c.mm(pC[:, 0:128], ones1f[0:1, :], vr)
                    yield
                    c.tt(oftmp.full(), ofsb.full(), pC[:, 0:128], A.mult)
                    yield
                    c.stt(ogbuf[:, vh, cc * CH:(cc + 1) * CH], oftmp.full(), nwcol[:, 0:1],
                          zbuf[:, vh, cc * CH:(cc + 1) * CH], A.mult, A.mult)
                    ucol = colp[:, 6:7]
                    c.tt(ucol, DTsb[:, 127:128], rnc, A.mult)
                    yield
                    c.ts(kusb.full(), ktok.full(), ucol, A.mult)
                    yield
                    c.mm(pA[:, 256:384], kusb.full(), dsb.full())
                    lcc = colp[:, 7:8]
                    c.tt(lcc, DTsb[:, 127:128], Lall[:, vh:vh + 1], A.mult)
                    yield
                    c.stt(Snew[:, vh, :], Sprev[:, vh, :], lcc, pA[:, 256:384], A.mult, A.add)
                    yield


            def drive(gens):
                gens = [g for g in gens]
                while gens:
                    nxt = []
                    for g in gens:
                        try:
                            next(g)
                            nxt.append(g)
                        except StopIteration:
                            pass
                    gens = nxt

            for kh0 in range(0, HKh, 2):
                drive([kh_group(kh0), kh_group(kh0 + 1)])

        for dt_ in range(16):
            c.dma(wstg[:, dt_ % 2, :], io["wout"][dt_], "dw")
            c.cp(wbf[:, dt_ % 2, :], wstg[:, dt_ % 2, :])
            for vh in range(16):
                c.mm(p0[:, 0:TS], wbf[:, dt_ % 2, vh * 128:(vh + 1) * 128],
                     ogbuf[:, vh, :], start=(vh == 0), stop=(vh == 15))
            c.cp(outst[:, dt_ % 2, :], p0[:, 0:TS])
            c.dma(io["part"][dt_ * 128:(dt_ + 1) * 128, t0:t0 + TS],
                  outst[:, dt_ % 2, :], "do", eng="pool")


def finalize_core(c):
    nc = c.nc
    s = c.s
    for name in sorted(s.dmacnt):
        s.wait_all_dma("pool", name)
        s.wait_all_dma("sp", name)
    sems = {}
    for i, name in enumerate(sorted(s.semnames)):
        sems[name] = nc.alloc_semaphore(f"s{i}")
    with nc.Block() as block:
        s.finalize(block, sems)
    return nc


def _tile_lhsT(W, nk, no):
    K, M = W.shape
    assert K == nk * 128 and M == no * 128
    return np.ascontiguousarray(
        W.reshape(nk, 128, no, 128).transpose(2, 1, 0, 3))


def prep_l0attn(core, inp):
    b, sd = core // 2, core % 2
    x = np.asarray(inp["L0_attn_hidden_states"][b])
    xfeat = np.ascontiguousarray(x.T)
    wqkv = np.asarray(inp["L0_attn_w_qkv"])
    qs = wqkv[:, sd * QC:(sd + 1) * QC]
    ks = wqkv[:, 2048 + sd * KC:2048 + (sd + 1) * KC]
    vs = wqkv[:, 4096 + sd * VC:4096 + (sd + 1) * VC]
    wq = np.concatenate([qs, ks, vs], axis=1)
    conv = np.asarray(inp["L0_attn_conv_w"])
    cq = conv[sd * QC:(sd + 1) * QC]
    ck = conv[2048 + sd * KC:2048 + (sd + 1) * KC]
    cv = conv[4096 + sd * VC:4096 + (sd + 1) * VC]
    cw = np.concatenate([cq, ck, cv], axis=0)
    wz = np.asarray(inp["L0_attn_w_z"])[:, sd * 2048:(sd + 1) * 2048]
    wb = np.asarray(inp["L0_attn_w_b"])[:, sd * 16:(sd + 1) * 16]
    wa = np.asarray(inp["L0_attn_w_a"])[:, sd * 16:(sd + 1) * 16]
    wba = np.concatenate([wb, wa], axis=1)
    wout = np.asarray(inp["L0_attn_w_out"])[sd * 2048:(sd + 1) * 2048]
    alog = np.asarray(inp["L0_attn_A_log"])[sd * 16:(sd + 1) * 16]
    dtb = np.asarray(inp["L0_attn_dt_bias"])[sd * 16:(sd + 1) * 16]
    nw = np.asarray(inp["L0_attn_norm_weight"])
    return {
        "xfeat": np.ascontiguousarray(xfeat, np.float32),
        "wqkv": _tile_lhsT(wq, 16, 32),
        "wz": _tile_lhsT(wz, 16, 16),
        "wba": np.ascontiguousarray(wba.reshape(16, 128, 32).transpose(1, 0, 2), np.float32),
        "convw": np.ascontiguousarray(cw.reshape(32, 128, 4).transpose(1, 0, 2), np.float32),
        "alog": np.ascontiguousarray(alog.reshape(1, 16), np.float32),
        "dtb": np.ascontiguousarray(dtb.reshape(1, 16), np.float32),
        "nw": np.ascontiguousarray(nw.reshape(128, 1), np.float32),
        "wout": _tile_lhsT(wout, 16, 16),
    }


def build_l0attn(T_used=T):
    nc = bass.Bass("TRN2", target_bir_lowering=False)
    c = Ctx(nc)
    TS_ = 512
    nsc = T_used // TS_
    io = {
        "xfeat": c.din("xfeat", [D, T_used], F32),
        "wqkv": c.din("wqkv", [32, 128, 16, 128], F32),
        "wz": c.din("wz", [16, 128, 16, 128], F32),
        "wba": c.din("wba", [128, 16, 32], F32),
        "convw": c.din("convw", [128, 32, 4], F32),
        "alog": c.din("alog", [1, 16], F32),
        "dtb": c.din("dtb", [1, 16], F32),
        "nw": c.din("nw", [128, 1], F32),
        "wout": c.din("wout", [16, 128, 16, 128], F32),
        "part": c.dout("part", [D, T_used], BF16),
    }
    emit_l0attn(c, TS_, nsc, io)
    finalize_core(c)
    return nc


def emit_decode_nvfp4(c, io, key, nrow_tiles, nkcode, scales_blocks, dstfn, pbank, bufs):
    bts, nib, ff, uu, vv, r4, dec, scl, dtr = bufs
    NBC = 512
    npass = (nkcode // 2) // NBC
    for rt in range(nrow_tiles):
        c.dma(scl.full()[0:1] if False else scl[:, 0:scales_blocks], io[key + "_s"][rt], "dw")
        for ps_ in range(npass):
            c.dma(bts[:, 0:NBC], io[key].tv_custom(
                lambda ap, rt=rt, ps_=ps_: ap[rt, :, ps_ * NBC:(ps_ + 1) * NBC],
                rt, rt, ps_ * NBC, (ps_ + 1) * NBC - 1), "dw")
            c.ts(nib[:, 0, 0:NBC], bts[:, 0:NBC], 15, A.bitwise_and)
            c.ts(nib[:, 1, 0:NBC], bts[:, 0:NBC], 4, A.logical_shift_right)
            nibf = nib.tv_custom(lambda ap: ap[:, :, 0:NBC].rearrange("p a b -> p (a b)"),
                                 0, 127, 0, 2 * NBC - 1)
            c.cp(ff[:, 0:2 * NBC], nibf)
            c.ts(uu[:, 0:2 * NBC], ff[:, 0:2 * NBC], 8.0, A.is_ge)
            c.stt(vv[:, 0:2 * NBC], uu[:, 0:2 * NBC], -8.0, ff[:, 0:2 * NBC], A.mult, A.add)
            c.ts(r4[:, 0:2 * NBC], vv[:, 0:2 * NBC], 4.0, A.subtract, 0.0, A.max)
            c.ts(r4[:, 0:2 * NBC], r4[:, 0:2 * NBC], 0.5, A.mult)
            c.stt(r4[:, 0:2 * NBC], vv[:, 0:2 * NBC], 0.5, r4[:, 0:2 * NBC], A.mult, A.add)
            c.ts(vv[:, 0:2 * NBC], vv[:, 0:2 * NBC], 6.0, A.subtract, 0.0, A.max)
            c.tt(r4[:, 0:2 * NBC], r4[:, 0:2 * NBC], vv[:, 0:2 * NBC], A.add)
            c.ts(uu[:, 0:2 * NBC], uu[:, 0:2 * NBC], -2.0, A.mult, 1.0, A.add)
            c.tt(r4[:, 0:2 * NBC], r4[:, 0:2 * NBC], uu[:, 0:2 * NBC], A.mult)
            deck = dec.tv_custom(
                lambda ap: ap[:, 0:2 * NBC].rearrange("p (b a) -> p a b", a=2),
                0, 127, 0, 2 * NBC - 1)
            r43 = r4.tv_custom(lambda ap: ap[:, 0:2 * NBC].rearrange("p (a b) -> p a b", a=2),
                               0, 127, 0, 2 * NBC - 1)
            uu3 = uu.tv_custom(lambda ap: ap[:, 0:2 * NBC].rearrange("p (a b) -> p a b", a=2),
                               0, 127, 0, 2 * NBC - 1)
            c.cp(deck, r43)
            nbl = 2 * NBC // 32
            dec3 = dec.tv_custom(
                lambda ap: ap[:, 0:2 * NBC].rearrange("p (j i) -> p j i", j=nbl),
                0, 127, 0, 2 * NBC - 1)
            scl3 = scl.tv_custom(
                lambda ap, ps_=ps_: ap[:, ps_ * nbl:(ps_ + 1) * nbl].unsqueeze(2).broadcast_to([128, nbl, 32]),
                0, 127, 0, scales_blocks - 1)
            c.tt(dec3, dec3, scl3, A.mult)
            for j in range(2 * NBC // 128):
                kc = ps_ * (2 * NBC // 128) + j
                c.tr(pbank[:, 0:128], dec[:, j * 128:(j + 1) * 128], c.ident_bf[:, :])
                c.cp(dtr.full(), pbank[:, 0:128])
                c.dma(dstfn(rt, kc), dtr.full(), "ddec", eng="pool")


def emit_l0mid(c, TS, NSC, io):
    NMT = 64
    c.build_consts()
    ones1f = c.ones1f
    wg = c.dint("b_wg", [NMT, 128, 16, 128], BF16)
    wu = c.dint("b_wu", [NMT, 128, 16, 128], BF16)
    wdt = c.dint("b_wdt", [2, 64, 128, 8, 128], BF16)
    p0 = c.ps("b_p0", [128, 512])
    p1 = c.ps("b_p1", [128, 512])
    p5 = c.ps("b_p5", [128, 512])
    p6 = c.ps("b_p6", [128, 512])
    p2 = c.ps("b_p2", [128, 512])
    p3 = c.ps("b_p3", [128, 512])
    p4 = c.ps("b_p4", [128, 512])
    p7 = c.ps("b_p7", [128, 512])
    pt = c.ps_bf16_alias(p7)
    dbufs = (
        c.sb("de_bts", [128, 512], U8),
        c.sb("de_nib", [128, 2, 512], U8),
        c.sb("de_ff", [128, 1024], F32),
        c.sb("de_uu", [128, 1024], F32),
        c.sb("de_vv", [128, 1024], F32),
        c.sb("de_r4", [128, 1024], F32),
        c.sb("de_dec", [128, 1024], BF16),
        c.sb("de_scl", [128, 256], F32),
        c.sb("de_dtr", [128, 128], BF16),
    )
    def dst_gu(w):
        def f(rt, kc):
            return w.tv_custom(lambda ap, rt=rt, kc=kc: ap[rt, :, kc, :],
                               rt, rt, kc * 128, (kc + 1) * 128 * 16 - 1)
        return f
    def dst_dn(rt, kc):
        return wdt.tv_custom(lambda ap, rt=rt, kc=kc: ap[rt // 8, kc, :, rt % 8, :],
                             0, 1, 0, 1 << 40)
    emit_decode_nvfp4(c, io, "gatep", NMT, 2048, 64, dst_gu(wg), pt, dbufs)
    emit_decode_nvfp4(c, io, "upp", NMT, 2048, 64, dst_gu(wu), pt, dbufs)
    emit_decode_nvfp4(c, io, "downp", 16, 8192, 256, dst_dn, pt, dbufs)

    at = c.sb("b_at", [128, 16, TS], BF16)
    xsq = c.sb("b_xsq", [128, 2, TS], BF16)
    xin = c.sb("b_xin", [128, 2, TS], F32)
    x1 = c.sb("b_x1", [128, 16, TS], F32)
    x1b = c.sb("b_x1b", [128, 16, TS], BF16)
    rowp = c.sb("b_rowp", [1, 8 * TS], F32)
    lnr = c.sb("b_lnr", [128, 2, TS], F32)
    wstg = c.sb("b_wstg", [128, 4, 2048], BF16)
    gt = c.sb("b_gt", [128, TS], F32)
    gt2 = c.sb("b_gt2", [128, TS], F32)
    outst = c.sb("b_outst", [128, 2, TS], BF16)
    intr = c.sb("b_intr", [128, 2, TS], BF16)
    iscr = c.dint("b_iscr", [NMT, 128, TS], BF16)

    def ln_rows(src_tiles, n, pb1, pb2):
        for kt in range(n):
            c.mm(pb1[0:1, 0:TS], c.onesk_bf[:, 0:1], src_tiles(kt),
                 start=(kt == 0), stop=(kt == n - 1))
        m_ = rowp.tv_sub(0, TS)
        c.ts(m_, pb1[0:1, 0:TS], 1.0 / D, A.mult)
        for kt in range(n):
            c.act(xsq[:, kt % 2, :], src_tiles(kt), AF.Square)
            c.mm(pb2[0:1, 0:TS], c.onesk_bf[:, 0:1], xsq[:, kt % 2, :],
                 start=(kt == 0), stop=(kt == n - 1))
        sq_ = rowp.tv_sub(1, TS)
        var_ = rowp.tv_sub(2, TS)
        rst_ = rowp.tv_sub(3, TS)
        mr_ = rowp.tv_sub(4, TS)
        c.ts(sq_, pb2[0:1, 0:TS], 1.0 / D, A.mult)
        c.tt(var_, m_, m_, A.mult)
        c.tt(var_, sq_, var_, A.subtract)
        c.ts(var_, var_, EPS_LN, A.add)
        c.act(rst_, var_, AF.Ln)
        c.act(rst_, rst_, AF.Exp, scale=-0.5)
        c.tt(mr_, m_, rst_, A.mult)
        c.mm(pb1[:, 0:TS], ones1f[0:1, :], rst_)
        c.cp(lnr[:, 0, :], pb1[:, 0:TS])
        c.mm(pb2[:, 0:TS], ones1f[0:1, :], mr_)
        c.cp(lnr[:, 1, :], pb2[:, 0:TS])

    for sc in range(NSC):
        t0 = sc * TS
        for kt in range(16):
            c.dma(at[:, kt, :], io["pa"][kt * 128:(kt + 1) * 128, t0:t0 + TS], "dx")
            if "pb" in io:
                c.dma(x1b[:, kt, :], io["pb"][kt * 128:(kt + 1) * 128, t0:t0 + TS], "dx")
                c.tt(at[:, kt, :], at[:, kt, :], x1b[:, kt, :], A.add)
        ln_rows(lambda kt: at[:, kt, :], 16, p5, p6)
        for kt in range(16):
            c.dma(xin[:, kt % 2, :], io["xfeat"][kt * 128:(kt + 1) * 128, t0:t0 + TS], "dx")
            c.tt(gt.full(), at[:, kt, :], lnr[:, 0, :], A.mult)
            c.tt(gt.full(), gt.full(), lnr[:, 1, :], A.subtract)
            c.tt(x1[:, kt, :], xin[:, kt % 2, :], gt.full(), A.add)
            c.dma(io["x1f"][kt * 128:(kt + 1) * 128, t0:t0 + TS], x1[:, kt, :], "do", eng="pool")
            c.cp(at[:, kt, :], x1[:, kt, :])
        ln_rows(lambda kt: at[:, kt, :], 16, p5, p6)
        for kt in range(16):
            c.tt(x1b[:, kt, :], x1[:, kt, :], lnr[:, 0, :], A.mult)
            c.tt(x1b[:, kt, :], x1b[:, kt, :], lnr[:, 1, :], A.subtract)
        for mt in range(NMT):
            pg = [p0, p1][mt % 2]
            pu = [p2, p3][mt % 2]
            gtm = [gt, gt2][mt % 2]
            sg = (2 * mt) % 4
            su = (2 * mt + 1) % 4
            c.dma(wstg[:, sg, :], wg.tv_custom(
                lambda ap, mt=mt: ap[mt].rearrange("b a c -> b (a c)"),
                mt, mt, 0, 16 * 128 * 128 - 1), "dwg")
            c.dma(wstg[:, su, :], wu.tv_custom(
                lambda ap, mt=mt: ap[mt].rearrange("b a c -> b (a c)"),
                mt, mt, 0, 16 * 128 * 128 - 1), "dwg")
            for kc in range(16):
                c.mm(pg[:, 0:TS], wstg[:, sg, kc * 128:(kc + 1) * 128], x1b[:, kc, :],
                     start=(kc == 0), stop=(kc == 15))
            for kc in range(16):
                c.mm(pu[:, 0:TS], wstg[:, su, kc * 128:(kc + 1) * 128], x1b[:, kc, :],
                     start=(kc == 0), stop=(kc == 15))
            c.act(gtm.full(), pg[:, 0:TS], AF.Gelu_apprx_tanh)
            c.tt(intr[:, mt % 2, :], pu[:, 0:TS], gtm.full(), A.mult)
            c.dma(iscr[mt], intr[:, mt % 2, :], "dis", eng="pool")
        banks = [p0, p1, p2, p3, p4, p5, p6, p7]
        for half in range(2):
            for kc in range(NMT):
                c.dma(intr[:, kc % 2, :], iscr[kc], "dis2")
                c.dma(wstg[:, kc % 4, 0:1024], wdt.tv_custom(
                    lambda ap, half=half, kc=kc: ap[half, kc].rearrange("b a c -> b (a c)"),
                    0, 1, 0, 1 << 40), "dwd")
                for j in range(8):
                    c.mm(banks[j][:, 0:TS], wstg[:, kc % 4, j * 128:(j + 1) * 128], intr[:, kc % 2, :],
                         start=(kc == 0), stop=(kc == NMT - 1))
            for j in range(8):
                dt_ = half * 8 + j
                c.cp(outst[:, j % 2, :], banks[j][:, 0:TS])
                c.dma(io["dpart"][dt_ * 128:(dt_ + 1) * 128, t0:t0 + TS],
                      outst[:, j % 2, :], "do", eng="pool")


def prep_l0mid(core, inp):
    sd = core % 2

    def bytes_of(a):
        return np.ascontiguousarray(a.view(np.uint8).reshape(a.shape[0], a.shape[1], 4)[:, :, 0])

    gp = bytes_of(np.asarray(inp["L0_ffn_gate_W_packed"]))[sd * 8192:(sd + 1) * 8192]
    up = bytes_of(np.asarray(inp["L0_ffn_up_W_packed"]))[sd * 8192:(sd + 1) * 8192]
    dp = bytes_of(np.asarray(inp["L0_ffn_down_W_packed"]))[:, sd * 4096:(sd + 1) * 4096]
    gs = np.asarray(inp["L0_ffn_gate_scales"])[sd * 8192:(sd + 1) * 8192]
    us = np.asarray(inp["L0_ffn_up_scales"])[sd * 8192:(sd + 1) * 8192]
    ds = np.asarray(inp["L0_ffn_down_scales"])[:, sd * 256:(sd + 1) * 256]
    return {
        "gatep": np.ascontiguousarray(gp.reshape(64, 128, 1024)),
        "gatep_s": np.ascontiguousarray(gs.reshape(64, 128, 64), np.float32),
        "upp": np.ascontiguousarray(up.reshape(64, 128, 1024)),
        "upp_s": np.ascontiguousarray(us.reshape(64, 128, 64), np.float32),
        "downp": np.ascontiguousarray(dp.reshape(16, 128, 4096)),
        "downp_s": np.ascontiguousarray(ds.reshape(16, 128, 256), np.float32),
    }


def build_l0mid(T_used=T):
    nc = bass.Bass("TRN2", target_bir_lowering=False)
    c = Ctx(nc)
    TS_ = 512
    nsc = T_used // TS_
    io = {
        "xfeat": c.din("xfeat", [D, T_used], F32),
        "pa": c.din("pa", [D, T_used], BF16),
        "pb": c.din("pb", [D, T_used], BF16),
        "gatep": c.din("gatep", [64, 128, 1024], U8),
        "gatep_s": c.din("gatep_s", [64, 128, 64], F32),
        "upp": c.din("upp", [64, 128, 1024], U8),
        "upp_s": c.din("upp_s", [64, 128, 64], F32),
        "downp": c.din("downp", [16, 128, 4096], U8),
        "downp_s": c.din("downp_s", [16, 128, 256], F32),
        "x1f": c.dout("x1f", [D, T_used], F32),
        "dpart": c.dout("dpart", [D, T_used], BF16),
    }
    emit_l0mid(c, TS_, nsc, io)
    finalize_core(c)
    return nc


def emit_l1pre(c, TS, NSC, io):
    NE = 16
    c.build_consts()
    ones1f = c.ones1f
    p0 = c.ps("c_p0", [128, 512])
    p1 = c.ps("c_p1", [128, 512])
    p5 = c.ps("c_p5", [128, 512])
    p6 = c.ps("c_p6", [128, 512])
    aa = c.sb("c_aa", [128, 16, TS], BF16)
    bb = c.sb("c_bb", [128, 16, TS], F32)
    bt_ = c.sb("c_bt", [128, 16, TS], BF16)
    rnb = c.sb("c_rnb", [128, 16, TS], BF16)
    xsq = c.sb("c_xsq", [128, 2, TS], BF16)
    rowp = c.sb("c_rowp", [1, 8 * TS], F32)
    lnr = c.sb("c_lnr", [128, 2, TS], F32)
    gt = c.sb("c_gt", [128, TS], F32)
    wstg = c.sb("c_wstg", [128, 2, 2048], F32)
    wbf = c.sb("c_wbf", [128, 2, 2048], BF16)
    mixr = c.sb("c_mixr", [128, NE, CONVK - 1 + TS], BF16)
    tails = c.sb("c_tails", [128, NE, CONVK - 1], BF16)
    convt = c.sb("c_convt", [128, TS], F32)
    cwsb = c.sb("c_cw", [128, NE, CONVK], F32)
    cbsb = c.sb("c_cb", [128, NE], F32)
    nwsb = c.sb("c_nw", [128, 16], F32)
    wd1sb = c.sb("c_wd1", [128, 16, 128], BF16)
    wd1st = c.sb("c_wd1st", [128, 16, 128], F32)
    wbcsb = c.sb("c_wbc", [128, 16, 32], BF16)
    wbcst = c.sb("c_wbcst", [128, 16, 32], F32)
    outst = c.sb("c_outst", [128, 2, TS], BF16)
    c.memset(tails.full(), 0.0)
    c.dma(cwsb.full(), io["convw1"].full(), "dw")
    c.dma(cbsb.full(), io["convb"].full(), "dw")
    c.dma(nwsb.full(), io["nrmw"].full(), "dw")
    c.dma(wd1st.full(), io["wd1"].full(), "dw")
    c.cp(wd1sb.full(), wd1st.full())
    c.dma(wbcst.full(), io["wbc"].full(), "dw")
    c.cp(wbcsb.full(), wbcst.full())

    def ln_rows(src_tiles, n, pb1, pb2, eps=EPS_LN, mean=True):
        for kt in range(n):
            c.act(xsq[:, kt % 2, :], src_tiles(kt), AF.Square)
            c.mm(pb2[0:1, 0:TS], c.onesk_bf[:, 0:1], xsq[:, kt % 2, :],
                 start=(kt == 0), stop=(kt == n - 1))
        sq_ = rowp.tv_sub(1, TS)
        var_ = rowp.tv_sub(2, TS)
        rst_ = rowp.tv_sub(3, TS)
        mr_ = rowp.tv_sub(4, TS)
        c.ts(sq_, pb2[0:1, 0:TS], 1.0 / D, A.mult)
        if mean:
            for kt in range(n):
                c.mm(pb1[0:1, 0:TS], c.onesk_bf[:, 0:1], src_tiles(kt),
                     start=(kt == 0), stop=(kt == n - 1))
            m_ = rowp.tv_sub(0, TS)
            c.ts(m_, pb1[0:1, 0:TS], 1.0 / D, A.mult)
            c.tt(var_, m_, m_, A.mult)
            c.tt(var_, sq_, var_, A.subtract)
        else:
            c.cp(var_, sq_)
        c.ts(var_, var_, eps, A.add)
        c.act(rst_, var_, AF.Ln)
        c.act(rst_, rst_, AF.Exp, scale=-0.5)
        c.mm(pb1[:, 0:TS], ones1f[0:1, :], rst_)
        c.cp(lnr[:, 0, :], pb1[:, 0:TS])
        if mean:
            c.tt(mr_, m_, rst_, A.mult)
            c.mm(pb2[:, 0:TS], ones1f[0:1, :], mr_)
            c.cp(lnr[:, 1, :], pb2[:, 0:TS])

    for sc in range(NSC):
        t0 = sc * TS
        for kt in range(16):
            c.dma(aa[:, kt, :], io["dpa"][kt * 128:(kt + 1) * 128, t0:t0 + TS], "dx")
            if "dpb" in io:
                c.dma(bb[:, kt, :], io["dpb"][kt * 128:(kt + 1) * 128, t0:t0 + TS], "dx")
                c.tt(aa[:, kt, :], aa[:, kt, :], bb[:, kt, :], A.add)
            c.cp(bt_[:, kt, :], aa[:, kt, :])
        ln_rows(lambda kt: bt_[:, kt, :], 16, p5, p6)
        for kt in range(16):
            c.dma(bb[:, kt, :], io["x1f"][kt * 128:(kt + 1) * 128, t0:t0 + TS], "dx")
        for kt in range(16):
            c.tt(gt.full(), aa[:, kt, :], lnr[:, 0, :], A.mult)
            c.tt(gt.full(), gt.full(), lnr[:, 1, :], A.subtract)
            c.tt(bb[:, kt, :], bb[:, kt, :], gt.full(), A.add)
            c.dma(io["x2f"][kt * 128:(kt + 1) * 128, t0:t0 + TS], bb[:, kt, :], "do", eng="pool")
            c.cp(bt_[:, kt, :], bb[:, kt, :])
        ln_rows(lambda kt: bt_[:, kt, :], 16, p5, p6)
        for kt in range(16):
            c.tt(aa[:, kt, :], bb[:, kt, :], lnr[:, 0, :], A.mult)
            c.tt(aa[:, kt, :], aa[:, kt, :], lnr[:, 1, :], A.subtract)
            c.cp(bt_[:, kt, :], aa[:, kt, :])
        ln_rows(lambda kt: bt_[:, kt, :], 16, p5, p6, eps=EPS_RMS, mean=False)
        for kt in range(16):
            c.tt(gt.full(), aa[:, kt, :], lnr[:, 0, :], A.mult)
            c.ts(rnb[:, kt, :], gt.full(), nwsb[:, kt:kt + 1], A.mult)
        for ot in range(NE):
            c.dma(wstg[:, ot % 2, :], io["wskip"][ot], "dw")
            c.cp(wbf[:, ot % 2, :], wstg[:, ot % 2, :])
            for kc in range(16):
                c.mm(p0[:, 0:TS], wbf[:, ot % 2, kc * 128:(kc + 1) * 128], rnb[:, kc, :],
                     start=(kc == 0), stop=(kc == 15))
            c.act(gt.full(), p0[:, 0:TS], AF.Silu)
            c.cp(convt.full(), gt.full())
            c.dma(io["ssil"][ot * 128:(ot + 1) * 128, t0:t0 + TS],
                  convt.tv_custom(lambda ap: ap[:, 0:TS], 0, 127, 0, TS - 1), "do", eng="pool") \
                if False else None
            c.cp(mixr[:, ot, CONVK - 1:], gt.full())
            c.dma(io["ssil"][ot * 128:(ot + 1) * 128, t0:t0 + TS], mixr[:, ot, CONVK - 1:],
                  "do", eng="pool")
        for ot in range(NE):
            c.dma(wstg[:, ot % 2, :], io["win"][ot], "dw")
            c.cp(wbf[:, ot % 2, :], wstg[:, ot % 2, :])
            for kc in range(16):
                c.mm(p1[:, 0:TS], wbf[:, ot % 2, kc * 128:(kc + 1) * 128], rnb[:, kc, :],
                     start=(kc == 0), stop=(kc == 15))
            c.cp(mixr[:, ot, CONVK - 1:], p1[:, 0:TS])
        for ot in range(NE):
            c.cp(mixr[:, ot, 0:CONVK - 1], tails[:, ot, :])
            c.cp(tails[:, ot, :], mixr[:, ot, TS:TS + CONVK - 1])
            c.ts(convt.full(), mixr[:, ot, 0:TS], cwsb[:, ot, 0:1], A.mult)
            for j in range(1, CONVK):
                c.stt(convt.full(), mixr[:, ot, j:j + TS], cwsb[:, ot, j:j + 1],
                      convt.full(), A.mult, A.add)
            c.act(mixr[:, ot, CONVK - 1:], convt.full(), AF.Silu, bias=cbsb[:, ot:ot + 1])
            c.dma(io["xg"][ot * 128:(ot + 1) * 128, t0:t0 + TS], mixr[:, ot, CONVK - 1:],
                  "do", eng="pool")
        for kc in range(16):
            c.mm(p0[:, 0:TS], wd1sb[:, kc, :], mixr[:, kc, CONVK - 1:],
                 start=(kc == 0), stop=(kc == 15))
        c.cp(outst[:, 0, :], p0[:, 0:TS])
        c.dma(io["dpre"][0:128, t0:t0 + TS], outst[:, 0, :], "do", eng="pool")
        for kc in range(16):
            c.mm(p1[:, 0:32] if False else p1[0:32, 0:TS], wbcsb[:, kc, :],
                 mixr[:, kc, CONVK - 1:], start=(kc == 0), stop=(kc == 15))
        c.cp(outst[0:32, 1, :], p1[0:32, 0:TS])
        c.dma(io["dpre"][128:160, t0:t0 + TS], outst[0:32, 1, :], "do", eng="pool")


def prep_l1pre(core, inp):
    sd = core % 2
    ws = np.asarray(inp["L1_attn_W_skip"])[:, sd * 2048:(sd + 1) * 2048]
    wi = np.asarray(inp["L1_attn_W_in"])[:, sd * 2048:(sd + 1) * 2048]
    cw = np.asarray(inp["L1_attn_conv_w"])[:, 0, sd * 2048:(sd + 1) * 2048].T
    cb = np.asarray(inp["L1_attn_conv_b"])[sd * 2048:(sd + 1) * 2048]
    nw = np.asarray(inp["L1_attn_norm_w"])
    wd1 = np.asarray(inp["L1_attn_W_delta_1"])[sd * 2048:(sd + 1) * 2048]
    wb = np.asarray(inp["L1_attn_W_B"])[sd * 2048:(sd + 1) * 2048]
    wc = np.asarray(inp["L1_attn_W_C"])[sd * 2048:(sd + 1) * 2048]
    wbc = np.concatenate([wb, wc], axis=1)
    return {
        "wskip": _tile_lhsT(ws, 16, 16),
        "win": _tile_lhsT(wi, 16, 16),
        "convw1": np.ascontiguousarray(cw.reshape(16, 128, 4).transpose(1, 0, 2), np.float32),
        "convb": np.ascontiguousarray(cb.reshape(16, 128).T, np.float32),
        "nrmw": np.ascontiguousarray(nw.reshape(16, 128).T, np.float32),
        "wd1": np.ascontiguousarray(wd1.reshape(16, 128, 128).transpose(1, 0, 2), np.float32),
        "wbc": np.ascontiguousarray(wbc.reshape(16, 128, 32).transpose(1, 0, 2), np.float32),
    }


def build_l1pre(T_used=T):
    nc = bass.Bass("TRN2", target_bir_lowering=False)
    c = Ctx(nc)
    TS_ = 512
    nsc = T_used // TS_
    io = {
        "x1f": c.din("x1f", [D, T_used], F32),
        "dpa": c.din("dpa", [D, T_used], BF16),
        "dpb": c.din("dpb", [D, T_used], BF16),
        "wskip": c.din("wskip", [16, 128, 16, 128], F32),
        "win": c.din("win", [16, 128, 16, 128], F32),
        "convw1": c.din("convw1", [128, 16, 4], F32),
        "convb": c.din("convb", [128, 16], F32),
        "nrmw": c.din("nrmw", [128, 16], F32),
        "wd1": c.din("wd1", [128, 16, 128], F32),
        "wbc": c.din("wbc", [128, 16, 32], F32),
        "x2f": c.dout("x2f", [D, T_used], F32),
        "ssil": c.dout("ssil", [2048, T_used], BF16),
        "xg": c.dout("xg", [2048, T_used], BF16),
        "dpre": c.dout("dpre", [160, T_used], BF16),
    }
    emit_l1pre(c, TS_, nsc, io)
    finalize_core(c)
    return nc


def emit_l1scan(c, TS, NSC, io):
    NE = 16
    N1_ = 16
    c.build_consts()
    ones1f = c.ones1f
    p0 = c.ps("s_p0", [128, 512])
    p1 = c.ps("s_p1", [128, 512])
    p2 = c.ps("s_p2", [128, 512])
    p5 = c.ps("s_p5", [128, 512])
    p6 = c.ps("s_p6", [128, 512])
    dbuf = c.sb("s_dbuf", [128, NE, TS], BF16)
    wxg = c.sb("s_wxg", [128, NE, TS], BF16)
    xgl = c.sb("s_xgl", [128, NE, TS], BF16)
    x2t = c.sb("s_x2t", [128, 2, TS], F32)
    bt_ = c.sb("s_bt", [128, 16, TS], BF16)
    xsq = c.sb("s_xsq", [128, 2, TS], BF16)
    rowp = c.sb("s_rowp", [1, 5 * TS], F32)
    lnr = c.sb("s_lnr", [128, 2, TS], F32)
    ygb = c.sb("s_ygb", [128, NE, TS], BF16)
    d32 = c.sb("s_d32", [32, 2, TS], BF16)
    d32f = c.sb("s_d32f", [32, TS], F32)
    brow = c.sb("s_brow", [1, 2, TS], F32)
    dect_p = [c.sb("s_dect0", [128, TS], F32), c.sb("s_dect1", [128, TS], F32)]
    usb_p = [c.sb("s_usb0", [128, TS], F32), c.sb("s_usb1", [128, TS], F32)]
    hb_p = [c.sb("s_hb0", [128, TS], BF16), c.sb("s_hb1", [128, TS], BF16)]
    dect = dect_p[0]
    usb = usb_p[0]
    tmp_ = usb_p[1]
    yacc = c.sb("s_yacc", [128, TS], F32)
    Bbc = c.sb("s_Bbc", [128, 16, TS], BF16)
    Cbc = c.sb("s_Cbc", [128, 16, TS], BF16)
    hst = c.sb("s_hst", [128, NE, N1_], F32)
    A1 = c.sb("s_A1", [128, NE, N1_], F32)
    A1st = c.sb("s_A1st", [128, NE, N1_], F32)
    bd2 = c.sb("s_bd2", [128, 16], F32)
    wD = c.sb("s_wD", [128, 16], F32)
    rwc = c.sb("s_rwc", [128, 1], F32)
    wstg = c.sb("s_wstg", [128, 2, 2048], F32)
    wbf = c.sb("s_wbf", [128, 2, 2048], BF16)
    wd2sb = c.sb("s_wd2", [128, 16, 128], BF16)
    wd2st = c.sb("s_wd2st", [128, 16, 128], F32)
    dsb = c.sb("s_dsb", [128, TS], BF16)
    sp1b = c.sb("s_sp1b", [128, TS], BF16)
    ssl = c.sb("s_ssl", [128, 2, TS], BF16)
    outst = c.sb("s_outst", [128, 1, TS], BF16)
    rt_ = c.sb("s_rt", [128, TS], F32)
    bscr = c.dint("s_bscr", [32, T], F32)
    c.memset(hst.full(), 0.0)
    c.dma(A1st.full(), io["alog1"].full(), "dw")
    c.act(A1.full(), A1st.full(), AF.Exp)
    c.ts(A1.full(), A1.full(), -1.0, A.mult)
    c.dma(bd2.full(), io["bd2"].full(), "dw")
    c.dma(wD.full(), io["wDc"].full(), "dw")
    c.dma(rwc.full(), io["rw"].full(), "dw")
    c.dma(wd2st.full(), io["wd2"].full(), "dw")
    c.cp(wd2sb.full(), wd2st.full())

    def ln_rows(src_tiles, n, pb1, pb2):
        for kt in range(n):
            c.mm(pb1[0:1, 0:TS], c.onesk_bf[:, 0:1], src_tiles(kt),
                 start=(kt == 0), stop=(kt == n - 1))
        m_ = rowp.tv_sub(0, TS)
        c.ts(m_, pb1[0:1, 0:TS], 1.0 / D, A.mult)
        for kt in range(n):
            c.act(xsq[:, kt % 2, :], src_tiles(kt), AF.Square)
            c.mm(pb2[0:1, 0:TS], c.onesk_bf[:, 0:1], xsq[:, kt % 2, :],
                 start=(kt == 0), stop=(kt == n - 1))
        sq_ = rowp.tv_sub(1, TS)
        var_ = rowp.tv_sub(2, TS)
        rst_ = rowp.tv_sub(3, TS)
        mr_ = rowp.tv_sub(4, TS)
        c.ts(sq_, pb2[0:1, 0:TS], 1.0 / D, A.mult)
        c.tt(var_, m_, m_, A.mult)
        c.tt(var_, sq_, var_, A.subtract)
        c.ts(var_, var_, EPS_LN, A.add)
        c.act(rst_, var_, AF.Ln)
        c.act(rst_, rst_, AF.Exp, scale=-0.5)
        c.tt(mr_, m_, rst_, A.mult)
        c.mm(pb1[:, 0:TS], ones1f[0:1, :], rst_)
        c.cp(lnr[:, 0, :], pb1[:, 0:TS])
        c.mm(pb2[:, 0:TS], ones1f[0:1, :], mr_)
        c.cp(lnr[:, 1, :], pb2[:, 0:TS])

    for sc in range(NSC):
        t0 = sc * TS
        c.dma(dsb.full(), io["dpa"][0:128, t0:t0 + TS], "dx")
        if "dpb" in io:
            c.dma(sp1b.full(), io["dpb"][0:128, t0:t0 + TS], "dx")
            c.tt(dsb.full(), dsb.full(), sp1b.full(), A.add)
        c.dma(d32[:, 0, :], io["dpa"][128:160, t0:t0 + TS], "dx")
        if "dpb" in io:
            c.dma(d32[:, 1, :], io["dpb"][128:160, t0:t0 + TS], "dx")
            c.tt(d32[:, 0, :], d32[:, 0, :], d32[:, 1, :], A.add)
        c.cp(d32f.full(), d32[:, 0, :])
        c.dma(bscr[:, t0:t0 + TS], d32f.full(), "dbs", eng="pool")
        for n in range(N1_):
            c.dma(brow[0:1, 0, :], bscr.tv_custom(
                lambda ap, n=n, t0=t0: ap[n:n + 1, t0:t0 + TS], n, n, t0, t0 + TS - 1), "dbr")
            c.mm([p5, p6][n % 2][:, 0:TS], ones1f[0:1, :], brow[0:1, 0, :])
            c.cp(Bbc[:, n, :], [p5, p6][n % 2][:, 0:TS])
            c.dma(brow[0:1, 1, :], bscr.tv_custom(
                lambda ap, n=n, t0=t0: ap[16 + n:17 + n, t0:t0 + TS], 16 + n, 16 + n,
                t0, t0 + TS - 1), "dbr")
            c.mm([p2, p1][n % 2][:, 0:TS], ones1f[0:1, :], brow[0:1, 1, :])
            c.cp(Cbc[:, n, :], [p2, p1][n % 2][:, 0:TS])
        for et in range(NE):
            c.mm(p0[:, 0:TS], wd2sb[:, et, :], dsb.full())
            c.ts(tmp_.full(), p0[:, 0:TS], bd2[:, et:et + 1], A.add)
            c.act(usb.full(), tmp_.full(), AF.Abs)
            c.act(usb.full(), usb.full(), AF.Exp, scale=-1.0)
            c.act(usb.full(), usb.full(), AF.Ln, bias=1.0)
            c.ts(tmp_.full(), tmp_.full(), 0.0, A.max)
            c.tt(dbuf[:, et, :], tmp_.full(), usb.full(), A.add)
            c.dma(xgl[:, et, :], io["xg"][et * 128:(et + 1) * 128, t0:t0 + TS], "dx")
            c.tt(wxg[:, et, :], dbuf[:, et, :], xgl[:, et, :], A.mult)
        for et in range(NE):
            def scan_n(n, et=et):
                Pn = n % 2
                dectn, usbn, hbn, tmpn = dect_p[Pn], usb_p[Pn], hb_p[Pn], dect_p[Pn]
                c.act(dectn.full(), dbuf[:, et, :], AF.Exp, scale=A1[:, et, n:n + 1])
                yield
                c.tt(usbn.full(), wxg[:, et, :], Bbc[:, n, :], A.mult, eng="pool")
                yield
                c.scan(hbn.full(), dectn.full(), usbn.full(), hst[:, et, n:n + 1], A.mult, A.add)
                yield
                c.cp(hst[:, et, n:n + 1], hbn[:, TS - 1:TS])
                yield
                if n == 0:
                    c.tt(yacc.full(), hbn.full(), Cbc[:, n, :], A.mult, eng="pool")
                else:
                    c.tt(tmpn.full(), hbn.full(), Cbc[:, n, :], A.mult, eng="pool")
                    yield
                    c.tt(yacc.full(), yacc.full(), tmpn.full(), A.add, eng="pool")

            def drive2(gens):
                gens = [g for g in gens]
                while gens:
                    nxt = []
                    for g in gens:
                        try:
                            next(g)
                            nxt.append(g)
                        except StopIteration:
                            pass
                    gens = nxt
            for n0 in range(0, N1_, 2):
                drive2([scan_n(n0), scan_n(n0 + 1)])
            c.stt(yacc.full(), xgl[:, et, :], wD[:, et:et + 1], yacc.full(), A.mult, A.add)
            c.dma(ssl[:, et % 2, :], io["ssil"][et * 128:(et + 1) * 128, t0:t0 + TS], "dx")
            c.tt(ygb[:, et, :], yacc.full(), ssl[:, et % 2, :], A.mult)
        for kt in range(16):
            c.dma(x2t[:, kt % 2, :], io["x2f"][kt * 128:(kt + 1) * 128, t0:t0 + TS], "dx")
            c.cp(bt_[:, kt, :], x2t[:, kt % 2, :])
        ln_rows(lambda kt: bt_[:, kt, :], 16, p5, p6)
        for dt_ in range(16):
            c.dma(wstg[:, dt_ % 2, :], io["wout1"][dt_], "dw")
            c.cp(wbf[:, dt_ % 2, :], wstg[:, dt_ % 2, :])
            for et in range(NE):
                c.mm([p0, p1][dt_ % 2][:, 0:TS], wbf[:, dt_ % 2, et * 128:(et + 1) * 128],
                     ygb[:, et, :], start=(et == 0), stop=(et == 15))
            c.tt(rt_.full(), bt_[:, dt_, :], lnr[:, 0, :], A.mult)
            c.tt(rt_.full(), rt_.full(), lnr[:, 1, :], A.subtract)
            c.stt(outst[:, 0, :], rt_.full(), rwc[:, 0:1],
                  [p0, p1][dt_ % 2][:, 0:TS], A.mult, A.add)
            c.dma(io["wpart"][dt_ * 128:(dt_ + 1) * 128, t0:t0 + TS],
                  outst[:, 0, :], "do", eng="pool")


def prep_l1scan(core, inp):
    sd = core % 2
    wd2 = np.asarray(inp["L1_attn_W_delta_2"])[:, sd * 2048:(sd + 1) * 2048]
    bd2 = np.asarray(inp["L1_attn_b_delta_2"])[sd * 2048:(sd + 1) * 2048]
    al = np.asarray(inp["L1_attn_A_log"])[sd * 2048:(sd + 1) * 2048]
    wD = np.asarray(inp["L1_attn_W_D"])[sd * 2048:(sd + 1) * 2048]
    wo = np.asarray(inp["L1_attn_W_out"])[sd * 2048:(sd + 1) * 2048]
    return {
        "wd2": np.ascontiguousarray(wd2.reshape(128, 16, 128), np.float32),
        "bd2": np.ascontiguousarray(bd2.reshape(16, 128).T, np.float32),
        "alog1": np.ascontiguousarray(al.reshape(16, 128, 16).transpose(1, 0, 2), np.float32),
        "wDc": np.ascontiguousarray(wD.reshape(16, 128).T, np.float32),
        "wout1": _tile_lhsT(wo, 16, 16),
        "rw": np.full((128, 1), 1.0 if sd == 0 else 0.0, np.float32),
    }


def build_l1scan(T_used=T):
    nc = bass.Bass("TRN2", target_bir_lowering=False)
    c = Ctx(nc)
    TS_ = 512
    nsc = T_used // TS_
    io = {
        "x2f": c.din("x2f", [D, T_used], F32),
        "dpa": c.din("dpa", [160, T_used], BF16),
        "dpb": c.din("dpb", [160, T_used], BF16),
        "xg": c.din("xg", [2048, T_used], BF16),
        "ssil": c.din("ssil", [2048, T_used], BF16),
        "wd2": c.din("wd2", [128, 16, 128], F32),
        "bd2": c.din("bd2", [128, 16], F32),
        "alog1": c.din("alog1", [128, 16, 16], F32),
        "wDc": c.din("wDc", [128, 16], F32),
        "wout1": c.din("wout1", [16, 128, 16, 128], F32),
        "rw": c.din("rw", [128, 1], F32),
        "wpart": c.dout("wpart", [D, T_used], BF16),
    }
    emit_l1scan(c, TS_, nsc, io)
    finalize_core(c)
    return nc


def emit_l1post(c, TS, NSC, io):
    NH = 32
    c.build_consts()
    ones1f = c.ones1f
    p0 = c.ps("e_p0", [128, 512])
    p1 = c.ps("e_p1", [128, 512])
    p5 = c.ps("e_p5", [128, 512])
    p6 = c.ps("e_p6", [128, 512])
    aa = c.sb("e_aa", [128, 16, TS], BF16)
    bb = c.sb("e_bb", [128, 16, TS], F32)
    bt_ = c.sb("e_bt", [128, 16, TS], BF16)
    x3b = c.sb("e_x3b", [128, 16, TS], BF16)
    xsq = c.sb("e_xsq", [128, 2, TS], BF16)
    rowp = c.sb("e_rowp", [1, 8 * TS], F32)
    lnr = c.sb("e_lnr", [128, 2, TS], F32)
    gt = c.sb("e_gt", [128, TS], F32)
    wstg = c.sb("e_wstg", [128, 2, 2048], F32)
    wsg = c.sb("e_wsg", [128, 2, 2048], BF16)
    hbuf = c.sb("e_hbuf", [128, NH, TS], BF16)
    upsc = c.sb("e_upsc", [128, NH], F32)
    outst = c.sb("e_outst", [128, 2, TS], BF16)
    c.dma(upsc.full(), io["upsc"].full(), "dw")

    def ln_rows(src_tiles, n, pb1, pb2):
        for kt in range(n):
            c.mm(pb1[0:1, 0:TS], c.onesk_bf[:, 0:1], src_tiles(kt),
                 start=(kt == 0), stop=(kt == n - 1))
        m_ = rowp.tv_sub(0, TS)
        c.ts(m_, pb1[0:1, 0:TS], 1.0 / D, A.mult)
        for kt in range(n):
            c.act(xsq[:, kt % 2, :], src_tiles(kt), AF.Square)
            c.mm(pb2[0:1, 0:TS], c.onesk_bf[:, 0:1], xsq[:, kt % 2, :],
                 start=(kt == 0), stop=(kt == n - 1))
        sq_ = rowp.tv_sub(1, TS)
        var_ = rowp.tv_sub(2, TS)
        rst_ = rowp.tv_sub(3, TS)
        mr_ = rowp.tv_sub(4, TS)
        c.ts(sq_, pb2[0:1, 0:TS], 1.0 / D, A.mult)
        c.tt(var_, m_, m_, A.mult)
        c.tt(var_, sq_, var_, A.subtract)
        c.ts(var_, var_, EPS_LN, A.add)
        c.act(rst_, var_, AF.Ln)
        c.act(rst_, rst_, AF.Exp, scale=-0.5)
        c.tt(mr_, m_, rst_, A.mult)
        c.mm(pb1[:, 0:TS], ones1f[0:1, :], rst_)
        c.cp(lnr[:, 0, :], pb1[:, 0:TS])
        c.mm(pb2[:, 0:TS], ones1f[0:1, :], mr_)
        c.cp(lnr[:, 1, :], pb2[:, 0:TS])

    for sc in range(NSC):
        t0 = sc * TS
        for kt in range(16):
            c.dma(aa[:, kt, :], io["wpa"][kt * 128:(kt + 1) * 128, t0:t0 + TS], "dx")
            if "wpb" in io:
                c.dma(bb[:, kt, :], io["wpb"][kt * 128:(kt + 1) * 128, t0:t0 + TS], "dx")
                c.tt(aa[:, kt, :], aa[:, kt, :], bb[:, kt, :], A.add)
            c.cp(bt_[:, kt, :], aa[:, kt, :])
        ln_rows(lambda kt: bt_[:, kt, :], 16, p5, p6)
        for kt in range(16):
            c.dma(bb[:, kt, :], io["x2f"][kt * 128:(kt + 1) * 128, t0:t0 + TS], "dx")
        for kt in range(16):
            c.tt(gt.full(), aa[:, kt, :], lnr[:, 0, :], A.mult)
            c.tt(gt.full(), gt.full(), lnr[:, 1, :], A.subtract)
            c.tt(bb[:, kt, :], bb[:, kt, :], gt.full(), A.add)
            c.dma(io["x3f"][kt * 128:(kt + 1) * 128, t0:t0 + TS], bb[:, kt, :], "do", eng="pool")
            c.cp(bt_[:, kt, :], bb[:, kt, :])
        ln_rows(lambda kt: bt_[:, kt, :], 16, p5, p6)
        for kt in range(16):
            c.tt(x3b[:, kt, :], bb[:, kt, :], lnr[:, 0, :], A.mult)
            c.tt(x3b[:, kt, :], x3b[:, kt, :], lnr[:, 1, :], A.subtract)
        for ht in range(NH):
            c.dma(wstg[:, ht % 2, :], io["upw"][ht], "dw")
            c.act(wsg[:, ht % 2, :], wstg[:, ht % 2, :], AF.Sign)
            pb = [p0, p1][ht % 2]
            for kc in range(16):
                c.mm(pb[:, 0:TS], wsg[:, ht % 2, kc * 128:(kc + 1) * 128], x3b[:, kc, :],
                     start=(kc == 0), stop=(kc == 15))
            c.act(hbuf[:, ht, :], pb[:, 0:TS], AF.Relu, scale=upsc[:, ht:ht + 1])
        for dt_ in range(16):
            pb = [p0, p1][dt_ % 2]
            for g2 in range(2):
                c.dma(wstg[:, dt_ % 2, :], io["dww"].tv_custom(
                    lambda ap, dt_=dt_, g2=g2: ap[dt_, :, g2 * 16:(g2 + 1) * 16, :].rearrange("b a c -> b a c"),
                    dt_, dt_, g2 * 16 * 128 * 128, (g2 + 1) * 16 * 128 * 128 - 1), "dw")
                c.act(wsg[:, dt_ % 2, :], wstg[:, dt_ % 2, :], AF.Sign)
                for j in range(16):
                    kc = g2 * 16 + j
                    c.mm(pb[:, 0:TS], wsg[:, dt_ % 2, j * 128:(j + 1) * 128], hbuf[:, kc, :],
                         start=(kc == 0), stop=(kc == 31))
            c.cp(outst[:, dt_ % 2, :], pb[:, 0:TS])
            c.dma(io["dp2"][dt_ * 128:(dt_ + 1) * 128, t0:t0 + TS],
                  outst[:, dt_ % 2, :], "do", eng="pool")


def prep_l1post(core, inp):
    sd = core % 2
    upw = np.asarray(inp["L1_ffn_up_weight"])[sd * 4096:(sd + 1) * 4096]
    us = np.asarray(inp["L1_ffn_up_scale"])[sd * 4096:(sd + 1) * 4096]
    dw = np.asarray(inp["L1_ffn_down_weight"])[:, sd * 4096:(sd + 1) * 4096]
    return {
        "upw": _tile_lhsT(np.ascontiguousarray(upw.T), 16, 32),
        "upsc": np.ascontiguousarray(us.reshape(32, 128).T, np.float32),
        "dww": _tile_lhsT(np.ascontiguousarray(dw.T), 32, 16),
    }


def build_l1post(T_used=T):
    nc = bass.Bass("TRN2", target_bir_lowering=False)
    c = Ctx(nc)
    TS_ = 512
    nsc = T_used // TS_
    io = {
        "x2f": c.din("x2f", [D, T_used], F32),
        "wpa": c.din("wpa", [D, T_used], BF16),
        "wpb": c.din("wpb", [D, T_used], BF16),
        "upw": c.din("upw", [32, 128, 16, 128], F32),
        "upsc": c.din("upsc", [128, 32], F32),
        "dww": c.din("dww", [16, 128, 32, 128], F32),
        "x3f": c.dout("x3f", [D, T_used], F32),
        "dp2": c.dout("dp2", [D, T_used], BF16),
    }
    emit_l1post(c, TS_, nsc, io)
    finalize_core(c)
    return nc


def emit_final(c, TS, NSC, io):
    c.build_consts()
    ones1f = c.ones1f
    p5 = c.ps("f_p5", [128, 512])
    p6 = c.ps("f_p6", [128, 512])
    aa = c.sb("f_aa", [128, 16, TS], BF16)
    bb = c.sb("f_bb", [128, 16, TS], F32)
    bt_ = c.sb("f_bt", [128, 16, TS], BF16)
    xsq = c.sb("f_xsq", [128, 2, TS], BF16)
    rowp = c.sb("f_rowp", [1, 8 * TS], F32)
    lnr = c.sb("f_lnr", [128, 2, TS], F32)
    gt = c.sb("f_gt", [128, TS], F32)
    dsc = c.sb("f_dsc", [128, 16], F32)
    osc = c.sb("f_osc", [1, 1], F32)
    oscb = c.sb("f_oscb", [128, 1], F32)
    c.dma(dsc.full(), io["dscale"].full(), "dw")
    c.dma(osc.full(), io["oscale"].full(), "dw")
    c.mm(p5[:, 0:1], ones1f[0:1, :], osc[0:1, 0:1])
    c.cp(oscb.full(), p5[:, 0:1])

    def ln_rows(src_tiles, n, pb1, pb2):
        for kt in range(n):
            c.mm(pb1[0:1, 0:TS], c.onesk_bf[:, 0:1], src_tiles(kt),
                 start=(kt == 0), stop=(kt == n - 1))
        m_ = rowp.tv_sub(0, TS)
        c.ts(m_, pb1[0:1, 0:TS], 1.0 / D, A.mult)
        for kt in range(n):
            c.act(xsq[:, kt % 2, :], src_tiles(kt), AF.Square)
            c.mm(pb2[0:1, 0:TS], c.onesk_bf[:, 0:1], xsq[:, kt % 2, :],
                 start=(kt == 0), stop=(kt == n - 1))
        sq_ = rowp.tv_sub(1, TS)
        var_ = rowp.tv_sub(2, TS)
        rst_ = rowp.tv_sub(3, TS)
        mr_ = rowp.tv_sub(4, TS)
        c.ts(sq_, pb2[0:1, 0:TS], 1.0 / D, A.mult)
        c.tt(var_, m_, m_, A.mult)
        c.tt(var_, sq_, var_, A.subtract)
        c.ts(var_, var_, EPS_LN, A.add)
        c.act(rst_, var_, AF.Ln)
        c.act(rst_, rst_, AF.Exp, scale=-0.5)
        c.tt(mr_, m_, rst_, A.mult)
        c.mm(pb1[:, 0:TS], ones1f[0:1, :], rst_)
        c.cp(lnr[:, 0, :], pb1[:, 0:TS])
        c.mm(pb2[:, 0:TS], ones1f[0:1, :], mr_)
        c.cp(lnr[:, 1, :], pb2[:, 0:TS])

    for sc in range(NSC):
        t0 = sc * TS
        for kt in range(16):
            c.dma(aa[:, kt, :], io["da"][kt * 128:(kt + 1) * 128, t0:t0 + TS], "dx")
            if "db" in io:
                c.dma(bb[:, kt, :], io["db"][kt * 128:(kt + 1) * 128, t0:t0 + TS], "dx")
                c.tt(aa[:, kt, :], aa[:, kt, :], bb[:, kt, :], A.add)
            c.ts(aa[:, kt, :], aa[:, kt, :], dsc[:, kt:kt + 1], A.mult)
            c.ts(aa[:, kt, :], aa[:, kt, :], oscb[:, 0:1], A.mult)
            c.cp(bt_[:, kt, :], aa[:, kt, :])
        ln_rows(lambda kt: bt_[:, kt, :], 16, p5, p6)
        for kt in range(16):
            c.dma(bb[:, kt, :], io["x3f"][kt * 128:(kt + 1) * 128, t0:t0 + TS], "dx")
        for kt in range(16):
            c.tt(gt.full(), aa[:, kt, :], lnr[:, 0, :], A.mult)
            c.tt(gt.full(), gt.full(), lnr[:, 1, :], A.subtract)
            c.tt(bb[:, kt, :], bb[:, kt, :], gt.full(), A.add)
            c.dma(io["xout"][kt * 128:(kt + 1) * 128, t0:t0 + TS], bb[:, kt, :], "do", eng="pool")


def prep_final(core, inp):
    ds = np.asarray(inp["L1_ffn_down_scale"])
    os_ = np.asarray(inp["L1_ffn_output_scale"])
    return {
        "dscale": np.ascontiguousarray(ds.reshape(16, 128).T, np.float32),
        "oscale": np.ascontiguousarray(os_.reshape(1, 1), np.float32),
    }


def build_final(T_used=T):
    nc = bass.Bass("TRN2", target_bir_lowering=False)
    c = Ctx(nc)
    TS_ = 512
    nsc = T_used // TS_
    io = {
        "x3f": c.din("x3f", [D, T_used], F32),
        "da": c.din("da", [D, T_used], BF16),
        "db": c.din("db", [D, T_used], BF16),
        "dscale": c.din("dscale", [128, 16], F32),
        "oscale": c.din("oscale", [1, 1], F32),
        "xout": c.dout("xout", [D, T_used], F32),
    }
    emit_final(c, TS_, nsc, io)
    finalize_core(c)
    return nc


def kernel_unfused(**inputs):
    ncores = 8
    nc1 = build_l0attn()
    maps1 = [prep_l0attn(cr, inputs) for cr in range(ncores)]
    r1 = run_bass_kernel_spmd(nc1, maps1, core_ids=list(range(ncores)))
    parts = [r1.results[cr]["part"] for cr in range(ncores)]
    xf = [maps1[cr]["xfeat"] for cr in range(ncores)]
    nc2 = build_l0mid()
    maps2 = []
    for cr in range(ncores):
        m = prep_l0mid(cr, inputs)
        m["xfeat"] = xf[cr]
        m["pa"] = parts[cr]
        m["pb"] = parts[cr ^ 1]
        maps2.append(m)
    r2 = run_bass_kernel_spmd(nc2, maps2, core_ids=list(range(ncores)))
    x1f = [r2.results[cr]["x1f"] for cr in range(ncores)]
    dpart = [r2.results[cr]["dpart"] for cr in range(ncores)]
    nc3 = build_l1pre()
    maps3 = []
    for cr in range(ncores):
        m = prep_l1pre(cr, inputs)
        m["x1f"] = x1f[cr]
        m["dpa"] = dpart[cr]
        m["dpb"] = dpart[cr ^ 1]
        maps3.append(m)
    r3 = run_bass_kernel_spmd(nc3, maps3, core_ids=list(range(ncores)))
    x2f = [r3.results[cr]["x2f"] for cr in range(ncores)]
    ssil = [r3.results[cr]["ssil"] for cr in range(ncores)]
    xg = [r3.results[cr]["xg"] for cr in range(ncores)]
    dpre = [r3.results[cr]["dpre"] for cr in range(ncores)]
    nc4 = build_l1scan()
    maps4 = []
    for cr in range(ncores):
        m = prep_l1scan(cr, inputs)
        m["x2f"] = x2f[cr]
        m["dpa"] = dpre[cr]
        m["dpb"] = dpre[cr ^ 1]
        m["xg"] = xg[cr].view(ml_dtypes.bfloat16) if xg[cr].dtype != ml_dtypes.bfloat16 else xg[cr]
        m["ssil"] = ssil[cr].view(ml_dtypes.bfloat16) if ssil[cr].dtype != ml_dtypes.bfloat16 else ssil[cr]
        maps4.append(m)
    r4 = run_bass_kernel_spmd(nc4, maps4, core_ids=list(range(ncores)))
    wpart = [r4.results[cr]["wpart"] for cr in range(ncores)]
    nc5 = build_l1post()
    maps5 = []
    for cr in range(ncores):
        m = prep_l1post(cr, inputs)
        m["x2f"] = x2f[cr]
        m["wpa"] = wpart[cr]
        m["wpb"] = wpart[cr ^ 1]
        maps5.append(m)
    r5 = run_bass_kernel_spmd(nc5, maps5, core_ids=list(range(ncores)))
    x3f = [r5.results[cr]["x3f"] for cr in range(ncores)]
    dp2 = [r5.results[cr]["dp2"] for cr in range(ncores)]
    nc6 = build_final()
    maps6 = []
    for cr in range(ncores):
        m = prep_final(cr, inputs)
        m["x3f"] = x3f[cr]
        m["da"] = dp2[cr]
        m["db"] = dp2[cr ^ 1]
        maps6.append(m)
    r6 = run_bass_kernel_spmd(nc6, maps6, core_ids=list(range(ncores)))
    out = np.empty((B, T, D), np.float32)
    for b_ in range(B):
        out[b_] = r6.results[2 * b_]["xout"].T
    return out


PAIRS = [[0, 1], [2, 3], [4, 5], [6, 7]]


def build_fused(T_used=T):
    nc = bass.Bass("TRN2", target_bir_lowering=False)
    c = Ctx(nc)
    TS_ = 512
    nsc = T_used // TS_
    din, dout, dint = c.din, c.dout, c.dint
    xfeat = din("xfeat", [D, T_used], F32)
    part = dint("i_part", [D, T_used], BF16)
    parts = dint("i_parts", [D, T_used], BF16)
    x1f = dint("i_x1f", [D, T_used], F32)
    dpart = dint("i_dpart", [D, T_used], BF16)
    dparts = dint("i_dparts", [D, T_used], BF16)
    x2f = dint("i_x2f", [D, T_used], F32)
    ssil = dint("i_ssil", [2048, T_used], BF16)
    xg = dint("i_xg", [2048, T_used], BF16)
    dpre = dint("i_dpre", [160, T_used], BF16)
    dpres = dint("i_dpres", [160, T_used], BF16)
    wpart = dint("i_wpart", [D, T_used], BF16)
    wparts = dint("i_wparts", [D, T_used], BF16)
    x3f = dint("i_x3f", [D, T_used], F32)
    dp2 = dint("i_dp2", [D, T_used], BF16)
    dp2s = dint("i_dp2s", [D, T_used], BF16)
    xout = dout("xout", [D, T_used], F32)

    c.build_consts()
    ioA = {
        "xfeat": xfeat, "part": part,
        "wqkv": din("wqkv", [32, 128, 16, 128], F32),
        "wz": din("wz", [16, 128, 16, 128], F32),
        "wba": din("wba", [128, 16, 32], F32),
        "convw": din("convw", [128, 32, 4], F32),
        "alog": din("alog", [1, 16], F32),
        "dtb": din("dtb", [1, 16], F32),
        "nw": din("nw", [128, 1], F32),
        "wout": din("wout", [16, 128, 16, 128], F32),
    }
    c.phase_begin()
    emit_l0attn(c, TS_, nsc, ioA)
    c.phase_end()
    c.s.cc("AllReduce", A.add, PAIRS, part.full(), parts.full())
    ioB = {
        "xfeat": xfeat, "pa": parts, "x1f": x1f, "dpart": dpart,
        "gatep": din("gatep", [64, 128, 1024], U8),
        "gatep_s": din("gatep_s", [64, 128, 64], F32),
        "upp": din("upp", [64, 128, 1024], U8),
        "upp_s": din("upp_s", [64, 128, 64], F32),
        "downp": din("downp", [16, 128, 4096], U8),
        "downp_s": din("downp_s", [16, 128, 256], F32),
    }
    c.phase_begin()
    emit_l0mid(c, TS_, nsc, ioB)
    c.phase_end()
    c.s.cc("AllReduce", A.add, PAIRS, dpart.full(), dparts.full())
    ioC = {
        "x1f": x1f, "dpa": dparts, "x2f": x2f, "ssil": ssil, "xg": xg, "dpre": dpre,
        "wskip": din("wskip", [16, 128, 16, 128], F32),
        "win": din("win", [16, 128, 16, 128], F32),
        "convw1": din("convw1", [128, 16, 4], F32),
        "convb": din("convb", [128, 16], F32),
        "nrmw": din("nrmw", [128, 16], F32),
        "wd1": din("wd1", [128, 16, 128], F32),
        "wbc": din("wbc", [128, 16, 32], F32),
    }
    c.phase_begin()
    emit_l1pre(c, TS_, nsc, ioC)
    c.phase_end()
    c.s.cc("AllReduce", A.add, PAIRS, dpre.full(), dpres.full())
    ioD = {
        "x2f": x2f, "dpa": dpres, "xg": xg, "ssil": ssil, "wpart": wpart,
        "wd2": din("wd2", [128, 16, 128], F32),
        "bd2": din("bd2", [128, 16], F32),
        "alog1": din("alog1", [128, 16, 16], F32),
        "wDc": din("wDc", [128, 16], F32),
        "wout1": din("wout1", [16, 128, 16, 128], F32),
        "rw": din("rw", [128, 1], F32),
    }
    c.phase_begin()
    emit_l1scan(c, TS_, nsc, ioD)
    c.phase_end()
    c.s.cc("AllReduce", A.add, PAIRS, wpart.full(), wparts.full())
    ioE = {
        "x2f": x2f, "wpa": wparts, "x3f": x3f, "dp2": dp2,
        "upw": din("upw", [32, 128, 16, 128], F32),
        "upsc": din("upsc", [128, 32], F32),
        "dww": din("dww", [16, 128, 32, 128], F32),
    }
    c.phase_begin()
    emit_l1post(c, TS_, nsc, ioE)
    c.phase_end()
    c.s.cc("AllReduce", A.add, PAIRS, dp2.full(), dp2s.full())
    ioF = {
        "x3f": x3f, "da": dp2s, "xout": xout,
        "dscale": din("dscale", [128, 16], F32),
        "oscale": din("oscale", [1, 1], F32),
    }
    c.phase_begin()
    emit_final(c, TS_, nsc, ioF)
    c.phase_end()
    finalize_core(c)
    return nc


def prep_fused(core, inputs):
    m = {}
    m.update(prep_l0attn(core, inputs))
    m.update(prep_l0mid(core, inputs))
    m.update(prep_l1pre(core, inputs))
    m.update(prep_l1scan(core, inputs))
    m.update(prep_l1post(core, inputs))
    m.update(prep_final(core, inputs))
    return m


def kernel(**inputs):
    ncores = 8
    nc = build_fused()
    maps = [prep_fused(cr, inputs) for cr in range(ncores)]
    r = run_bass_kernel_spmd(nc, maps, core_ids=list(range(ncores)))
    out = np.empty((B, T, D), np.float32)
    for b_ in range(B):
        out[b_] = r.results[2 * b_]["xout"].T
    return out
